# version 07
# baseline (speedup 1.0000x reference)
import numpy as np
import ml_dtypes
from contextlib import ExitStack

import concourse.bass as bass
import concourse.mybir as mybir
from concourse.bass_utils import run_bass_kernel_spmd
from concourse.bass_core_rust import IndirectOffsetOnAxis

F32 = mybir.dt.float32
BF16 = mybir.dt.bfloat16
ALU = mybir.AluOpType
AF = mybir.ActivationFunctionType
AX = mybir.AxisListType

B, S, D = 8, 2048, 1024
TOK = 2048
NT = 16
DC = 8
EPS = 1e-5
CAP0 = 2048.0
CAP1 = 2560.0
M1 = 384
NSLOT = 8 * M1
FF = 4096

DEV_TAPS = []
BUILD_UPTO = 99


class Sem:
    def __init__(self, nc, ctx, name):
        self.h = ctx.enter_context(nc.semaphore(name))
        self.v = 0
        self.mw = 0

    def inc(self, inst, n=1):
        inst.then_inc(self.h, n)
        self.v += n
        return self.v

    def wait(self, eng, v=None):
        w = self.v if v is None else v
        self.mw = max(self.mw, w)
        eng.wait_ge(self.h, w)

    def bar(self, eng):
        if self.mw > 0:
            eng.wait_ge(self.h, self.mw)


def build():
    nc = bass.Bass("TRN2", target_bir_lowering=False, debug=False)

    def din(name, shape, dt=F32):
        return nc.dram_tensor(name, shape, dt, kind="ExternalInput").ap()

    xb = din("xb", [TOK, D])
    xs = din("xs", [TOK, D])
    xh = din("xh", [24, D])
    ln1_w = din("ln1_w", [D]); ln1_b = din("ln1_b", [D])
    convw = din("convw", [4, D]); convb = din("convb", [D])
    lns_w = din("lns_w", [D]); lns_b = din("lns_b", [D])
    in_w = din("in_w", [D, 2 * D]); in_b = din("in_b", [128, 16])
    xp_w = din("xp_w", [D, 16]); xp_b = din("xp_b", [16, 1])
    A_log = din("A_log", [16, 1])
    Dmat = din("Dmat", [16, D])
    out_w = din("out_w", [D, D]); out_b = din("out_b", [D])
    ln2_w = din("ln2_w", [D]); ln2_b = din("ln2_b", [D])
    mlp_w1 = din("mlp_w1", [D, 2 * D]); mlp_b1 = din("mlp_b1", [128, 16])
    mlp_w2 = din("mlp_w2", [2 * D, D]); mlp_b2 = din("mlp_b2", [D])
    Wg = din("Wg", [D, 8])
    We = din("We", [D, D]); be = din("be", [D])
    Wq = din("Wq", [D, D]); Wk = din("Wk", [D, D])
    Wv = din("Wv", [D, D]); Wr = din("Wr", [D, D])
    gate_wT = din("gate_wT", [D, 8]); gate_b = din("gate_b", [1, 8])
    w1 = din("w1", [8, 2, D, FF // 2]); b1r_d = din("b1r", [8, 128, 32])
    w2 = din("w2", [8, FF, D]); b2 = din("b2", [8, D])
    identf = din("identf", [128, 128])
    identb = din("identb", [128, 128], BF16)
    tri_bf = din("tri_bf", [128, 128], BF16)
    cmask = din("cmask", [8, 1])
    cexp = din("cexp", [8, 1])
    hmask = din("hmask", [24, 1])
    causm = din("causm", [128, 128], BF16)
    iota8 = din("iota8", [1, 8])
    iota512 = din("iota512", [1, 512])
    iota_st = din("iota_st", [128, 32])
    seg256 = din("seg256", [16, 1])

    out = nc.dram_tensor("out", [TOK, D], F32, kind="ExternalOutput").ap()

    y_sp = nc.dram_tensor("y_sp", [TOK, D], BF16, kind="Internal").ap()
    eout_sp = nc.dram_tensor("eout_sp", [4096, D], BF16, kind="Internal").ap()
    x2_sp = nc.dram_tensor("x2_sp", [TOK, D], F32, kind="Internal").ap()
    s1_sp = nc.dram_tensor("s1_sp", [TOK, D], BF16, kind="Internal").ap()
    ag1_in = nc.dram_tensor("ag1_in", [1, 128], F32, kind="Internal").ap()
    ag1_out = nc.dram_tensor("ag1_out", [8, 128], F32, kind="Internal").ap()
    ag2_in = nc.dram_tensor("ag2_in", [1, 8], F32, kind="Internal").ap()
    ag3_in = nc.dram_tensor("ag3_in", [1, 8], F32, kind="Internal").ap()
    ag2_out = nc.dram_tensor("ag2_out", [8, 8], F32, kind="Internal").ap()
    ag3_out = nc.dram_tensor("ag3_out", [8, 8], F32, kind="Internal").ap()

    dev_outs = {}
    for name, shape, dt in DEV_TAPS:
        dev_outs[name] = nc.dram_tensor("dev_" + name, shape, dt,
                                        kind="ExternalOutput").ap()

    ctx = ExitStack()
    with ctx:
        block = ctx.enter_context(nc.Block())

        def sb(name, shape, dt=F32):
            return ctx.enter_context(nc.sbuf_tensor(name, shape, dt))

        def ps(name, shape):
            return ctx.enter_context(nc.psum_tensor(name, shape, F32))

        tS = Sem(nc, ctx, "tS"); vS = Sem(nc, ctx, "vS")
        sS = Sem(nc, ctx, "sS"); gS = Sem(nc, ctx, "gS")
        qS = Sem(nc, ctx, "qS"); oS = Sem(nc, ctx, "oS")
        cS = Sem(nc, ctx, "cS"); gq = Sem(nc, ctx, "gq")

        A = sb("A", [128, 16 * D], F32)
        C = sb("C", [128, 16 * D], BF16)
        Dd = sb("Dd", [128, 16 * D], BF16)
        SCR = sb("SCR", [128, 4 * D], F32)
        tmp4 = sb("tmp4", [128, D], F32)
        tmp4b = sb("tmp4b", [128, D], F32)
        Abf = A[:, :].bitcast(BF16)
        Dbf32 = Dd[:, :].bitcast(F32)

        idf = sb("idf", [128, 128])
        idb = sb("idb", [128, 128], BF16)
        tri = sb("tri", [128, 128], BF16)
        cm8 = sb("cm8", [8, 1]); ce8 = sb("ce8", [8, 1])
        hm24 = sb("hm24", [24, 1])
        caus = sb("caus", [128, 128], BF16)
        io8 = sb("io8", [1, 8])
        io512 = sb("io512", [1, 512])
        iost = sb("iost", [128, 32])
        s256 = sb("s256", [16, 1])
        ones_f = sb("ones_f", [128, 1])
        ones_b = sb("ones_b", [128, 1], BF16)
        onesr_f = sb("onesr_f", [1, 128])
        lnv = sb("lnv", [1, D])
        xpw8 = sb("xpw8", [128, 8 * 16], BF16)
        inb_sb = sb("inb_sb", [128, 16])
        m1b_sb = sb("m1b_sb", [128, 16])
        xpb_sb = sb("xpb_sb", [16, 1])
        alog = sb("alog", [16, 1])
        svec = sb("svec", [16, 1])
        dvec = sb("dvec", [16, 1])
        stat_m = sb("stat_m", [128, 16])
        stat_i = sb("stat_i", [128, 16])
        AUX = sb("AUX", [128, 8192], F32)
        AUXbf = AUX[:, :].bitcast(BF16)
        u_fm = AUX[0:16, 0:2048]
        hs_fm = AUX[0:16, 2048:4096]
        dect = AUX[0:16, 4096:6144]
        dmat = AUX[0:16, 6144:7168]
        halo_x = AUX[0:24, 7168:8192]
        e16 = sb("e16", [16, 8])
        eag = sb("eag", [8, 136])
        wjn = sb("wjn", [8, 16])
        prod8 = sb("prod8", [8, 128])
        c128 = sb("c128", [128, 1])
        ccorr = sb("ccorr", [16, 8])
        s8x16 = sb("s8x16", [8, 16])
        srow = sb("srow", [1, 16])

        halo_n1 = sb("halo_n1", [24, D], BF16)
        hst = sb("hst", [24, 2])
        wka = sb("wka", [128, 2048], BF16)
        wkb = sb("wkb", [128, 2048], BF16)

        pb = [ps(f"pb{i}", [128, 512]) for i in range(4)]
        pbig = ps("pbig", [128, 2048])

        lnw_bc = SCR[:, 0:D]
        lnb_bc = SCR[:, D:2 * D]
        bias_bc = SCR[:, 2 * D:3 * D]
        w_bc = SCR[:, 0:4 * D]

        def t_blk(fn):
            block.tensor(fn)

        def v_blk(fn):
            block.vector(fn)

        def s_blk(fn):
            block.scalar(fn)

        def g_blk(fn):
            block.gpsimd(fn)

        def q_blk(fn):
            block.sync(fn)

        def load_row(src_dram_row, width=D):
            def qf(sync):
                qS.bar(sync)
                tS.wait(sync)
                qS.inc(sync.dma_start(out=lnv[0:1, 0:width],
                                      in_=src_dram_row), 16)
            q_blk(qf)

        def bcast_lnv(dst, width=D):
            nchunk = (width + 511) // 512

            def tf(tensor):
                vS.wait(tensor); qS.wait(tensor); gS.wait(tensor)
                for ci in range(nchunk):
                    w = min(512, width - ci * 512)
                    tS.inc(tensor.matmul(pb[ci % 4][:, 0:w], onesr_f[0:1, :],
                                         lnv[0:1, ci * 512:ci * 512 + w],
                                         start=True, stop=True))
            t_blk(tf)

            def vf(vector):
                tS.wait(vector)
                for ci in range(nchunk):
                    w = min(512, width - ci * 512)
                    vS.inc(vector.tensor_copy(dst[:, ci * 512:ci * 512 + w],
                                              pb[ci % 4][:, 0:w]))
            v_blk(vf)

        def load_bcast(src_dram_row, dst, width=D):
            load_row(src_dram_row, width)
            bcast_lnv(dst, width)

        def layernorm(src, dst, tiles, scratch, wb=None, mcol=None, icol=None,
                      halo_mask=None, width=D):
            def v1(vector):
                qS.wait(vector); gS.wait(vector); sS.wait(vector)
                gq.wait(vector)
                for t in tiles:
                    vS.inc(vector.reduce_sum(mcol[:, t:t + 1],
                                             src[:, t * width:(t + 1) * width],
                                             axis=AX.X))
                vS.wait(vector)
                for t in tiles:
                    vS.inc(vector.tensor_scalar(mcol[:, t:t + 1],
                                                mcol[:, t:t + 1],
                                                1.0 / width, None, ALU.mult))
            v_blk(v1)

            npart = src.shape[0]

            def s1(scalar):
                vS.wait(scalar)
                for n, t in enumerate(tiles):
                    sS.wait(scalar)
                    sS.inc(scalar.activation(
                        scratch[0:npart, (n % 2) * width:(n % 2 + 1) * width],
                        src[:, t * width:(t + 1) * width],
                        AF.Square, bias=mcol[:, t:t + 1], scale=-1.0,
                        accum_out=icol[:, t:t + 1]))
            s_blk(s1)

            def v2(vector):
                sS.wait(vector)
                for t in tiles:
                    vS.inc(vector.tensor_scalar(icol[:, t:t + 1],
                                                icol[:, t:t + 1],
                                                1.0 / width, EPS,
                                                ALU.mult, ALU.add))
                vS.wait(vector)
                for t in tiles:
                    vS.inc(vector.reciprocal(icol[:, t:t + 1], icol[:, t:t + 1]))
            v_blk(v2)

            def s2(scalar):
                vS.wait(scalar)
                for t in tiles:
                    sS.inc(scalar.activation(icol[:, t:t + 1], icol[:, t:t + 1],
                                             AF.Sqrt))
            s_blk(s2)

            if wb is None:
                def v3(vector):
                    sS.wait(vector)
                    for t in tiles:
                        vS.inc(vector.tensor_scalar(
                            dst[:, t * width:(t + 1) * width],
                            src[:, t * width:(t + 1) * width],
                            mcol[:, t:t + 1], icol[:, t:t + 1],
                            ALU.subtract, ALU.mult))
                v_blk(v3)
            else:
                wt, bt = wb

                def v3(vector):
                    sS.wait(vector)
                    for t in tiles:
                        vS.inc(vector.scalar_tensor_tensor(
                            dst[:, t * width:(t + 1) * width],
                            src[:, t * width:(t + 1) * width],
                            mcol[:, t:t + 1], wt[0:npart, 0:width],
                            ALU.subtract, ALU.mult))
                    vS.wait(vector)
                    for t in tiles:
                        vS.inc(vector.scalar_tensor_tensor(
                            dst[:, t * width:(t + 1) * width],
                            dst[:, t * width:(t + 1) * width],
                            icol[:, t:t + 1], bt[0:npart, 0:width],
                            ALU.mult, ALU.add))
                v_blk(v3)
            if halo_mask is not None:
                def v4(vector):
                    vS.wait(vector)
                    for t in tiles:
                        vS.inc(vector.tensor_scalar(
                            dst[:, t * width:(t + 1) * width],
                            dst[:, t * width:(t + 1) * width],
                            halo_mask, None, ALU.mult))
                v_blk(v4)

        pbig_bf = pbig[:, :].bitcast(BF16)

        def transpose_fm(src, dst, src_is_bf=False, ntile=16):
            ntok = ntile * 128
            ident = idb if src_is_bf else idf
            pout = pbig_bf if src_is_bf else pbig
            pairs = [(t, dc) for t in range(ntile) for dc in range(DC)]
            for gs in range(0, len(pairs), 16):
                grp = pairs[gs:gs + 16]

                def tf(tensor, grp=grp):
                    vS.wait(tensor); sS.wait(tensor)
                    for n, (t, dc) in enumerate(grp):
                        tS.inc(tensor.transpose(
                            pout[:, n * 128:(n + 1) * 128],
                            src[:, t * D + dc * 128:t * D + (dc + 1) * 128],
                            ident[:, :]))
                t_blk(tf)

                def ef(vector, grp=grp):
                    tS.wait(vector)
                    for n, (t, dc) in enumerate(grp):
                        vS.inc(vector.tensor_copy(
                            dst[:, dc * ntok + t * 128:dc * ntok + (t + 1) * 128],
                            pout[:, n * 128:(n + 1) * 128]))
                v_blk(ef)

        def dev_tap(name, src_ap):
            if name not in dev_outs:
                return

            def qf(sync):
                qS.bar(sync)
                vS.wait(sync); sS.wait(sync); tS.wait(sync); gS.wait(sync)
                gq.wait(sync)
                oS.inc(sync.dma_start(out=dev_outs[name], in_=src_ap), 16)
                oS.wait(sync)
            q_blk(qf)

            def bar(eng):
                oS.wait(eng)
            for blkfn in (t_blk, v_blk, s_blk, g_blk):
                blkfn(bar)

        def p0_sync(sync):
            qS.bar(sync)
            qS.inc(sync.dma_start(out=idf[:, :], in_=identf), 16)
            qS.inc(sync.dma_start(out=idb[:, :], in_=identb), 16)
            qS.inc(sync.dma_start(out=tri[:, :], in_=tri_bf), 16)
            qS.inc(sync.dma_start(out=cm8[:, :], in_=cmask), 16)
            qS.inc(sync.dma_start(out=ce8[:, :], in_=cexp), 16)
            qS.inc(sync.dma_start(out=hm24[:, :], in_=hmask), 16)
            qS.inc(sync.dma_start(out=caus[:, :], in_=causm), 16)
            qS.inc(sync.dma_start(out=io8[:, :], in_=iota8), 16)
            qS.inc(sync.dma_start(out=io512[:, :], in_=iota512), 16)
            qS.inc(sync.dma_start(out=iost[:, :], in_=iota_st), 16)
            qS.inc(sync.dma_start(out=s256[:, :], in_=seg256), 16)
            qS.inc(sync.dma_start(out=inb_sb[:, :], in_=in_b), 16)
            qS.inc(sync.dma_start(out=m1b_sb[:, :], in_=mlp_b1), 16)
            qS.inc(sync.dma_start(out=xpb_sb[:, :], in_=xp_b), 16)
            qS.inc(sync.dma_start(out=alog[:, :], in_=A_log), 16)
            qS.inc(sync.dma_start(out=dmat[:, :], in_=Dmat), 16)
            for bq in range(4):
                qS.inc(sync.dma_start(
                    out=A[:, bq * 4 * D:(bq + 1) * 4 * D].rearrange(
                        "p (t d) -> p t d", d=D),
                    in_=xs.rearrange("(t p) d -> p t d", p=128)[:, bq * 4:(bq + 1) * 4, :]), 16)
            qS.inc(sync.dma_start(out=halo_x[:, :], in_=xh), 16)
        q_blk(p0_sync)

        def p0_gp(gpsimd):
            gq.bar(gpsimd)
            gq.inc(gpsimd.dma_start(
                out=xpw8[:, :].rearrange("p (k n) -> p k n", k=8),
                in_=xp_w.rearrange("(k p) n -> p k n", p=128)), 16)
        g_blk(p0_gp)

        def p0_v(vector):
            vector.memset(ones_f[:, :], 1.0)
            vector.memset(ones_b[:, :], 1.0)
            vS.inc(vector.memset(onesr_f[:, :], 1.0))
        v_blk(p0_v)

        def p0_s(scalar):
            qS.wait(scalar)
            sS.inc(scalar.activation(svec[:, :], alog[:, :], AF.Exp))
            sS.wait(scalar)
            sS.inc(scalar.activation(dvec[:, :], svec[:, :], AF.Exp, scale=-1.0))
        s_blk(p0_s)

        def p0_v2(vector):
            sS.wait(vector)
            vS.inc(vector.memset(dect[:, :], 1.0))
            vS.wait(vector)
            vS.inc(vector.tensor_scalar(dect[:, :], dect[:, :], dvec[:, :],
                                        None, ALU.mult))
            vS.wait(vector)
            vS.inc(vector.memset(
                dect[:, :].rearrange("p (b t) -> p b t", b=8)[:, :, 0:1], 0.0))
        v_blk(p0_v2)

        def p0_t(tensor):
            sS.wait(tensor); vS.wait(tensor); qS.wait(tensor)
            tS.inc(tensor.transpose(pb[0][0:1, 0:16], svec[:, :],
                                    idf[0:16, 0:16]))
        t_blk(p0_t)

        def p0_v3(vector):
            tS.wait(vector)
            vS.inc(vector.tensor_copy(srow[0:1, :], pb[0][0:1, 0:16]))
        v_blk(p0_v3)

        layernorm(A, A, list(range(16)), Dd, wb=None,
                  mcol=stat_m, icol=stat_i)
        layernorm(halo_x, halo_x, [0], halo_n1, wb=None,
                  mcol=hst[:, 0:1], icol=hst[:, 1:2])
        dev_tap("xn_p", A[:, :])

        load_bcast(ln1_w[None, :], lnw_bc)
        load_bcast(ln1_b[None, :], lnb_bc)
        layernorm(A, C, list(range(16)), Dd, wb=(lnw_bc, lnb_bc),
                  mcol=stat_m, icol=stat_i)
        layernorm(halo_x, halo_n1, [0], Dd, wb=(lnw_bc, lnb_bc),
                  mcol=hst[:, 0:1], icol=hst[:, 1:2], halo_mask=hm24[:, :])
        dev_tap("xn1_p", C[:, :])

        for kk in range(4):
            load_row(convw[kk:kk + 1, :])
            bcast_lnv(w_bc[:, kk * D:(kk + 1) * D])

        def conv_round(get_src, kk):
            pre_v = vS.v
            pre_g = gS.v
            pre_gq = gq.v

            def vf(vector):
                gS.wait(vector, pre_g); vS.wait(vector, pre_v)
                gq.wait(vector, pre_gq)
                for t in range(10):
                    vS.inc(vector.tensor_tensor(
                        tmp4[:, :], get_src(t), w_bc[:, kk * D:(kk + 1) * D],
                        ALU.mult))
                    vS.wait(vector)
                    vS.inc(vector.tensor_tensor(
                        A[:, t * D:(t + 1) * D], A[:, t * D:(t + 1) * D],
                        tmp4[:, :], ALU.add))
                    vS.wait(vector)
            v_blk(vf)

            def gf(gpsimd):
                gS.wait(gpsimd, pre_g); vS.wait(gpsimd, pre_v)
                gq.wait(gpsimd, pre_gq)
                for t in range(10, 16):
                    gS.inc(gpsimd.tensor_tensor(
                        tmp4b[:, :], get_src(t), w_bc[:, kk * D:(kk + 1) * D],
                        ALU.mult))
                    gS.wait(gpsimd)
                    gS.inc(gpsimd.tensor_tensor(
                        A[:, t * D:(t + 1) * D], A[:, t * D:(t + 1) * D],
                        tmp4b[:, :], ALU.add))
                    gS.wait(gpsimd)
            g_blk(gf)

        load_row(convb[None, :])

        def cb_t(tensor):
            qS.wait(tensor); vS.wait(tensor)
            tS.inc(tensor.matmul(pbig[:, 0:512], onesr_f[0:1, :],
                                 lnv[0:1, 0:512], start=True, stop=True))
            tS.inc(tensor.matmul(pbig[:, 512:1024], onesr_f[0:1, :],
                                 lnv[0:1, 512:1024], start=True, stop=True))
        t_blk(cb_t)

        def cb_v(vector):
            tS.wait(vector)
            for t in range(16):
                vS.inc(vector.tensor_tensor(
                    A[:, t * D:(t + 1) * D], A[:, t * D:(t + 1) * D],
                    pbig[:, 0:1024], ALU.add))
        v_blk(cb_v)

        conv_round(lambda t: C[:, t * D:(t + 1) * D], 3)
        for shv in (1, 2, 3):
            kk = 3 - shv

            def conv_g(gpsimd, shv=shv):
                gq.bar(gpsimd)
                vS.wait(gpsimd); gS.wait(gpsimd); gq.wait(gpsimd)
                gq.inc(gpsimd.dma_start(
                    out=Dd[shv:128, :].rearrange("p (t d) -> p t d", d=D),
                    in_=C[0:128 - shv, :].rearrange("p (t d) -> p t d", d=D)), 16)
                gq.inc(gpsimd.dma_start(
                    out=Dd[0:shv, :].rearrange("p (b c d) -> p b c d",
                                               b=8, c=2)[:, :, 1, :],
                    in_=C[128 - shv:128, :].rearrange("p (b c d) -> p b c d",
                                                      b=8, c=2)[:, :, 0, :]), 16)
                for bi in range(8):
                    gq.inc(gpsimd.dma_start(
                        out=Dd[0:shv, bi * 2 * D:bi * 2 * D + D],
                        in_=halo_n1[3 * bi + 3 - shv:3 * bi + 3, :]), 16)
            g_blk(conv_g)
            conv_round(lambda t: Dd[:, t * D:(t + 1) * D], kk)

        dev_tap("xloc_p", A[:, :])
        if BUILD_UPTO < 2:
            return nc, dev_outs

        load_bcast(lns_w[None, :], lnw_bc)
        load_bcast(lns_b[None, :], lnb_bc)
        layernorm(A, C, list(range(16)), Dd, wb=(lnw_bc, lnb_bc),
                  mcol=stat_m, icol=stat_i)
        dev_tap("xn2_p", C[:, :])
        transpose_fm(C, Dd, src_is_bf=True)
        sxp_fm = Abf[:, 0:8 * TOK]
        z_fm = Abf[:, 8 * TOK:16 * TOK]

        def inw_load(gpsimd):
            gq.bar(gpsimd)
            vS.wait(gpsimd)
            gq.inc(gpsimd.dma_start(
                out=C[:, :].rearrange("p (kk n) -> p kk n", kk=8),
                in_=in_w.rearrange("(kk p) n -> p kk n", p=128)), 16)
        g_blk(inw_load)

        for fc in range(16):
            def inw_t(tensor, fc=fc):
                gq.wait(tensor); vS.wait(tensor)
                if fc > 0:
                    sS.wait(tensor)
                for tq in range(4):
                    for kk in range(8):
                        tS.inc(tensor.matmul(
                            pb[tq][:, :],
                            C[:, kk * 2048 + fc * 128:kk * 2048 + (fc + 1) * 128],
                            Dd[:, kk * 2048 + tq * 512:kk * 2048 + (tq + 1) * 512],
                            start=(kk == 0), stop=(kk == 7)))
            t_blk(inw_t)

            def inw_s(scalar, fc=fc):
                tS.wait(scalar)
                func = AF.Silu if fc < 8 else AF.Sigmoid
                dst = sxp_fm if fc < 8 else z_fm
                dc = fc % 8
                for tq in range(4):
                    sS.inc(scalar.activation(
                        dst[:, dc * 2048 + tq * 512:dc * 2048 + (tq + 1) * 512],
                        pb[tq][:, :], func, bias=inb_sb[:, fc:fc + 1]))
            s_blk(inw_s)

        dev_tap("sxp_fm", sxp_fm[:, 0:2048])
        dev_tap("z_fm", z_fm[:, 0:2048])

        def u_t(tensor):
            sS.wait(tensor)
            for tq in range(4):
                for kk in range(8):
                    tS.inc(tensor.matmul(
                        pbig[0:16, tq * 512:(tq + 1) * 512],
                        xpw8[:, kk * 16:(kk + 1) * 16],
                        sxp_fm[:, kk * 2048 + tq * 512:kk * 2048 + (tq + 1) * 512],
                        start=(kk == 0), stop=(kk == 7)))
        t_blk(u_t)

        def u_v(vector):
            tS.wait(vector); qS.wait(vector)
            vS.inc(vector.tensor_scalar(u_fm[:, :], pbig[0:16, :],
                                        xpb_sb[:, :], None, ALU.add))
        v_blk(u_v)
        dev_tap("u_fm", u_fm[:, :])

        def scan1(vector):
            vS.wait(vector)
            vS.inc(vector.tensor_tensor_scan(hs_fm[:, :], dect[:, :],
                                             u_fm[:, :], 0.0,
                                             ALU.mult, ALU.add))
            vS.wait(vector)
            vS.inc(vector.tensor_copy(
                e16[:, :],
                hs_fm[:, :].rearrange("p (b t) -> p b t", b=8)[:, :, 255]))
        v_blk(scan1)
        dev_tap("hs1", hs_fm[:, :])
        if BUILD_UPTO < 3:
            return nc, dev_outs

        def ag1_q(sync):
            qS.bar(sync)
            vS.wait(sync)
            dst = ag1_in[0:1, 0:128].rearrange("o (b n) -> (o n) b", b=8)
            qS.inc(sync.dma_start(out=dst, in_=e16[:, :],
                                  allow_slow_non_contiguous=True), 16)
        q_blk(ag1_q)

        def ag1_g(gpsimd):
            qS.wait(gpsimd)
            cS.inc(gpsimd.collective_compute(
                "AllGather", ALU.bypass, replica_groups=[list(range(8))],
                ins=[ag1_in], outs=[ag1_out]))
        g_blk(ag1_g)

        def ag1_back(sync):
            qS.bar(sync)
            cS.wait(sync)
            qS.inc(sync.dma_start(out=eag[:, 0:128], in_=ag1_out[:, 0:128]), 16)
        q_blk(ag1_back)

        def carry_t(tensor):
            vS.wait(tensor)
            tS.inc(tensor.matmul(pb[0][0:8, 0:16], onesr_f[0:1, 0:8],
                                 srow[0:1, 0:16], start=True, stop=True))
        t_blk(carry_t)

        def carry_v(vector):
            tS.wait(vector); qS.wait(vector)
            vS.inc(vector.tensor_scalar(s8x16[:, :], pb[0][0:8, 0:16],
                                        ce8[:, :], -256.0, ALU.mult, ALU.mult))
        v_blk(carry_v)

        def carry_s(scalar):
            vS.wait(scalar)
            sS.inc(scalar.activation(wjn[:, :], s8x16[:, :], AF.Exp))
        s_blk(carry_s)

        def carry_v2(vector):
            sS.wait(vector)
            vS.inc(vector.tensor_scalar(wjn[:, :], wjn[:, :], cm8[:, :],
                                        None, ALU.mult))
            vS.wait(vector)
            vS.inc(vector.tensor_tensor(
                prod8[:, :].rearrange("p (b n) -> p b n", b=8),
                eag[:, 0:128].rearrange("p (b n) -> p b n", b=8),
                wjn[:, :].unsqueeze(1).broadcast_to([8, 8, 16]),
                ALU.mult))
        v_blk(carry_v2)

        def carry_t2(tensor):
            vS.wait(tensor)
            tS.inc(tensor.matmul(pb[1][:, 0:1], prod8[:, :], ones_f[0:8, 0:1],
                                 start=True, stop=True))
        t_blk(carry_t2)

        def carry_v3(vector):
            tS.wait(vector)
            vS.inc(vector.tensor_copy(c128[:, :], pb[1][:, 0:1]))
        v_blk(carry_v3)

        def carry_g(gpsimd):
            gq.bar(gpsimd)
            vS.wait(gpsimd)
            for bi in range(8):
                gq.inc(gpsimd.dma_start(out=ccorr[:, bi:bi + 1],
                                        in_=c128[bi * 16:(bi + 1) * 16, :]), 16)
        g_blk(carry_g)

        def carry_v4(vector):
            gq.wait(vector)
            vS.inc(vector.tensor_scalar(ccorr[:, :], ccorr[:, :], dvec[:, :],
                                        None, ALU.mult))
            vS.wait(vector)
            vS.inc(vector.tensor_tensor(
                u_fm[:, :].rearrange("p (b t) -> p b t", b=8)[:, :, 0],
                u_fm[:, :].rearrange("p (b t) -> p b t", b=8)[:, :, 0],
                ccorr[:, :], ALU.add))
            vS.wait(vector)
            vS.inc(vector.tensor_tensor_scan(hs_fm[:, :], dect[:, :],
                                             u_fm[:, :], 0.0,
                                             ALU.mult, ALU.add))
        v_blk(carry_v4)
        dev_tap("hs2", hs_fm[:, :])
        if BUILD_UPTO < 4:
            return nc, dev_outs

        g1_fm = Dd
        for dc in range(DC):
            def dmm_t(tensor, dc=dc):
                vS.wait(tensor); qS.wait(tensor); sS.wait(tensor)
                for tq in range(4):
                    tS.inc(tensor.matmul(
                        pb[tq][:, :], dmat[:, dc * 128:(dc + 1) * 128],
                        hs_fm[:, tq * 512:(tq + 1) * 512],
                        start=True, stop=True))
            t_blk(dmm_t)

            def dmm_v(vector, dc=dc):
                tS.wait(vector)
                for tq in range(4):
                    vS.inc(vector.tensor_tensor(
                        g1_fm[:, dc * 2048 + tq * 512:dc * 2048 + (tq + 1) * 512],
                        pb[tq][:, :],
                        z_fm[:, dc * 2048 + tq * 512:dc * 2048 + (tq + 1) * 512],
                        ALU.mult))
            v_blk(dmm_v)
        dev_tap("g1_fm", g1_fm[:, 0:2048])

        load_row(out_b[None, :])
        bcast_lnv(bias_bc)
        def outw_g(gpsimd):
            gq.bar(gpsimd)
            vS.wait(gpsimd)
            gq.inc(gpsimd.dma_start(
                out=C[:, 0:8 * D].rearrange("p (kk n) -> p kk n", kk=8),
                in_=out_w.rearrange("(kk p) n -> p kk n", p=128)), 16)
        g_blk(outw_g)

        y_bf = Abf[:, 8 * TOK:16 * TOK]
        for t in range(16):
            def y_t(tensor, t=t):
                gq.wait(tensor); vS.wait(tensor)
                if t > 0:
                    vS.wait(tensor)
                for dh in range(2):
                    for kk in range(8):
                        tS.inc(tensor.matmul(
                            pb[2 * (t % 2) + dh][:, :],
                            g1_fm[:, kk * 2048 + t * 128:kk * 2048 + (t + 1) * 128],
                            C[:, kk * D + dh * 512:kk * D + (dh + 1) * 512],
                            start=(kk == 0), stop=(kk == 7)))
            t_blk(y_t)

            def y_v(vector, t=t):
                tS.wait(vector)
                for dh in range(2):
                    vS.inc(vector.tensor_tensor(
                        y_bf[:, t * D + dh * 512:t * D + (dh + 1) * 512],
                        pb[2 * (t % 2) + dh][:, :],
                        bias_bc[:, dh * 512:(dh + 1) * 512], ALU.add))
            v_blk(y_v)

        def yspill_q(sync):
            qS.bar(sync)
            vS.wait(sync)
            src = y_bf.rearrange("p (b c d) -> p b c d", b=8, c=2)
            dst = y_sp.rearrange("(c t b) d -> t b c d", c=2, b=8)
            qS.inc(sync.dma_start(out=dst, in_=src), 16)
        q_blk(yspill_q)
        dev_tap("y_p", y_bf[:, :])

        if BUILD_UPTO < 5:
            return nc, dev_outs


        def p5_load(sync):
            qS.bar(sync)
            vS.wait(sync)
            qS.wait(sync)
            for bq in range(4):
                qS.inc(sync.dma_start(
                    out=A[:, bq * 4 * D:(bq + 1) * 4 * D].rearrange(
                        "p (t d) -> p t d", d=D),
                    in_=xb.rearrange("(t p) d -> p t d", p=128)[:, bq * 4:(bq + 1) * 4, :]), 16)
        q_blk(p5_load)

        layernorm(A, A, list(range(16)), Dd, wb=None, mcol=stat_m, icol=stat_i)
        dev_tap("xn_c", A[:, :])

        def p5_hi(vector):
            vS.wait(vector); gq.wait(vector)
            for t in range(16):
                vS.inc(vector.tensor_copy(Dd[:, t * D:(t + 1) * D],
                                          A[:, t * D:(t + 1) * D]))
        v_blk(p5_hi)
        transpose_fm(Dd, C, src_is_bf=True)

        wg_f = sb("wg_f", [128, 64])
        wg_hi = sb("wg_hi", [128, 64], BF16)
        wg_lo = sb("wg_lo", [128, 64], BF16)
        SCRbf = SCR[:, :].bitcast(BF16)
        lob = SCRbf[:, 0:2048]
        lg_sb = sb("lg_sb", [128, 128])
        io8bc = sb("io8bc", [128, 8])
        gbbc = sb("gbbc", [128, 8])
        m8 = sb("m8", [128, 1]); negm = sb("negm", [128, 1])
        sume = sb("sume", [128, 1]); gate8 = sb("gate8", [128, 16])
        exp8 = sb("exp8", [128, 8])
        oh1 = sb("oh1", [128, 8]); mi8 = sb("mi8", [128, 8])
        idx8 = sb("idx8", [128, 16])
        ohb_sb = sb("ohb_sb", [128, 16 * 8], BF16)
        cs_sb = sb("cs_sb", [8, 17])
        pst_sb = sb("pst_sb", [16, 8])
        cnt8 = sb("cnt8", [8, 1])
        cAG = sb("cAG", [8, 8])
        pgrow = sb("pgrow", [1, 8])
        pgbc = sb("pgbc", [128, 8])
        gb_row = sb("gb_row", [1, 8])
        tmp8 = sb("tmp8", [128, 8])
        psrow = sb("psrow", [1, 128])
        cntrow = psrow[0:1, 32:40]
        brow = psrow[0:1, 16:27]
        base_bc = AUX[:, 0:8]
        erow_f = psrow[0:1, 48:64]
        erow_i = sb("erow_i", [1, 16], mybir.dt.int32)
        prefbc_all = sb("prefbc_all", [128, 128])
        cum_sb = sb("cum_sb", [128, 128])
        posc = sb("posc", [128, 1])
        keepc = sb("keepc", [128, 1])
        kg_sb = sb("kg_sb", [128, 16])
        kg2_sb = sb("kg2_sb", [128, 16])
        slot_sb = sb("slot_sb", [128, 16])

        def gb_q(sync):
            qS.bar(sync)
            qS.inc(sync.dma_start(out=gb_row[:, :], in_=gate_b), 16)
        q_blk(gb_q)

        def io8_t(tensor):
            vS.wait(tensor); qS.wait(tensor)
            tS.inc(tensor.matmul(pb[0][:, 0:8], onesr_f[0:1, :], io8[0:1, :],
                                 start=True, stop=True))
            tS.inc(tensor.matmul(pb[1][:, 0:8], onesr_f[0:1, :],
                                 gb_row[0:1, :], start=True, stop=True))
        t_blk(io8_t)

        def io8_v(vector):
            tS.wait(vector)
            vS.inc(vector.tensor_copy(io8bc[:, :], pb[0][:, 0:8]))
            vS.inc(vector.tensor_copy(gbbc[:, :], pb[1][:, 0:8]))
        v_blk(io8_v)

        def router_logits(wgT_dram, hi_fm):
            def wg_ld(sync):
                qS.bar(sync)
                tS.wait(sync)
                qS.inc(sync.dma_start(
                    out=wg_f[:, :].rearrange("p (k n) -> p k n", k=8),
                    in_=wgT_dram.rearrange("(k p) n -> p k n", p=128)), 16)
            q_blk(wg_ld)

            def wg_split(vector):
                qS.wait(vector)
                vS.inc(vector.tensor_copy(wg_hi[:, :], wg_f[:, :]))
                vS.wait(vector)
                vS.inc(vector.tensor_tensor(wg_lo[:, :], wg_f[:, :],
                                            wg_hi[:, :], ALU.subtract))
            v_blk(wg_split)

            def lo_v(vector):
                vS.wait(vector)
                for t in range(16):
                    vS.inc(vector.tensor_tensor(
                        Dd[:, t * D:(t + 1) * D], A[:, t * D:(t + 1) * D],
                        Dd[:, t * D:(t + 1) * D], ALU.subtract))
            v_blk(lo_v)

            for kk in range(DC):
                def lo_tr(tensor, kk=kk):
                    vS.wait(tensor)
                    for n in range(16):
                        tS.inc(tensor.transpose(
                            pbig_bf[:, 2048 + n * 128:2048 + (n + 1) * 128],
                            Dd[:, n * D + kk * 128:n * D + (kk + 1) * 128],
                            idb[:, :]))
                t_blk(lo_tr)

                def lo_ev(vector, kk=kk):
                    tS.wait(vector)
                    for n in range(16):
                        vS.inc(vector.tensor_copy(
                            lob[:, n * 128:(n + 1) * 128],
                            pbig_bf[:, 2048 + n * 128:2048 + (n + 1) * 128]))
                v_blk(lo_ev)

                def lg_t(tensor, kk=kk):
                    vS.wait(tensor)
                    for n in range(16):
                        st = (kk == 0 and n == 0)
                        tS.inc(tensor.matmul(
                            pbig[:, n * 8:(n + 1) * 8],
                            hi_fm[:, kk * 2048 + n * 128:kk * 2048 + (n + 1) * 128],
                            wg_hi[:, kk * 8:(kk + 1) * 8],
                            start=st, stop=False))
                        tS.inc(tensor.matmul(
                            pbig[:, n * 8:(n + 1) * 8],
                            lob[:, n * 128:(n + 1) * 128],
                            wg_hi[:, kk * 8:(kk + 1) * 8],
                            start=False, stop=False))
                        last = (kk == DC - 1 and n == 15)
                        tS.inc(tensor.matmul(
                            pbig[:, n * 8:(n + 1) * 8],
                            hi_fm[:, kk * 2048 + n * 128:kk * 2048 + (n + 1) * 128],
                            wg_lo[:, kk * 8:(kk + 1) * 8],
                            start=False, stop=last))
                t_blk(lg_t)

            def lg_ev(vector):
                tS.wait(vector)
                vS.inc(vector.tensor_copy(lg_sb[:, :], pbig[:, 0:128]))
            v_blk(lg_ev)

        def router_chain(use_gb, cap, kgdst, want_slot, agin, agout):
            for t in range(16):
                def rv(vector, t=t):
                    vS.wait(vector); sS.wait(vector)
                    lg = lg_sb[:, t * 8:(t + 1) * 8]
                    if use_gb:
                        vS.inc(vector.tensor_tensor(lg, lg, gbbc[:, :],
                                                    ALU.add))
                        vS.wait(vector)
                    vS.inc(vector.reduce_max(m8[:, :], lg, axis=AX.X))
                    vS.wait(vector)
                    vS.inc(vector.tensor_scalar(negm[:, :], m8[:, :], -1.0,
                                                None, ALU.mult))
                v_blk(rv)

                def rs(scalar, t=t):
                    vS.wait(scalar)
                    sS.inc(scalar.activation(exp8[:, :],
                                             lg_sb[:, t * 8:(t + 1) * 8],
                                             AF.Exp, bias=negm[:, :],
                                             accum_out=sume[:, :]))
                s_blk(rs)

                def rv2(vector, t=t):
                    sS.wait(vector)
                    lg = lg_sb[:, t * 8:(t + 1) * 8]
                    vS.inc(vector.reciprocal(gate8[:, t:t + 1], sume[:, :]))
                    vS.inc(vector.tensor_scalar(oh1[:, :], lg, m8[:, :],
                                                None, ALU.is_ge))
                    vS.wait(vector)
                    vS.inc(vector.tensor_scalar(oh1[:, :], oh1[:, :],
                                                -99.0, 99.0,
                                                ALU.mult, ALU.add))
                    vS.wait(vector)
                    vS.inc(vector.tensor_tensor(mi8[:, :], io8bc[:, :],
                                                oh1[:, :], ALU.add))
                    vS.wait(vector)
                    vS.inc(vector.tensor_reduce(idx8[:, t:t + 1], mi8[:, :],
                                                axis=AX.X, op=ALU.min))
                    vS.wait(vector)
                    vS.inc(vector.tensor_scalar(oh1[:, :], io8bc[:, :],
                                                idx8[:, t:t + 1], None,
                                                ALU.is_equal))
                    vS.wait(vector)
                    vS.inc(vector.tensor_copy(ohb_sb[:, t * 8:(t + 1) * 8],
                                              oh1[:, :]))
                v_blk(rv2)

            def cum_t(tensor):
                vS.wait(tensor)
                for t in range(16):
                    tS.inc(tensor.matmul(
                        pbig[:, 512 + t * 8:512 + (t + 1) * 8],
                        tri[:, :], ohb_sb[:, t * 8:(t + 1) * 8],
                        start=(t == 0), stop=(t == 15)))
                for t in range(16):
                    tS.inc(tensor.matmul(
                        pb[0][0:8, t:t + 1], ohb_sb[:, t * 8:(t + 1) * 8],
                        ones_b[:, :], start=(t == 0), stop=(t == 15)))
            t_blk(cum_t)

            def cum_v(vector):
                tS.wait(vector)
                vS.inc(vector.tensor_copy(cum_sb[:, :], pbig[:, 512:640]))
                vS.inc(vector.tensor_copy(cs_sb[:, 0:16], pb[0][0:8, 0:16]))
                vS.wait(vector)
                vS.inc(vector.memset(cs_sb[:, 16:17], 0.0))
                for t in range(16):
                    vS.wait(vector)
                    vS.inc(vector.tensor_tensor(tmp8[0:8, 0:1],
                                                cs_sb[:, 16:17],
                                                cs_sb[:, t:t + 1], ALU.add))
                    vS.wait(vector)
                    vS.inc(vector.tensor_copy(cs_sb[:, t:t + 1],
                                              cs_sb[:, 16:17]))
                    vS.wait(vector)
                    vS.inc(vector.tensor_copy(cs_sb[:, 16:17], tmp8[0:8, 0:1]))
            v_blk(cum_v)

            def pst_t(tensor):
                vS.wait(tensor)
                tS.inc(tensor.transpose(pb[1][0:16, 0:8], cs_sb[:, 0:16],
                                        idf[0:8, 0:8]))
            t_blk(pst_t)

            def pst_v(vector):
                tS.wait(vector)
                vS.inc(vector.tensor_copy(pst_sb[:, :], pb[1][0:16, 0:8]))
                vS.inc(vector.tensor_copy(cnt8[:, :], cs_sb[:, 16:17]))
            v_blk(pst_v)

            def psr_g(gpsimd):
                gq.bar(gpsimd)
                vS.wait(gpsimd)
                gq.inc(gpsimd.dma_start(
                    out=psrow[0:1, :].rearrange("o (t e) -> o t e", t=16),
                    in_=pst_sb[:, :]), 16)
            g_blk(psr_g)

            def psr_t(tensor):
                gq.wait(tensor); vS.wait(tensor)
                tS.inc(tensor.matmul(pb[3][:, 0:128], onesr_f[0:1, :],
                                     psrow[0:1, :], start=True, stop=True))
            t_blk(psr_t)

            def psr_v(vector):
                tS.wait(vector)
                vS.inc(vector.tensor_copy(prefbc_all[:, :], pb[3][:, 0:128]))
            v_blk(psr_v)

            if want_slot:
                def sb_t(tensor):
                    vS.wait(tensor)
                    tS.inc(tensor.transpose(pb[2][0:1, 0:8], cnt8[:, :],
                                            idf[0:8, 0:8]))
                t_blk(sb_t)

                def sb_v(vector):
                    thr16 = psrow[0:1, 80:96]
                    ge16 = psrow[0:1, 64:80]
                    tS.wait(vector)
                    vS.inc(vector.tensor_copy(cntrow[:, :], pb[2][0:1, 0:8]))
                    vS.wait(vector)
                    vS.inc(vector.memset(brow[0:1, 0:1], 0.0))
                    vS.wait(vector)
                    vS.inc(vector.tensor_scalar(
                        thr16[:, :], io512[0:1, 0:16], 256.0, 1.0,
                        ALU.mult, ALU.add))
                    for e in range(7):
                        vS.wait(vector)
                        vS.inc(vector.tensor_tensor(
                            brow[0:1, 9:10], brow[0:1, e:e + 1],
                            cntrow[0:1, e:e + 1], ALU.add))
                        vS.wait(vector)
                        vS.inc(vector.tensor_tensor(
                            ge16[:, :],
                            brow[0:1, 9:10].broadcast_to([1, 16]),
                            thr16[:, :], ALU.is_ge))
                        vS.wait(vector)
                        vS.inc(vector.tensor_reduce(
                            brow[0:1, 10:11], ge16[:, :],
                            axis=AX.X, op=ALU.add))
                        vS.wait(vector)
                        vS.inc(vector.tensor_scalar(
                            brow[0:1, e + 1:e + 2], brow[0:1, 10:11],
                            256.0, None, ALU.mult))
                v_blk(sb_v)

                def sb_t2(tensor):
                    vS.wait(tensor)
                    tS.inc(tensor.matmul(pb[2][:, 0:8], onesr_f[0:1, :],
                                         brow[0:1, 0:8], start=True, stop=True))
                    tS.inc(tensor.matmul(pb[3][0:16, 0:8], onesr_f[0:1, 0:16],
                                         brow[0:1, 0:8], start=True, stop=True))
                t_blk(sb_t2)

                def sb_v2(vector):
                    tS.wait(vector)
                    vS.inc(vector.tensor_copy(base_bc[:, :], pb[2][:, 0:8]))
                    vS.wait(vector)
                    vS.inc(vector.tensor_scalar(ccorr[:, 0:8], pb[3][0:16, 0:8],
                                                s256[:, :], None, ALU.is_le))
                    vS.wait(vector)
                    vS.inc(vector.reduce_sum(e16[:, 0:1], ccorr[:, 0:8],
                                             axis=AX.X))
                    vS.wait(vector)
                    vS.inc(vector.tensor_scalar(e16[:, 0:1], e16[:, 0:1],
                                                1.0, -1.0, ALU.mult, ALU.add))
                v_blk(sb_v2)

                def sb_t3(tensor):
                    vS.wait(tensor)
                    tS.inc(tensor.transpose(pb[2][0:1, 0:16], e16[:, 0:1],
                                            idf[0:16, 0:16]))
                t_blk(sb_t3)

                def sb_v3(vector):
                    tS.wait(vector)
                    vS.inc(vector.tensor_copy(erow_f[:, :], pb[2][0:1, 0:16]))
                    vS.wait(vector)
                    vS.inc(vector.tensor_copy(erow_i[:, :], erow_f[:, :]))
                v_blk(sb_v3)

            def ag_q(sync):
                qS.bar(sync)
                vS.wait(sync)
                qS.inc(sync.dma_start(out=agin[0:1, 0:8], in_=cnt8[:, :],
                                      allow_slow_non_contiguous=True), 16)
            q_blk(ag_q)

            def ag_g(gpsimd):
                qS.wait(gpsimd)
                cS.inc(gpsimd.collective_compute(
                    "AllGather", ALU.bypass, replica_groups=[list(range(8))],
                    ins=[agin], outs=[agout]))
            g_blk(ag_g)

            def ag_b(sync):
                qS.bar(sync)
                cS.wait(sync)
                qS.inc(sync.dma_start(out=cAG[:, :], in_=agout), 16)
            q_blk(ag_b)

            def pg_t(tensor):
                qS.wait(tensor); vS.wait(tensor)
                tS.inc(tensor.matmul(pb[2][0:1, 0:8], cm8[:, :], cAG[:, :],
                                     start=True, stop=True))
            t_blk(pg_t)

            def pg_v(vector):
                tS.wait(vector)
                vS.inc(vector.tensor_copy(pgrow[:, :], pb[2][0:1, 0:8]))
            v_blk(pg_v)

            def pg_t2(tensor):
                vS.wait(tensor)
                tS.inc(tensor.matmul(pb[2][:, 0:8], onesr_f[0:1, :],
                                     pgrow[0:1, :], start=True, stop=True))
            t_blk(pg_t2)

            def pg_v2(vector):
                tS.wait(vector)
                vS.inc(vector.tensor_copy(pgbc[:, :], pb[2][:, 0:8]))
            v_blk(pg_v2)

            for t in range(16):
                def pc_v(vector, t=t):
                    vS.wait(vector)
                    vS.inc(vector.tensor_tensor(
                        tmp8[:, :], cum_sb[:, t * 8:(t + 1) * 8],
                        prefbc_all[:, t * 8:(t + 1) * 8], ALU.add))
                    vS.wait(vector)
                    if want_slot:
                        vS.inc(vector.tensor_tensor(
                            mi8[:, :], ohb_sb[:, t * 8:(t + 1) * 8],
                            tmp8[:, :], ALU.mult))
                        vS.wait(vector)
                        vS.inc(vector.reduce_sum(posc[:, :], mi8[:, :],
                                                 axis=AX.X))
                        vS.wait(vector)
                        vS.inc(vector.tensor_scalar(posc[:, :], posc[:, :],
                                                    1.0, -1.0, ALU.mult,
                                                    ALU.add))
                        vS.wait(vector)
                        vS.inc(vector.tensor_tensor(
                            mi8[:, :], ohb_sb[:, t * 8:(t + 1) * 8],
                            base_bc[:, :], ALU.mult))
                        vS.wait(vector)
                        vS.inc(vector.reduce_sum(keepc[:, :], mi8[:, :],
                                                 axis=AX.X))
                        vS.wait(vector)
                        vS.inc(vector.tensor_tensor(slot_sb[:, t:t + 1],
                                                    posc[:, :], keepc[:, :],
                                                    ALU.add))
                        vS.wait(vector)
                    vS.inc(vector.tensor_tensor(tmp8[:, :], tmp8[:, :],
                                                pgbc[:, :], ALU.add))
                    vS.wait(vector)
                    vS.inc(vector.tensor_tensor(
                        mi8[:, :], ohb_sb[:, t * 8:(t + 1) * 8],
                        tmp8[:, :], ALU.mult))
                    vS.wait(vector)
                    vS.inc(vector.reduce_sum(posc[:, :], mi8[:, :], axis=AX.X))
                    vS.wait(vector)
                    vS.inc(vector.tensor_scalar(posc[:, :], posc[:, :],
                                                1.0, -1.0, ALU.mult, ALU.add))
                    vS.wait(vector)
                    vS.inc(vector.tensor_scalar(keepc[:, :], posc[:, :],
                                                float(cap), None, ALU.is_lt))
                    vS.wait(vector)
                    vS.inc(vector.tensor_tensor(kgdst[:, t:t + 1],
                                                keepc[:, :],
                                                gate8[:, t:t + 1], ALU.mult))
                v_blk(pc_v)

        router_logits(Wg, C)
        dev_tap("lg0", lg_sb[:, :])
        router_chain(False, CAP0, kg_sb, False, ag2_in, ag2_out)
        dev_tap("kg0", kg_sb[:, :])

        g_sp = nc.dram_tensor("g_sp", [TOK, D], BF16, kind="Internal").ap()
        load_bcast(be[None, :], bias_bc)

        def we_load(gpsimd):
            gq.bar(gpsimd)
            vS.wait(gpsimd)
            gq.inc(gpsimd.dma_start(
                out=Dd[:, 0:8 * D].rearrange("p (kk n) -> p kk n", kk=8),
                in_=We.rearrange("(kk p) n -> p kk n", p=128)), 16)
        g_blk(we_load)

        gstage = [SCRbf[:, 0:2048], SCRbf[:, 6144:8192]]
        for t in range(16):
            def we_t(tensor, t=t):
                gq.wait(tensor); vS.wait(tensor)
                if t > 1:
                    qS.wait(tensor)
                for dh in range(2):
                    for kk in range(8):
                        tS.inc(tensor.matmul(
                            pb[2 * (t % 2) + dh][:, :],
                            C[:, kk * 2048 + t * 128:kk * 2048 + (t + 1) * 128],
                            Dd[:, kk * D + dh * 512:kk * D + (dh + 1) * 512],
                            start=(kk == 0), stop=(kk == 7)))
            t_blk(we_t)

            def we_v(vector, t=t):
                tS.wait(vector); sS.wait(vector)
                for dh in range(2):
                    vS.inc(vector.tensor_tensor(
                        tmp4[:, dh * 512:(dh + 1) * 512],
                        pb[2 * (t % 2) + dh][:, :],
                        bias_bc[:, dh * 512:(dh + 1) * 512], ALU.add))
            v_blk(we_v)

            def we_s(scalar, t=t):
                vS.wait(scalar)
                sS.inc(scalar.activation(gstage[t % 2][:, 0:D],
                                         tmp4[:, :], AF.Silu,
                                         scale=kg_sb[:, t:t + 1]))
            s_blk(we_s)

            def we_q(sync, t=t):
                qS.bar(sync)
                sS.wait(sync)
                qS.inc(sync.dma_start(
                    out=g_sp[t * 128:(t + 1) * 128, :],
                    in_=gstage[t % 2][:, 0:D]), 16)
            q_blk(we_q)
        dev_tap("xn_c2", A[:, :])

        if BUILD_UPTO < 6:
            return nc, dev_outs

        load_bcast(ln1_w[None, :], lnw_bc)
        load_bcast(ln1_b[None, :], lnb_bc)
        layernorm(A, C, list(range(16)), Dd, wb=(lnw_bc, lnb_bc),
                  mcol=stat_m, icol=stat_i)
        def ch_v(vector):
            vS.wait(vector)
            vS.inc(vector.memset(halo_n1[:, :], 0.0))
        v_blk(ch_v)

        for kk in range(4):
            load_row(convw[kk:kk + 1, :])
            bcast_lnv(w_bc[:, kk * D:(kk + 1) * D])
        load_row(convb[None, :])

        def cb2_t(tensor):
            qS.wait(tensor); vS.wait(tensor)
            tS.inc(tensor.matmul(pbig[:, 0:512], onesr_f[0:1, :],
                                 lnv[0:1, 0:512], start=True, stop=True))
            tS.inc(tensor.matmul(pbig[:, 512:1024], onesr_f[0:1, :],
                                 lnv[0:1, 512:1024], start=True, stop=True))
        t_blk(cb2_t)

        def cb2_v(vector):
            tS.wait(vector)
            for t in range(16):
                vS.inc(vector.tensor_tensor(
                    A[:, t * D:(t + 1) * D], A[:, t * D:(t + 1) * D],
                    pbig[:, 0:1024], ALU.add))
        v_blk(cb2_v)

        conv_round(lambda t: C[:, t * D:(t + 1) * D], 3)
        for shv in (1, 2, 3):
            kk = 3 - shv

            def conv_g2(gpsimd, shv=shv):
                gq.bar(gpsimd)
                vS.wait(gpsimd); gS.wait(gpsimd); gq.wait(gpsimd)
                gq.inc(gpsimd.dma_start(
                    out=Dd[shv:128, :].rearrange("p (t d) -> p t d", d=D),
                    in_=C[0:128 - shv, :].rearrange("p (t d) -> p t d", d=D)), 16)
                gq.inc(gpsimd.dma_start(
                    out=Dd[0:shv, D:16 * D].rearrange("p (t d) -> p t d", d=D),
                    in_=C[128 - shv:128, 0:15 * D].rearrange(
                        "p (t d) -> p t d", d=D)), 16)
                gq.inc(gpsimd.dma_start(
                    out=Dd[0:shv, 0:D],
                    in_=halo_n1[3 - shv:3, :]), 16)
            g_blk(conv_g2)
            conv_round(lambda t: Dd[:, t * D:(t + 1) * D], kk)

        dev_tap("xloc_c", A[:, :])

        def yp_q(sync):
            qS.bar(sync)
            vS.wait(sync); gS.wait(sync)
            for bq in range(2):
                qS.inc(sync.dma_start(
                    out=C[:, bq * 8 * D:(bq + 1) * 8 * D].rearrange(
                        "p (t d) -> p t d", d=D),
                    in_=y_sp.rearrange("(t p) d -> p t d", p=128)[:, bq * 8:(bq + 1) * 8, :]), 16)
        q_blk(yp_q)

        def x1_v(vector):
            qS.wait(vector); vS.wait(vector)
            for t in range(16):
                vS.inc(vector.tensor_tensor(
                    A[:, t * D:(t + 1) * D], A[:, t * D:(t + 1) * D],
                    C[:, t * D:(t + 1) * D], ALU.add))
        v_blk(x1_v)
        dev_tap("x1_c", A[:, :])

        load_bcast(ln2_w[None, :], lnw_bc)
        load_bcast(ln2_b[None, :], lnb_bc)
        layernorm(A, C, list(range(16)), Dd, wb=(lnw_bc, lnb_bc),
                  mcol=stat_m, icol=stat_i)

        def w1_load(gpsimd):
            gq.bar(gpsimd)
            vS.wait(gpsimd)
            gq.inc(gpsimd.dma_start(
                out=Dd[:, :].rearrange("p (kk n) -> p kk n", kk=8),
                in_=mlp_w1.rearrange("(kk p) n -> p kk n", p=128)), 16)
        g_blk(w1_load)
        load_row(mlp_b2[None, :])
        bcast_lnv(bias_bc)

        ln2g = AUXbf[:, 0:4096]
        h1g = AUXbf[:, 4096:4096 + 8192]
        w2psum = [pb[0][:, :], pb[1][:, :], pb[2][:, :], pb[3][:, :],
                  pbig[:, 0:512], pbig[:, 512:1024],
                  pbig[:, 1024:1536], pbig[:, 1536:2048]]
        for tg in range(4):
            for rnd in range(2):
                def tg_tr(tensor, tg=tg, rnd=rnd):
                    vS.wait(tensor); sS.wait(tensor); gq.wait(tensor)
                    for n in range(16):
                        ti, dc = divmod(rnd * 16 + n, 8)
                        t = tg * 4 + ti
                        tS.inc(tensor.transpose(
                            pbig_bf[:, 2048 + n * 128:2048 + (n + 1) * 128],
                            C[:, t * D + dc * 128:t * D + (dc + 1) * 128],
                            idb[:, :]))
                t_blk(tg_tr)

                def tg_ev(vector, tg=tg, rnd=rnd):
                    tS.wait(vector)
                    for n in range(16):
                        ti, dc = divmod(rnd * 16 + n, 8)
                        vS.inc(vector.tensor_copy(
                            ln2g[:, dc * 512 + ti * 128:dc * 512 + (ti + 1) * 128],
                            pbig_bf[:, 2048 + n * 128:2048 + (n + 1) * 128]))
                v_blk(tg_ev)

            for fc in range(16):
                def h1_t(tensor, tg=tg, fc=fc):
                    vS.wait(tensor)
                    if fc > 0:
                        sS.wait(tensor)
                    for kk in range(8):
                        tS.inc(tensor.matmul(
                            pb[fc % 2][:, :],
                            Dd[:, kk * 2048 + fc * 128:kk * 2048 + (fc + 1) * 128],
                            ln2g[:, kk * 512:(kk + 1) * 512],
                            start=(kk == 0), stop=(kk == 7)))
                t_blk(h1_t)

                def h1_s(scalar, tg=tg, fc=fc):
                    tS.wait(scalar)
                    sS.inc(scalar.activation(
                        h1g[:, fc * 512:(fc + 1) * 512],
                        pb[fc % 2][:, :], AF.Gelu,
                        bias=m1b_sb[:, fc:fc + 1]))
                s_blk(h1_s)

            for kk in range(16):
                def w2_ld(gpsimd, tg=tg, kk=kk):
                    gq.bar(gpsimd)
                    sS.wait(gpsimd)
                    if kk > 1 or tg > 0:
                        tS.wait(gpsimd)
                    gq.inc(gpsimd.dma_start(
                        out=(wka if kk % 2 == 0 else wkb)[:, 0:D],
                        in_=mlp_w2[kk * 128:(kk + 1) * 128, :]), 16)
                g_blk(w2_ld)

                def w2_t(tensor, tg=tg, kk=kk):
                    gq.wait(tensor); sS.wait(tensor)
                    if kk == 0:
                        vS.wait(tensor)
                    wt = (wka if kk % 2 == 0 else wkb)
                    for ti in range(4):
                        for dh in range(2):
                            tS.inc(tensor.matmul(
                                w2psum[ti * 2 + dh],
                                h1g[:, kk * 512 + ti * 128:kk * 512 + (ti + 1) * 128],
                                wt[:, dh * 512:(dh + 1) * 512],
                                start=(kk == 0), stop=(kk == 15)))
                t_blk(w2_t)

            def w2_v(vector, tg=tg):
                tS.wait(vector)
                for ti in range(4):
                    t = tg * 4 + ti
                    for dh in range(2):
                        vS.inc(vector.tensor_tensor(
                            tmp4[:, dh * 512:(dh + 1) * 512],
                            w2psum[ti * 2 + dh],
                            bias_bc[:, dh * 512:(dh + 1) * 512], ALU.add))
                    vS.wait(vector)
                    vS.inc(vector.tensor_tensor(
                        A[:, t * D:(t + 1) * D], A[:, t * D:(t + 1) * D],
                        tmp4[:, :], ALU.add))
                    vS.wait(vector)
            v_blk(w2_v)
        dev_tap("smix_c", A[:, :])
        if BUILD_UPTO < 7:
            return nc, dev_outs

        for t in range(16):
            def x2_q(sync, t=t):
                qS.bar(sync)
                vS.wait(sync); sS.wait(sync)
                qS.inc(sync.dma_start(out=tmp4[:, :],
                                      in_=xb[t * 128:(t + 1) * 128, :]), 16)
                qS.inc(sync.dma_start(out=wka[:, 0:D],
                                      in_=g_sp[t * 128:(t + 1) * 128, :]), 16)
            q_blk(x2_q)

            def x2_v(vector, t=t):
                qS.wait(vector); vS.wait(vector)
                vS.inc(vector.tensor_tensor(tmp4b[:, :],
                                            A[:, t * D:(t + 1) * D],
                                            wka[:, 0:D], ALU.mult))
                vS.wait(vector)
                vS.inc(vector.tensor_tensor(A[:, t * D:(t + 1) * D],
                                            tmp4[:, :], tmp4b[:, :], ALU.add))
            v_blk(x2_v)

            def x2_s(sync, t=t):
                qS.bar(sync)
                vS.wait(sync)
                qS.inc(sync.dma_start(out=x2_sp[t * 128:(t + 1) * 128, :],
                                      in_=A[:, t * D:(t + 1) * D]), 16)
            q_blk(x2_s)
        dev_tap("x2_c", A[:, :])

        layernorm(A, A, list(range(16)), Dd, wb=None, mcol=stat_m,
                  icol=stat_i)
        dev_tap("xn1_L1", A[:, :])
        if BUILD_UPTO < 8:
            return nc, dev_outs


        xn1tok_sp = nc.dram_tensor("xn1tok_sp", [TOK, D], BF16,
                                   kind="Internal").ap()
        Qfm = Abf[:, 0:16384]
        Kfm = Abf[:, 16384:32768]
        Vtm = AUXbf[:, 0:16384]
        Ebuf = [SCRbf[:, 0:2048], SCRbf[:, 2048:4096]]
        cum_bf = cum_sb[:, :].bitcast(BF16)
        vaug = [cum_bf[:, 0:65], cum_bf[:, 128:193]]
        lst = SCR[64:65, 2048:4096]
        au_st = SCR[0:64, 2048:4096].bitcast(BF16)
        od_st = SCR[0:64, 3072:4096].bitcast(BF16)

        ones65 = prefbc_all
        nb8 = stat_m[:, 0:1]

        def p8_v0(vector):
            vS.wait(vector)
            vS.inc(vector.memset(nb8, -8.0))

            for t in range(16):
                vS.inc(vector.tensor_copy(Dd[:, t * D:(t + 1) * D],
                                          A[:, t * D:(t + 1) * D]))
        v_blk(p8_v0)

        def p8_spill(sync):
            qS.bar(sync)
            vS.wait(sync)
            qS.inc(sync.dma_start(
                out=xn1tok_sp.rearrange("(t p) d -> p t d", p=128),
                in_=Dd[:, :].rearrange("p (t d) -> p t d", d=D)), 16)
        q_blk(p8_spill)

        transpose_fm(Dd, C, src_is_bf=True)

        router_logits(gate_wT, C)
        dev_tap("lg1", lg_sb[:, :])
        router_chain(True, CAP1, kg2_sb, True, ag3_in, ag3_out)
        dev_tap("kg1", kg2_sb[:, :])
        dev_tap("slot1", slot_sb[:, :])

        def qk_load(gpsimd):
            gq.bar(gpsimd)
            vS.wait(gpsimd); tS.wait(gpsimd)
            gq.inc(gpsimd.dma_start(
                out=Dd[:, 0:8192].rearrange("p (kk n) -> p kk n", kk=8),
                in_=Wq.rearrange("(kk p) n -> p kk n", p=128)), 16)
            gq.inc(gpsimd.dma_start(
                out=Dd[:, 8192:16384].rearrange("p (kk n) -> p kk n", kk=8),
                in_=Wk.rearrange("(kk p) n -> p kk n", p=128)), 16)
        g_blk(qk_load)

        for which, dst in ((0, Qfm), (1, Kfm)):
            for fc in range(8):
                def qk_t(tensor, which=which, fc=fc):
                    gq.wait(tensor); vS.wait(tensor)
                    if which + fc > 0:
                        sS.wait(tensor)
                    for tq in range(4):
                        for kk in range(8):
                            tS.inc(tensor.matmul(
                                pb[tq][:, :],
                                Dd[:, which * 8192 + kk * 1024 + fc * 128:
                                   which * 8192 + kk * 1024 + (fc + 1) * 128],
                                C[:, kk * 2048 + tq * 512:kk * 2048 + (tq + 1) * 512],
                                start=(kk == 0), stop=(kk == 7)))
                t_blk(qk_t)

                def qk_s(scalar, which=which, fc=fc):
                    tS.wait(scalar)
                    for tq in range(4):
                        sS.inc(scalar.activation(
                            dst[:, fc * 2048 + tq * 512:fc * 2048 + (tq + 1) * 512],
                            pb[tq][:, :], AF.Copy))
                s_blk(qk_s)

        dev_tap("qfm0", Qfm[:, 0:2048])
        dev_tap("kfm0", Kfm[:, 0:2048])
        def wv_load(gpsimd):
            gq.bar(gpsimd)
            sS.wait(gpsimd)
            gq.inc(gpsimd.dma_start(
                out=Dd[:, 0:8192].rearrange("p (kk n) -> p kk n", kk=8),
                in_=Wv.rearrange("(kk p) n -> p kk n", p=128)), 16)
        g_blk(wv_load)

        for t in range(16):
            def v_t(tensor, t=t):
                gq.wait(tensor); vS.wait(tensor)
                if t > 1:
                    sS.wait(tensor)
                for dh in range(2):
                    for kk in range(8):
                        tS.inc(tensor.matmul(
                            pb[2 * (t % 2) + dh][:, :],
                            C[:, kk * 2048 + t * 128:kk * 2048 + (t + 1) * 128],
                            Dd[:, kk * 1024 + dh * 512:kk * 1024 + (dh + 1) * 512],
                            start=(kk == 0), stop=(kk == 7)))
            t_blk(v_t)

            def v_s(scalar, t=t):
                tS.wait(scalar)
                for dh in range(2):
                    sS.inc(scalar.activation(
                        Vtm[:, t * 1024 + dh * 512:t * 1024 + (dh + 1) * 512],
                        pb[2 * (t % 2) + dh][:, :], AF.Copy))
            s_blk(v_s)

        def p8_v1(vector):
            vS.wait(vector)
            vS.inc(vector.memset(vaug[0][:, 64:65], 1.0))
            vS.inc(vector.memset(vaug[1][:, 64:65], 1.0))
            vS.inc(vector.memset(ones65[64:65, :], 1.0))
        v_blk(p8_v1)
        sS_exp = {}
        for h in range(16):
            hc, ho = divmod(h, 2)
            for kt in range(16):
                q0 = 128 * kt
                if kt >= 8:
                    base = (kt % 2) * 1024
                    W = 2048 - q0
                    chunks = [(ls, min(512, W - ls))
                              for ls in range(0, W, 512)]
                    sdep = sS_exp.get((h, kt - 2)) if kt >= 9 else None
                else:
                    base = None
                    chunks = []
                    for c in range(q0 // 512, 4):
                        gs = max(q0, 512 * c)
                        chunks.append((gs - q0, 512 * (c + 1) - gs))
                    sdep = None

                def s_t(tensor, hc=hc, ho=ho, kt=kt, q0=q0, base=base,
                        chunks=chunks, sdep=sdep, h=h):
                    vS.wait(tensor)
                    if base is None or kt == 8:
                        sS.wait(tensor)
                    elif sdep:
                        sS.wait(tensor, sdep)
                    for (ls, w) in chunks:
                        dst0 = (q0 + ls) if base is None else (base + ls)
                        tS.inc(tensor.matmul(
                            pbig[:, dst0:dst0 + w],
                            Kfm[ho * 64:(ho + 1) * 64,
                                hc * 2048 + kt * 128:hc * 2048 + (kt + 1) * 128],
                            Qfm[ho * 64:(ho + 1) * 64,
                                hc * 2048 + q0 + ls:hc * 2048 + q0 + ls + w],
                            start=True, stop=True))
                t_blk(s_t)

                def s_m(vector, kt=kt, q0=q0, base=base):
                    tS.wait(vector)
                    d0 = q0 if base is None else base
                    vS.inc(vector.tensor_tensor(
                        pbig[:, d0:d0 + 128], pbig[:, d0:d0 + 128],
                        caus[:, :], ALU.add))
                v_blk(s_m)

                def s_e(scalar, kt=kt, q0=q0, base=base):
                    tS.wait(scalar); vS.wait(scalar)
                    d0 = q0 if base is None else base
                    sS.inc(scalar.activation(
                        Ebuf[kt % 2][:, 0:2048 - q0],
                        pbig[:, d0:d0 + 2048 - q0], AF.Exp,
                        bias=nb8, scale=0.125))
                s_blk(s_e)
                sS_exp[(h, kt)] = sS.v

                def va_v(vector, h=h, kt=kt):
                    tS.wait(vector)
                    vS.inc(vector.tensor_copy(
                        vaug[kt % 2][:, 0:64],
                        Vtm[:, kt * 1024 + h * 64:kt * 1024 + (h + 1) * 64]))
                v_blk(va_v)

                avch = []
                for c in range(q0 // 512, 4):
                    gs = max(q0, 512 * c)
                    avch.append((c, gs, 512 * (c + 1) - gs))

                def av_t(tensor, h=h, kt=kt, q0=q0, chunks=avch):
                    sS.wait(tensor); vS.wait(tensor)
                    for (c, gs, w) in chunks:
                        last = (kt == min(15, 4 * c + 3))
                        tS.inc(tensor.matmul(
                            pb[c][0:65, gs - 512 * c:gs - 512 * c + w],
                            vaug[kt % 2][:, :],
                            Ebuf[kt % 2][:, gs - q0:gs - q0 + w],
                            start=(kt == 0), stop=last))
                t_blk(av_t)

            def h_v1(vector, h=h):
                tS.wait(vector); vS.wait(vector); gq.wait(vector)
                for qc in range(4):
                    vS.inc(vector.tensor_copy(
                        lst[:, qc * 512:(qc + 1) * 512], pb[qc][64:65, :]))
                vS.wait(vector)
                vS.inc(vector.reciprocal(lst, lst))
                vS.wait(vector)
                for qc in range(4):
                    vS.inc(vector.tensor_copy(
                        au_st[:, qc * 512:(qc + 1) * 512], pb[qc][0:64, :]))
            v_blk(h_v1)

            for qc in range(4):
                def inv_t(tensor, qc=qc, h=h):
                    vS.wait(tensor)
                    tS.inc(tensor.matmul(
                        pbig[0:64, 0:512], ones65[64:65, 0:64],
                        lst[:, qc * 512:(qc + 1) * 512],
                        start=True, stop=True))
                t_blk(inv_t)

                def inv_v(vector, qc=qc, h=h, hc=hc, ho=ho):
                    tS.wait(vector)
                    if ho == 0:
                        vS.inc(vector.tensor_tensor(
                            C[0:64, hc * 2048 + qc * 512:hc * 2048 + (qc + 1) * 512],
                            au_st[:, qc * 512:(qc + 1) * 512],
                            pbig[0:64, 0:512], ALU.mult))
                    else:
                        vS.inc(vector.tensor_tensor(
                            od_st[:, qc * 512:(qc + 1) * 512],
                            au_st[:, qc * 512:(qc + 1) * 512],
                            pbig[0:64, 0:512], ALU.mult))
                v_blk(inv_v)

            if ho == 1:
                def od_g(gpsimd, hc=hc):
                    gq.bar(gpsimd)
                    vS.wait(gpsimd)
                    gq.inc(gpsimd.dma_start(
                        out=C[64:128, hc * 2048:(hc + 1) * 2048],
                        in_=od_st[:, :]), 16)
                g_blk(od_g)

        dev_tap("vtm0", Vtm[:, 0:2048])
        dev_tap("attnC", C[:, 0:2048])
        def wr_load(gpsimd):
            gq.bar(gpsimd)
            tS.wait(gpsimd)
            gq.inc(gpsimd.dma_start(
                out=Dd[:, 0:8192].rearrange("p (kk n) -> p kk n", kk=8),
                in_=Wr.rearrange("(kk p) n -> p kk n", p=128)), 16)
        g_blk(wr_load)

        for t in range(16):
            def wr_t(tensor, t=t):
                gq.wait(tensor); gS.wait(tensor); vS.wait(tensor)
                if t > 1:
                    sS.wait(tensor)
                for dh in range(2):
                    for kk in range(8):
                        tS.inc(tensor.matmul(
                            pb[2 * (t % 2) + dh][:, :],
                            C[:, kk * 2048 + t * 128:kk * 2048 + (t + 1) * 128],
                            Dd[:, kk * 1024 + dh * 512:kk * 1024 + (dh + 1) * 512],
                            start=(kk == 0), stop=(kk == 7)))
            t_blk(wr_t)

            def wr_s(scalar, t=t):
                tS.wait(scalar)
                for dh in range(2):
                    sS.inc(scalar.activation(
                        AUXbf[:, t * 1024 + dh * 512:t * 1024 + (dh + 1) * 512],
                        pb[2 * (t % 2) + dh][:, :], AF.Copy))
            s_blk(wr_s)

        def s1_spill(sync):
            qS.bar(sync)
            sS.wait(sync)
            qS.inc(sync.dma_start(
                out=s1_sp.rearrange("(t p) d -> p t d", p=128),
                in_=AUXbf[:, :].rearrange("p (t d) -> p t d", d=D)), 16)
        q_blk(s1_spill)
        dev_tap("s1_c", AUXbf[:, :])
        if BUILD_UPTO < 9:
            return nc, dev_outs

        io512bc = SCR[:, 3072:3584]
        ptile = [SCRbf[:, 0:512], SCRbf[:, 512:1024]]
        slotloc = SCR[:, 3584:3585]
        b1sd = [SCR[:, 3600:3632], SCR[:, 3632:3664]]
        b2r = SCRbf[:, 4096:5120][0:8, :]
        ct_tile = SCRbf[:, 5120:5248]
        srow128 = SCR[0:1, 2816:2944]
        b2tok = tmp4b
        tS_p8, vS_p8, sS_p8, qS_p8 = tS.v, vS.v, sS.v, qS.v
        w1sem = [Sem(nc, ctx, f"w1s{r}") for r in range(8)]
        w2sem = [Sem(nc, ctx, f"w2s{r}") for r in range(8)]
        b1sem = [Sem(nc, ctx, "b1s0"), Sem(nc, ctx, "b1s1")]
        eoq = Sem(nc, ctx, "eoq")
        b2row = SCRbf[0:1, 5120:6144]
        b2bc_sb = SCRbf[:, 6144:7168]

        def p9_ld(gpsimd):
            gq.bar(gpsimd)
            qS.wait(gpsimd); sS.wait(gpsimd); vS.wait(gpsimd)
            gq.inc(gpsimd.dma_start(
                out=Dd[:, :].rearrange("p (t d) -> p t d", d=D),
                in_=xn1tok_sp.rearrange("(t p) d -> p t d", p=128)), 16)
            gq.inc(gpsimd.dma_start(out=b2r, in_=b2), 16)
        g_blk(p9_ld)

        def p9_t(tensor):
            vS.wait(tensor); qS.wait(tensor)
            tS.inc(tensor.matmul(pb[0][:, :], onesr_f[0:1, :],
                                 io512[0:1, :], start=True, stop=True))
        t_blk(p9_t)

        def p9_v(vector):
            tS.wait(vector)
            vS.inc(vector.tensor_copy(io512bc[:, :], pb[0][:, :]))
        v_blk(p9_v)

        w1v = w1.rearrange("e h d f -> (e h d) f")
        w2v = w2.rearrange("e f n -> (e f) n")
        b1v = b1r_d.rearrange("e p c -> (e p) c")
        SCRi = SCR[:, :].bitcast(mybir.dt.int32)
        tblw1 = SCRi[:, 3664:3680]
        tblw2 = SCRi[:, 3680:3696]
        tblb1 = SCRi[:, 3696:3712]
        tbtmp = SCR[:, 3712:3728]

        def tbl_t(tensor):
            vS.wait(tensor); qS.wait(tensor)
            tS.inc(tensor.matmul(pb[0][:, 0:16], onesr_f[0:1, :],
                                 erow_f[:, :], start=True, stop=True))
        t_blk(tbl_t)

        sloti = SCRi[:, 3728:3744]
        onesr_b = SCRbf[0:1, 7488:7616]
        oh8seg = SCRbf[0:8, 7616:7632]

        def tbl_v(vector):
            tS.wait(vector); qS.wait(vector)
            for mult, dst in ((2048.0, tblw1), (4096.0, tblw2),
                              (128.0, tblb1)):
                vS.wait(vector)
                vS.inc(vector.tensor_scalar(tbtmp[:, :], pb[0][:, 0:16],
                                            mult, iost[:, 0:1],
                                            ALU.mult, ALU.add))
                vS.wait(vector)
                vS.inc(vector.tensor_copy(dst, tbtmp[:, :]))
            vS.wait(vector)
            vS.inc(vector.tensor_scalar(oh8seg[:, :], pb[0][0:8, 0:16],
                                        iost[0:8, 0:1], None, ALU.is_equal))
            vS.inc(vector.tensor_copy(sloti[:, :], slot_sb[:, :]))
            vS.inc(vector.memset(onesr_b[:, :], 1.0))
        v_blk(tbl_v)
        vS_tbl = vS.v

        dsp_psum = [pb[0][:, :], pb[1][:, :], pb[2][:, :], pb[3][:, :],
                    pbig[:, 0:512], pbig[:, 512:1024],
                    pbig[:, 1024:1536], pbig[:, 1536:2048]]
        for sc in range(8):
            for tt in range(16):
                def pt_v(vector, sc=sc, tt=tt):
                    vS.wait(vector)
                    if tt > 1 or sc > 0:
                        tS.wait(vector)
                    vS.inc(vector.tensor_scalar(
                        slotloc[:, :], slot_sb[:, tt:tt + 1], 1.0,
                        float(-512 * sc), ALU.mult, ALU.add))
                    vS.wait(vector)
                    vS.inc(vector.tensor_scalar(
                        ptile[tt % 2][:, 0:512], io512bc[:, :],
                        slotloc[:, :], None, ALU.is_equal))
                v_blk(pt_v)

                def pt_t(tensor, sc=sc, tt=tt):
                    vS.wait(tensor); gq.wait(tensor)
                    for dc in range(8):
                        tS.inc(tensor.matmul(
                            dsp_psum[dc],
                            Dd[:, tt * D + dc * 128:tt * D + (dc + 1) * 128],
                            ptile[tt % 2][:, 0:512],
                            start=(tt == 0), stop=(tt == 15)))
                t_blk(pt_t)

            def dsp_v(vector, sc=sc):
                tS.wait(vector)
                for dc in range(8):
                    for half in range(2):
                        sg = 2 * sc + half
                        vS.inc(vector.tensor_copy(
                            Abf[:, sg * 2048 + dc * 256:sg * 2048 + (dc + 1) * 256],
                            dsp_psum[dc][:, half * 256:(half + 1) * 256]))
            v_blk(dsp_v)

        w2_psum = [pb[0][:, :], pb[1][:, :], pb[2][:, :], pb[3][:, :]]

        def hbank(ft):
            pr, hf = ft // 2, ft % 2
            if pr < 4:
                return pbig[:, pr * 512 + hf * 256:pr * 512 + (hf + 1) * 256]
            return pb[pr - 4][:, hf * 256:(hf + 1) * 256]

        w1_free = {r: tS_p8 for r in range(8)}
        w2_free = {r: tS_p8 for r in range(8)}

        def w1slot(r):
            if r < 4:
                return C[:, r * 2048:(r + 1) * 2048]
            return AUXbf[:, 8192 + (r - 4) * 2048:8192 + (r - 3) * 2048]
        b1_free = {0: sS_p8, 1: sS_p8}
        evac_val = {}
        eov_val = None
        for sg in range(16):
            def b1_ld(gpsimd, sg=sg):
                if sg == 0:
                    gpsimd.wait_ge(tS.h, tS_p8)
                    gpsimd.wait_ge(vS.h, vS_tbl)
                    gpsimd.wait_ge(sS.h, sS_p8)
                    gpsimd.wait_ge(qS.h, qS_p8)
                sS.wait(gpsimd, b1_free[sg % 2])
                b1sem[sg % 2].inc(gpsimd.indirect_dma_start(
                    b1sd[sg % 2][:, :], None, b1v,
                    IndirectOffsetOnAxis(ap=tblb1[:, sg:sg + 1],
                                         axis=0)), 16)
            g_blk(b1_ld)
            for fh in range(2):
                for dc in range(8):
                    r = (fh * 8 + dc) % 8

                    def w1_ld(gpsimd, sg=sg, fh=fh, dc=dc, r=r):
                        if sg == 0 and fh == 0 and r >= 4:
                            gpsimd.wait_ge(qS.h, qS_p8)
                        tS.wait(gpsimd, w1_free[r])
                        w1sem[r].inc(gpsimd.indirect_dma_start(
                            w1slot(r), None, w1v,
                            IndirectOffsetOnAxis(ap=tblw1[:, sg:sg + 1],
                                                 axis=0),
                            element_offset=fh * 2097152 + dc * 262144), 16)
                    g_blk(w1_ld)

                    def hd_t(tensor, sg=sg, fh=fh, dc=dc, r=r):
                        w1sem[r].wait(tensor)
                        if dc == 0 and fh == 0 and sg == 0:
                            vS.wait(tensor)
                        for i, ft in enumerate((0, 2, 4, 6, 8, 10, 12, 14,
                                                1, 3, 5, 7, 9, 11, 13, 15)):
                            if dc == 0 and i < 8:
                                bk = ft // 2
                                if i == 0 and fh == 0 and eov_val is not None:
                                    vS.wait(tensor, eov_val)
                                if bk in evac_val:
                                    if bk < 4:
                                        sS.wait(tensor, evac_val[bk])
                                    else:
                                        vS.wait(tensor, evac_val[bk])
                            tS.inc(tensor.matmul(
                                hbank(ft),
                                w1slot(r)[:, ft * 128:(ft + 1) * 128],
                                Abf[:, sg * 2048 + dc * 256:
                                    sg * 2048 + (dc + 1) * 256],
                                start=(dc == 0 and ft % 2 == 0),
                                stop=(dc == 7 and ft % 2 == 1)))
                    t_blk(hd_t)
                    w1_free[r] = tS.v

                def hd_s(scalar, sg=sg, fh=fh):
                    tS.wait(scalar)
                    b1sem[sg % 2].wait(scalar)
                    for ft in range(8):
                        gfc = fh * 16 + ft
                        sS.inc(scalar.activation(
                            AUXbf[:, gfc * 256:(gfc + 1) * 256],
                            hbank(ft), AF.Relu,
                            bias=b1sd[sg % 2][:, gfc:gfc + 1]))
                s_blk(hd_s)
                for bk in range(4):
                    evac_val[bk] = sS.v

                def hd_v(vector, sg=sg, fh=fh):
                    tS.wait(vector)
                    b1sem[sg % 2].wait(vector)
                    for ft in range(8, 16):
                        gfc = fh * 16 + ft
                        vS.inc(vector.tensor_scalar(
                            AUXbf[:, gfc * 256:(gfc + 1) * 256],
                            hbank(ft), b1sd[sg % 2][:, gfc:gfc + 1],
                            0.0, ALU.add, ALU.max))
                v_blk(hd_v)
                for bk in range(4, 8):
                    evac_val[bk] = vS.v
            b1_free[sg % 2] = sS.v

            for fk in range(32):
                rk = fk % 8

                def w2_ld(gpsimd, sg=sg, fk=fk, rk=rk):
                    tS.wait(gpsimd, w2_free[rk])
                    w2sem[rk].inc(gpsimd.indirect_dma_start(
                        C[:, 8192 + rk * 1024:8192 + (rk + 1) * 1024],
                        None, w2v,
                        IndirectOffsetOnAxis(ap=tblw2[:, sg:sg + 1],
                                             axis=0),
                        element_offset=fk * 131072), 16)
                g_blk(w2_ld)

                def w2_t(tensor, sg=sg, fk=fk, rk=rk):
                    w2sem[rk].wait(tensor)
                    if fk == 0:
                        sS.wait(tensor); vS.wait(tensor)
                    for ts2 in range(2):
                        for dh in range(2):
                            tS.inc(tensor.matmul(
                                w2_psum[ts2 * 2 + dh],
                                AUXbf[:, fk * 256 + ts2 * 128:
                                      fk * 256 + (ts2 + 1) * 128],
                                C[:, 8192 + rk * 1024 + dh * 512:
                                  8192 + rk * 1024 + (dh + 1) * 512],
                                start=(fk == 0), stop=(fk == 31)))
                t_blk(w2_t)
                w2_free[rk] = tS.v

            def b2b_t1(tensor, sg=sg):
                for dh in range(2):
                    tS.inc(tensor.matmul(
                        pbig[0:1, dh * 512:(dh + 1) * 512],
                        oh8seg[:, sg:sg + 1],
                        b2r[:, dh * 512:(dh + 1) * 512],
                        start=True, stop=True))
            t_blk(b2b_t1)

            def b2b_v1(vector, sg=sg):
                tS.wait(vector)
                vS.inc(vector.tensor_copy(b2row[:, :], pbig[0:1, 0:1024]))
            v_blk(b2b_v1)

            def b2b_t2(tensor, sg=sg):
                vS.wait(tensor)
                for dh in range(2):
                    tS.inc(tensor.matmul(
                        pbig[:, dh * 512:(dh + 1) * 512], onesr_b[:, :],
                        b2row[0:1, dh * 512:(dh + 1) * 512],
                        start=True, stop=True))
            t_blk(b2b_t2)

            def b2b_v2(vector, sg=sg):
                tS.wait(vector)
                vS.inc(vector.tensor_copy(b2bc_sb[:, :], pbig[:, 0:1024]))
            v_blk(b2b_v2)

            def eo_v(vector, sg=sg):
                vS.wait(vector)
                for ts2 in range(2):
                    for dh in range(2):
                        vS.inc(vector.tensor_tensor(
                            Abf[:, sg * 2048 + ts2 * 1024 + dh * 512:
                                sg * 2048 + ts2 * 1024 + (dh + 1) * 512],
                            w2_psum[ts2 * 2 + dh],
                            b2bc_sb[:, dh * 512:(dh + 1) * 512], ALU.add))
            v_blk(eo_v)
            eov_val = vS.v

            def eo_q(sync, sg=sg):
                vS.wait(sync)
                for ts2 in range(2):
                    eoq.inc(sync.dma_start(
                        out=eout_sp[sg * 256 + ts2 * 128:
                                    sg * 256 + (ts2 + 1) * 128, :],
                        in_=Abf[:, sg * 2048 + ts2 * 1024:
                                sg * 2048 + (ts2 + 1) * 1024]), 16)
            q_blk(eo_q)

        def cg_g(gpsimd):
            eoq.wait(gpsimd)
            tS.wait(gpsimd); vS.wait(gpsimd)
            for t in range(16):
                gq.inc(gpsimd.indirect_dma_start(
                    Dd[:, t * 1024:(t + 1) * 1024], None, eout_sp,
                    IndirectOffsetOnAxis(ap=sloti[:, t:t + 1], axis=0)), 16)
        g_blk(cg_g)

        def cb_s(scalar):
            gq.wait(scalar); vS.wait(scalar)
            for t in range(16):
                sS.inc(scalar.activation(
                    AUXbf[:, t * 1024:(t + 1) * 1024],
                    Dd[:, t * 1024:(t + 1) * 1024],
                    AF.Silu, scale=kg2_sb[:, t:t + 1]))
        s_blk(cb_s)
        dev_tap("gm1", AUXbf[:, :])
        if BUILD_UPTO < 10:
            return nc, dev_outs

        for t in range(16):
            def f_q(sync, t=t):
                qS.bar(sync)
                vS.wait(sync); sS.wait(sync)
                qS.inc(sync.dma_start(out=tmp4[:, :],
                                      in_=x2_sp[t * 128:(t + 1) * 128, :]), 16)
                qS.inc(sync.dma_start(out=wka[:, 0:D],
                                      in_=s1_sp[t * 128:(t + 1) * 128, :]), 16)
            q_blk(f_q)

            def f_v(vector, t=t):
                qS.wait(vector); vS.wait(vector); sS.wait(vector)
                oS.wait(vector)
                vS.inc(vector.tensor_tensor(
                    tmp4b[:, :], wka[:, 0:D],
                    AUXbf[:, t * 1024:(t + 1) * 1024], ALU.mult))
                vS.wait(vector)
                vS.inc(vector.tensor_tensor(SCR[:, 0:D], tmp4[:, :],
                                            tmp4b[:, :], ALU.add))
            v_blk(f_v)

            def f_s(sync, t=t):
                qS.bar(sync)
                vS.wait(sync)
                oS.inc(sync.dma_start(out=out[t * 128:(t + 1) * 128, :],
                                      in_=SCR[:, 0:D]), 16)
                oS.wait(sync)
            q_blk(f_s)

        def fin_q(sync):
            oS.wait(sync)
        q_blk(fin_q)



    return nc, dev_outs


def _prep_inputs(core, **inp):
    x = np.asarray(inp["L0_attn_x"], np.float32)
    i = core
    xb = np.ascontiguousarray(x[i])
    xs = np.ascontiguousarray(
        x[:, 256 * i:256 * (i + 1), :].reshape(2048, D))
    if i == 0:
        xh = np.zeros((24, D), np.float32)
    else:
        xh = np.ascontiguousarray(
            x[:, 256 * i - 3:256 * i, :].reshape(24, D))
    d = {
        "xb": xb, "xs": xs, "xh": xh,
        "ln1_w": np.asarray(inp["L0_attn_ln1_w"], np.float32),
        "ln1_b": np.asarray(inp["L0_attn_ln1_b"], np.float32),
        "convw": np.ascontiguousarray(
            np.asarray(inp["L0_attn_conv_w"], np.float32)[:, 0, :].T),
        "convb": np.asarray(inp["L0_attn_conv_b"], np.float32),
        "lns_w": np.asarray(inp["L0_attn_lns_w"], np.float32),
        "lns_b": np.asarray(inp["L0_attn_lns_b"], np.float32),
        "in_w": np.asarray(inp["L0_attn_in_w"], np.float32),
        "in_b": np.ascontiguousarray(np.asarray(inp["L0_attn_in_b"], np.float32).reshape(16, 128).T),
        "xp_w": np.asarray(inp["L0_attn_xp_w"], np.float32),
        "xp_b": np.asarray(inp["L0_attn_xp_b"], np.float32).reshape(16, 1),
        "A_log": np.asarray(inp["L0_attn_A_log"], np.float32).reshape(16, 1),
        "Dmat": np.asarray(inp["L0_attn_D"], np.float32),
        "out_w": np.asarray(inp["L0_attn_out_w"], np.float32),
        "out_b": np.asarray(inp["L0_attn_out_b"], np.float32),
        "ln2_w": np.asarray(inp["L0_attn_ln2_w"], np.float32),
        "ln2_b": np.asarray(inp["L0_attn_ln2_b"], np.float32),
        "mlp_w1": np.asarray(inp["L0_attn_mlp_w1"], np.float32),
        "mlp_b1": np.ascontiguousarray(np.asarray(inp["L0_attn_mlp_b1"], np.float32).reshape(16, 128).T),
        "mlp_w2": np.asarray(inp["L0_attn_mlp_w2"], np.float32),
        "mlp_b2": np.asarray(inp["L0_attn_mlp_b2"], np.float32),
        "Wg": np.asarray(inp["L0_ffn_Wg"], np.float32),
        "We": np.asarray(inp["L0_ffn_We"], np.float32),
        "be": np.asarray(inp["L0_ffn_be"], np.float32),
        "Wq": np.asarray(inp["L1_attn_Wq"], np.float32),
        "Wk": np.asarray(inp["L1_attn_Wk"], np.float32),
        "Wv": np.asarray(inp["L1_attn_Wv"], np.float32),
        "Wr": np.asarray(inp["L1_attn_Wr"], np.float32),
        "gate_wT": np.ascontiguousarray(
            np.asarray(inp["L1_ffn_gate_w"], np.float32).T),
        "gate_b": np.asarray(inp["L1_ffn_gate_b"], np.float32).reshape(1, 8),
        "w1": np.ascontiguousarray(
            np.asarray(inp["L1_ffn_w1"], np.float32)
            .reshape(8, D, 2, FF // 2).transpose(0, 2, 1, 3)),
        "b1r": np.ascontiguousarray(np.asarray(inp["L1_ffn_b1"], np.float32).reshape(8, 32, 128).transpose(0, 2, 1)),
        "w2": np.asarray(inp["L1_ffn_w2"], np.float32),
        "b2": np.asarray(inp["L1_ffn_b2"], np.float32),
        "identf": np.eye(128, dtype=np.float32),
        "identb": np.eye(128, dtype=np.float32).astype(ml_dtypes.bfloat16),
        "tri_bf": np.triu(np.ones((128, 128), np.float32)).astype(
            ml_dtypes.bfloat16),
        "cmask": (np.arange(8) < i).astype(np.float32).reshape(8, 1),
        "cexp": np.maximum(i - 1 - np.arange(8), 0).astype(
            np.float32).reshape(8, 1),
        "hmask": np.full((24, 1), 0.0 if i == 0 else 1.0, np.float32),
        "causm": (-400.0 * (np.arange(128)[:, None] > np.arange(128)[None, :])
                  ).astype(ml_dtypes.bfloat16),
        "iota8": np.arange(8, dtype=np.float32).reshape(1, 8),
        "iota512": np.arange(512, dtype=np.float32).reshape(1, 512),
        "iota_st": (np.arange(32)[None, :] * 128
                    + np.arange(128)[:, None]).astype(np.float32),
        "seg256": (np.arange(16, dtype=np.float32) * 256).reshape(16, 1),
    }
    return d


def kernel(**inputs):
    nc, dev_outs = build()
    in_maps = [_prep_inputs(i, **inputs) for i in range(8)]
    res = run_bass_kernel_spmd(nc, in_maps, core_ids=list(range(8)))
    outp = np.stack([np.asarray(res.results[i]["out"], np.float32)
                     for i in range(8)], axis=0)
    return outp.reshape(8, 2048, 1024)


if __name__ == "__main__":
    print("kernel module ok")


# version 08
# speedup vs baseline: 1.0074x; 1.0074x over previous
import numpy as np
import ml_dtypes
from contextlib import ExitStack

import concourse.bass as bass
import concourse.mybir as mybir
from concourse.bass_utils import run_bass_kernel_spmd
from concourse.bass_core_rust import IndirectOffsetOnAxis

F32 = mybir.dt.float32
BF16 = mybir.dt.bfloat16
ALU = mybir.AluOpType
AF = mybir.ActivationFunctionType
AX = mybir.AxisListType

B, S, D = 8, 2048, 1024
TOK = 2048
NT = 16
DC = 8
EPS = 1e-5
CAP0 = 2048.0
CAP1 = 2560.0
M1 = 384
NSLOT = 8 * M1
FF = 4096

DEV_TAPS = []
BUILD_UPTO = 99


class Sem:
    def __init__(self, nc, ctx, name):
        self.h = ctx.enter_context(nc.semaphore(name))
        self.v = 0
        self.mw = 0

    def inc(self, inst, n=1):
        inst.then_inc(self.h, n)
        self.v += n
        return self.v

    def wait(self, eng, v=None):
        w = self.v if v is None else v
        self.mw = max(self.mw, w)
        eng.wait_ge(self.h, w)

    def bar(self, eng):
        if self.mw > 0:
            eng.wait_ge(self.h, self.mw)


def build():
    nc = bass.Bass("TRN2", target_bir_lowering=False, debug=False)

    def din(name, shape, dt=F32):
        return nc.dram_tensor(name, shape, dt, kind="ExternalInput").ap()

    xb = din("xb", [TOK, D])
    xs = din("xs", [TOK, D])
    xh = din("xh", [24, D])
    ln1_w = din("ln1_w", [D]); ln1_b = din("ln1_b", [D])
    convw = din("convw", [4, D]); convb = din("convb", [D])
    lns_w = din("lns_w", [D]); lns_b = din("lns_b", [D])
    in_w = din("in_w", [D, 2 * D]); in_b = din("in_b", [128, 16])
    xp_w = din("xp_w", [D, 16]); xp_b = din("xp_b", [16, 1])
    A_log = din("A_log", [16, 1])
    Dmat = din("Dmat", [16, D])
    out_w = din("out_w", [D, D]); out_b = din("out_b", [D])
    ln2_w = din("ln2_w", [D]); ln2_b = din("ln2_b", [D])
    mlp_w1 = din("mlp_w1", [D, 2 * D]); mlp_b1 = din("mlp_b1", [128, 16])
    mlp_w2 = din("mlp_w2", [2 * D, D]); mlp_b2 = din("mlp_b2", [D])
    Wg = din("Wg", [D, 8])
    We = din("We", [D, D]); be = din("be", [D])
    Wq = din("Wq", [D, D]); Wk = din("Wk", [D, D])
    Wv = din("Wv", [D, D]); Wr = din("Wr", [D, D])
    gate_wT = din("gate_wT", [D, 8]); gate_b = din("gate_b", [1, 8])
    w1 = din("w1", [8, 2, D, FF // 2]); b1r_d = din("b1r", [8, 128, 32])
    w2 = din("w2", [8, FF, D]); b2 = din("b2", [8, D])
    identf = din("identf", [128, 128])
    identb = din("identb", [128, 128], BF16)
    tri_bf = din("tri_bf", [128, 128], BF16)
    cmask = din("cmask", [8, 1])
    cexp = din("cexp", [8, 1])
    hmask = din("hmask", [24, 1])
    causm = din("causm", [128, 128], BF16)
    iota8 = din("iota8", [1, 8])
    iota512 = din("iota512", [1, 512])
    iota_st = din("iota_st", [128, 32])
    seg256 = din("seg256", [16, 1])

    out = nc.dram_tensor("out", [TOK, D], F32, kind="ExternalOutput").ap()

    y_sp = nc.dram_tensor("y_sp", [TOK, D], BF16, kind="Internal").ap()
    eout_sp = nc.dram_tensor("eout_sp", [4096, D], BF16, kind="Internal").ap()
    x2_sp = nc.dram_tensor("x2_sp", [TOK, D], F32, kind="Internal").ap()
    s1_sp = nc.dram_tensor("s1_sp", [TOK, D], BF16, kind="Internal").ap()
    ag1_in = nc.dram_tensor("ag1_in", [1, 128], F32, kind="Internal").ap()
    ag1_out = nc.dram_tensor("ag1_out", [8, 128], F32, kind="Internal").ap()
    ag2_in = nc.dram_tensor("ag2_in", [1, 8], F32, kind="Internal").ap()
    ag3_in = nc.dram_tensor("ag3_in", [1, 8], F32, kind="Internal").ap()
    ag2_out = nc.dram_tensor("ag2_out", [8, 8], F32, kind="Internal").ap()
    ag3_out = nc.dram_tensor("ag3_out", [8, 8], F32, kind="Internal").ap()

    dev_outs = {}
    for name, shape, dt in DEV_TAPS:
        dev_outs[name] = nc.dram_tensor("dev_" + name, shape, dt,
                                        kind="ExternalOutput").ap()

    ctx = ExitStack()
    with ctx:
        block = ctx.enter_context(nc.Block())

        def sb(name, shape, dt=F32):
            return ctx.enter_context(nc.sbuf_tensor(name, shape, dt))

        def ps(name, shape):
            return ctx.enter_context(nc.psum_tensor(name, shape, F32))

        tS = Sem(nc, ctx, "tS"); vS = Sem(nc, ctx, "vS")
        sS = Sem(nc, ctx, "sS"); gS = Sem(nc, ctx, "gS")
        qS = Sem(nc, ctx, "qS"); oS = Sem(nc, ctx, "oS")
        cS = Sem(nc, ctx, "cS"); gq = Sem(nc, ctx, "gq")

        A = sb("A", [128, 16 * D], F32)
        C = sb("C", [128, 16 * D], BF16)
        Dd = sb("Dd", [128, 16 * D], BF16)
        SCR = sb("SCR", [128, 4 * D], F32)
        tmp4 = sb("tmp4", [128, D], F32)
        tmp4b = sb("tmp4b", [128, D], F32)
        Abf = A[:, :].bitcast(BF16)
        Dbf32 = Dd[:, :].bitcast(F32)

        idf = sb("idf", [128, 128])
        idb = sb("idb", [128, 128], BF16)
        tri = sb("tri", [128, 128], BF16)
        cm8 = sb("cm8", [8, 1]); ce8 = sb("ce8", [8, 1])
        hm24 = sb("hm24", [24, 1])
        caus = sb("caus", [128, 128], BF16)
        io8 = sb("io8", [1, 8])
        io512 = sb("io512", [1, 512])
        iost = sb("iost", [128, 32])
        s256 = sb("s256", [16, 1])
        ones_f = sb("ones_f", [128, 1])
        ones_b = sb("ones_b", [128, 1], BF16)
        onesr_f = sb("onesr_f", [1, 128])
        lnv = sb("lnv", [1, D])
        xpw8 = sb("xpw8", [128, 8 * 16], BF16)
        inb_sb = sb("inb_sb", [128, 16])
        m1b_sb = sb("m1b_sb", [128, 16])
        xpb_sb = sb("xpb_sb", [16, 1])
        alog = sb("alog", [16, 1])
        svec = sb("svec", [16, 1])
        dvec = sb("dvec", [16, 1])
        stat_m = sb("stat_m", [128, 16])
        stat_i = sb("stat_i", [128, 16])
        AUX = sb("AUX", [128, 8192], F32)
        AUXbf = AUX[:, :].bitcast(BF16)
        u_fm = AUX[0:16, 0:2048]
        hs_fm = AUX[0:16, 2048:4096]
        dect = AUX[0:16, 4096:6144]
        dmat = AUX[0:16, 6144:7168]
        halo_x = AUX[0:24, 7168:8192]
        e16 = sb("e16", [16, 8])
        eag = sb("eag", [8, 136])
        wjn = sb("wjn", [8, 16])
        prod8 = sb("prod8", [8, 128])
        c128 = sb("c128", [128, 1])
        ccorr = sb("ccorr", [16, 8])
        s8x16 = sb("s8x16", [8, 16])
        srow = sb("srow", [1, 16])

        halo_n1 = sb("halo_n1", [24, D], BF16)
        hst = sb("hst", [24, 2])
        wka = sb("wka", [128, 2048], BF16)
        wkb = sb("wkb", [128, 2048], BF16)

        pb = [ps(f"pb{i}", [128, 512]) for i in range(4)]
        pbig = ps("pbig", [128, 2048])

        lnw_bc = SCR[:, 0:D]
        lnb_bc = SCR[:, D:2 * D]
        bias_bc = SCR[:, 2 * D:3 * D]
        w_bc = SCR[:, 0:4 * D]

        def t_blk(fn):
            block.tensor(fn)

        def v_blk(fn):
            block.vector(fn)

        def s_blk(fn):
            block.scalar(fn)

        def g_blk(fn):
            block.gpsimd(fn)

        def q_blk(fn):
            block.sync(fn)

        def load_row(src_dram_row, width=D):
            def qf(sync):
                qS.bar(sync)
                tS.wait(sync)
                qS.inc(sync.dma_start(out=lnv[0:1, 0:width],
                                      in_=src_dram_row), 16)
            q_blk(qf)

        def bcast_lnv(dst, width=D):
            nchunk = (width + 511) // 512

            def tf(tensor):
                vS.wait(tensor); qS.wait(tensor); gS.wait(tensor)
                for ci in range(nchunk):
                    w = min(512, width - ci * 512)
                    tS.inc(tensor.matmul(pb[ci % 4][:, 0:w], onesr_f[0:1, :],
                                         lnv[0:1, ci * 512:ci * 512 + w],
                                         start=True, stop=True))
            t_blk(tf)

            def vf(vector):
                tS.wait(vector)
                for ci in range(nchunk):
                    w = min(512, width - ci * 512)
                    vS.inc(vector.tensor_copy(dst[:, ci * 512:ci * 512 + w],
                                              pb[ci % 4][:, 0:w]))
            v_blk(vf)

        def load_bcast(src_dram_row, dst, width=D):
            load_row(src_dram_row, width)
            bcast_lnv(dst, width)

        def layernorm(src, dst, tiles, scratch, wb=None, mcol=None, icol=None,
                      halo_mask=None, width=D):
            def v1(vector):
                qS.wait(vector); gS.wait(vector); sS.wait(vector)
                gq.wait(vector)
                for t in tiles:
                    vS.inc(vector.reduce_sum(mcol[:, t:t + 1],
                                             src[:, t * width:(t + 1) * width],
                                             axis=AX.X))
                vS.wait(vector)
                for t in tiles:
                    vS.inc(vector.tensor_scalar(mcol[:, t:t + 1],
                                                mcol[:, t:t + 1],
                                                1.0 / width, None, ALU.mult))
            v_blk(v1)

            npart = src.shape[0]

            def s1(scalar):
                vS.wait(scalar)
                for n, t in enumerate(tiles):
                    sS.wait(scalar)
                    sS.inc(scalar.activation(
                        scratch[0:npart, (n % 2) * width:(n % 2 + 1) * width],
                        src[:, t * width:(t + 1) * width],
                        AF.Square, bias=mcol[:, t:t + 1], scale=-1.0,
                        accum_out=icol[:, t:t + 1]))
            s_blk(s1)

            def v2(vector):
                sS.wait(vector)
                for t in tiles:
                    vS.inc(vector.tensor_scalar(icol[:, t:t + 1],
                                                icol[:, t:t + 1],
                                                1.0 / width, EPS,
                                                ALU.mult, ALU.add))
                vS.wait(vector)
                for t in tiles:
                    vS.inc(vector.reciprocal(icol[:, t:t + 1], icol[:, t:t + 1]))
            v_blk(v2)

            def s2(scalar):
                vS.wait(scalar)
                for t in tiles:
                    sS.inc(scalar.activation(icol[:, t:t + 1], icol[:, t:t + 1],
                                             AF.Sqrt))
            s_blk(s2)

            if wb is None:
                def v3(vector):
                    sS.wait(vector)
                    for t in tiles:
                        vS.inc(vector.tensor_scalar(
                            dst[:, t * width:(t + 1) * width],
                            src[:, t * width:(t + 1) * width],
                            mcol[:, t:t + 1], icol[:, t:t + 1],
                            ALU.subtract, ALU.mult))
                v_blk(v3)
            else:
                wt, bt = wb

                def v3(vector):
                    sS.wait(vector)
                    for t in tiles:
                        vS.inc(vector.scalar_tensor_tensor(
                            dst[:, t * width:(t + 1) * width],
                            src[:, t * width:(t + 1) * width],
                            mcol[:, t:t + 1], wt[0:npart, 0:width],
                            ALU.subtract, ALU.mult))
                    vS.wait(vector)
                    for t in tiles:
                        vS.inc(vector.scalar_tensor_tensor(
                            dst[:, t * width:(t + 1) * width],
                            dst[:, t * width:(t + 1) * width],
                            icol[:, t:t + 1], bt[0:npart, 0:width],
                            ALU.mult, ALU.add))
                v_blk(v3)
            if halo_mask is not None:
                def v4(vector):
                    vS.wait(vector)
                    for t in tiles:
                        vS.inc(vector.tensor_scalar(
                            dst[:, t * width:(t + 1) * width],
                            dst[:, t * width:(t + 1) * width],
                            halo_mask, None, ALU.mult))
                v_blk(v4)

        pbig_bf = pbig[:, :].bitcast(BF16)

        def transpose_fm(src, dst, src_is_bf=False, ntile=16):
            ntok = ntile * 128
            ident = idb if src_is_bf else idf
            pout = pbig_bf if src_is_bf else pbig
            pairs = [(t, dc) for t in range(ntile) for dc in range(DC)]
            for gs in range(0, len(pairs), 16):
                grp = pairs[gs:gs + 16]

                def tf(tensor, grp=grp):
                    vS.wait(tensor); sS.wait(tensor)
                    for n, (t, dc) in enumerate(grp):
                        tS.inc(tensor.transpose(
                            pout[:, n * 128:(n + 1) * 128],
                            src[:, t * D + dc * 128:t * D + (dc + 1) * 128],
                            ident[:, :]))
                t_blk(tf)

                def ef(vector, grp=grp):
                    tS.wait(vector)
                    for n, (t, dc) in enumerate(grp):
                        vS.inc(vector.tensor_copy(
                            dst[:, dc * ntok + t * 128:dc * ntok + (t + 1) * 128],
                            pout[:, n * 128:(n + 1) * 128]))
                v_blk(ef)

        def dev_tap(name, src_ap):
            if name not in dev_outs:
                return

            def qf(sync):
                qS.bar(sync)
                vS.wait(sync); sS.wait(sync); tS.wait(sync); gS.wait(sync)
                gq.wait(sync)
                oS.inc(sync.dma_start(out=dev_outs[name], in_=src_ap), 16)
                oS.wait(sync)
            q_blk(qf)

            def bar(eng):
                oS.wait(eng)
            for blkfn in (t_blk, v_blk, s_blk, g_blk):
                blkfn(bar)

        def p0_sync(sync):
            qS.bar(sync)
            qS.inc(sync.dma_start(out=idf[:, :], in_=identf), 16)
            qS.inc(sync.dma_start(out=idb[:, :], in_=identb), 16)
            qS.inc(sync.dma_start(out=tri[:, :], in_=tri_bf), 16)
            qS.inc(sync.dma_start(out=cm8[:, :], in_=cmask), 16)
            qS.inc(sync.dma_start(out=ce8[:, :], in_=cexp), 16)
            qS.inc(sync.dma_start(out=hm24[:, :], in_=hmask), 16)
            qS.inc(sync.dma_start(out=caus[:, :], in_=causm), 16)
            qS.inc(sync.dma_start(out=io8[:, :], in_=iota8), 16)
            qS.inc(sync.dma_start(out=io512[:, :], in_=iota512), 16)
            qS.inc(sync.dma_start(out=iost[:, :], in_=iota_st), 16)
            qS.inc(sync.dma_start(out=s256[:, :], in_=seg256), 16)
            qS.inc(sync.dma_start(out=inb_sb[:, :], in_=in_b), 16)
            qS.inc(sync.dma_start(out=m1b_sb[:, :], in_=mlp_b1), 16)
            qS.inc(sync.dma_start(out=xpb_sb[:, :], in_=xp_b), 16)
            qS.inc(sync.dma_start(out=alog[:, :], in_=A_log), 16)
            qS.inc(sync.dma_start(out=dmat[:, :], in_=Dmat), 16)
            for bq in range(4):
                qS.inc(sync.dma_start(
                    out=A[:, bq * 4 * D:(bq + 1) * 4 * D].rearrange(
                        "p (t d) -> p t d", d=D),
                    in_=xs.rearrange("(t p) d -> p t d", p=128)[:, bq * 4:(bq + 1) * 4, :]), 16)
            qS.inc(sync.dma_start(out=halo_x[:, :], in_=xh), 16)
        q_blk(p0_sync)

        def p0_gp(gpsimd):
            gq.bar(gpsimd)
            gq.inc(gpsimd.dma_start(
                out=xpw8[:, :].rearrange("p (k n) -> p k n", k=8),
                in_=xp_w.rearrange("(k p) n -> p k n", p=128)), 16)
        g_blk(p0_gp)

        def p0_v(vector):
            vector.memset(ones_f[:, :], 1.0)
            vector.memset(ones_b[:, :], 1.0)
            vS.inc(vector.memset(onesr_f[:, :], 1.0))
        v_blk(p0_v)

        def p0_s(scalar):
            qS.wait(scalar)
            sS.inc(scalar.activation(svec[:, :], alog[:, :], AF.Exp))
            sS.wait(scalar)
            sS.inc(scalar.activation(dvec[:, :], svec[:, :], AF.Exp, scale=-1.0))
        s_blk(p0_s)

        def p0_v2(vector):
            sS.wait(vector)
            vS.inc(vector.memset(dect[:, :], 1.0))
            vS.wait(vector)
            vS.inc(vector.tensor_scalar(dect[:, :], dect[:, :], dvec[:, :],
                                        None, ALU.mult))
            vS.wait(vector)
            vS.inc(vector.memset(
                dect[:, :].rearrange("p (b t) -> p b t", b=8)[:, :, 0:1], 0.0))
        v_blk(p0_v2)

        def p0_t(tensor):
            sS.wait(tensor); vS.wait(tensor); qS.wait(tensor)
            tS.inc(tensor.transpose(pb[0][0:1, 0:16], svec[:, :],
                                    idf[0:16, 0:16]))
        t_blk(p0_t)

        def p0_v3(vector):
            tS.wait(vector)
            vS.inc(vector.tensor_copy(srow[0:1, :], pb[0][0:1, 0:16]))
        v_blk(p0_v3)

        layernorm(A, A, list(range(16)), Dd, wb=None,
                  mcol=stat_m, icol=stat_i)
        layernorm(halo_x, halo_x, [0], halo_n1, wb=None,
                  mcol=hst[:, 0:1], icol=hst[:, 1:2])
        dev_tap("xn_p", A[:, :])

        load_bcast(ln1_w[None, :], lnw_bc)
        load_bcast(ln1_b[None, :], lnb_bc)
        layernorm(A, C, list(range(16)), Dd, wb=(lnw_bc, lnb_bc),
                  mcol=stat_m, icol=stat_i)
        layernorm(halo_x, halo_n1, [0], Dd, wb=(lnw_bc, lnb_bc),
                  mcol=hst[:, 0:1], icol=hst[:, 1:2], halo_mask=hm24[:, :])
        dev_tap("xn1_p", C[:, :])

        for kk in range(4):
            load_row(convw[kk:kk + 1, :])
            bcast_lnv(w_bc[:, kk * D:(kk + 1) * D])

        def conv_round(get_src, kk):
            pre_v = vS.v
            pre_g = gS.v
            pre_gq = gq.v

            def vf(vector):
                gS.wait(vector, pre_g); vS.wait(vector, pre_v)
                gq.wait(vector, pre_gq)
                for t in range(10):
                    vS.inc(vector.tensor_tensor(
                        tmp4[:, :], get_src(t), w_bc[:, kk * D:(kk + 1) * D],
                        ALU.mult))
                    vS.wait(vector)
                    vS.inc(vector.tensor_tensor(
                        A[:, t * D:(t + 1) * D], A[:, t * D:(t + 1) * D],
                        tmp4[:, :], ALU.add))
                    vS.wait(vector)
            v_blk(vf)

            def gf(gpsimd):
                gS.wait(gpsimd, pre_g); vS.wait(gpsimd, pre_v)
                gq.wait(gpsimd, pre_gq)
                for t in range(10, 16):
                    gS.inc(gpsimd.tensor_tensor(
                        tmp4b[:, :], get_src(t), w_bc[:, kk * D:(kk + 1) * D],
                        ALU.mult))
                    gS.wait(gpsimd)
                    gS.inc(gpsimd.tensor_tensor(
                        A[:, t * D:(t + 1) * D], A[:, t * D:(t + 1) * D],
                        tmp4b[:, :], ALU.add))
                    gS.wait(gpsimd)
            g_blk(gf)

        load_row(convb[None, :])

        def cb_t(tensor):
            qS.wait(tensor); vS.wait(tensor)
            tS.inc(tensor.matmul(pbig[:, 0:512], onesr_f[0:1, :],
                                 lnv[0:1, 0:512], start=True, stop=True))
            tS.inc(tensor.matmul(pbig[:, 512:1024], onesr_f[0:1, :],
                                 lnv[0:1, 512:1024], start=True, stop=True))
        t_blk(cb_t)

        def cb_v(vector):
            tS.wait(vector)
            for t in range(16):
                vS.inc(vector.tensor_tensor(
                    A[:, t * D:(t + 1) * D], A[:, t * D:(t + 1) * D],
                    pbig[:, 0:1024], ALU.add))
        v_blk(cb_v)

        conv_round(lambda t: C[:, t * D:(t + 1) * D], 3)
        for shv in (1, 2, 3):
            kk = 3 - shv

            def conv_g(gpsimd, shv=shv):
                gq.bar(gpsimd)
                vS.wait(gpsimd); gS.wait(gpsimd); gq.wait(gpsimd)
                gq.inc(gpsimd.dma_start(
                    out=Dd[shv:128, :].rearrange("p (t d) -> p t d", d=D),
                    in_=C[0:128 - shv, :].rearrange("p (t d) -> p t d", d=D)), 16)
                gq.inc(gpsimd.dma_start(
                    out=Dd[0:shv, :].rearrange("p (b c d) -> p b c d",
                                               b=8, c=2)[:, :, 1, :],
                    in_=C[128 - shv:128, :].rearrange("p (b c d) -> p b c d",
                                                      b=8, c=2)[:, :, 0, :]), 16)
                for bi in range(8):
                    gq.inc(gpsimd.dma_start(
                        out=Dd[0:shv, bi * 2 * D:bi * 2 * D + D],
                        in_=halo_n1[3 * bi + 3 - shv:3 * bi + 3, :]), 16)
            g_blk(conv_g)
            conv_round(lambda t: Dd[:, t * D:(t + 1) * D], kk)

        dev_tap("xloc_p", A[:, :])
        if BUILD_UPTO < 2:
            return nc, dev_outs

        load_bcast(lns_w[None, :], lnw_bc)
        load_bcast(lns_b[None, :], lnb_bc)
        layernorm(A, C, list(range(16)), Dd, wb=(lnw_bc, lnb_bc),
                  mcol=stat_m, icol=stat_i)
        dev_tap("xn2_p", C[:, :])
        transpose_fm(C, Dd, src_is_bf=True)
        sxp_fm = Abf[:, 0:8 * TOK]
        z_fm = Abf[:, 8 * TOK:16 * TOK]

        def inw_load(gpsimd):
            gq.bar(gpsimd)
            vS.wait(gpsimd)
            gq.inc(gpsimd.dma_start(
                out=C[:, :].rearrange("p (kk n) -> p kk n", kk=8),
                in_=in_w.rearrange("(kk p) n -> p kk n", p=128)), 16)
        g_blk(inw_load)

        for fc in range(16):
            def inw_t(tensor, fc=fc):
                gq.wait(tensor); vS.wait(tensor)
                if fc > 0:
                    sS.wait(tensor)
                for tq in range(4):
                    for kk in range(8):
                        tS.inc(tensor.matmul(
                            pb[tq][:, :],
                            C[:, kk * 2048 + fc * 128:kk * 2048 + (fc + 1) * 128],
                            Dd[:, kk * 2048 + tq * 512:kk * 2048 + (tq + 1) * 512],
                            start=(kk == 0), stop=(kk == 7)))
            t_blk(inw_t)

            def inw_s(scalar, fc=fc):
                tS.wait(scalar)
                func = AF.Silu if fc < 8 else AF.Sigmoid
                dst = sxp_fm if fc < 8 else z_fm
                dc = fc % 8
                for tq in range(4):
                    sS.inc(scalar.activation(
                        dst[:, dc * 2048 + tq * 512:dc * 2048 + (tq + 1) * 512],
                        pb[tq][:, :], func, bias=inb_sb[:, fc:fc + 1]))
            s_blk(inw_s)

        dev_tap("sxp_fm", sxp_fm[:, 0:2048])
        dev_tap("z_fm", z_fm[:, 0:2048])

        def u_t(tensor):
            sS.wait(tensor)
            for tq in range(4):
                for kk in range(8):
                    tS.inc(tensor.matmul(
                        pbig[0:16, tq * 512:(tq + 1) * 512],
                        xpw8[:, kk * 16:(kk + 1) * 16],
                        sxp_fm[:, kk * 2048 + tq * 512:kk * 2048 + (tq + 1) * 512],
                        start=(kk == 0), stop=(kk == 7)))
        t_blk(u_t)

        def u_v(vector):
            tS.wait(vector); qS.wait(vector)
            vS.inc(vector.tensor_scalar(u_fm[:, :], pbig[0:16, :],
                                        xpb_sb[:, :], None, ALU.add))
        v_blk(u_v)
        dev_tap("u_fm", u_fm[:, :])

        def scan1(vector):
            vS.wait(vector)
            vS.inc(vector.tensor_tensor_scan(hs_fm[:, :], dect[:, :],
                                             u_fm[:, :], 0.0,
                                             ALU.mult, ALU.add))
            vS.wait(vector)
            vS.inc(vector.tensor_copy(
                e16[:, :],
                hs_fm[:, :].rearrange("p (b t) -> p b t", b=8)[:, :, 255]))
        v_blk(scan1)
        dev_tap("hs1", hs_fm[:, :])
        if BUILD_UPTO < 3:
            return nc, dev_outs

        def ag1_q(sync):
            qS.bar(sync)
            vS.wait(sync)
            dst = ag1_in[0:1, 0:128].rearrange("o (b n) -> (o n) b", b=8)
            qS.inc(sync.dma_start(out=dst, in_=e16[:, :],
                                  allow_slow_non_contiguous=True), 16)
        q_blk(ag1_q)

        def ag1_g(gpsimd):
            qS.wait(gpsimd)
            cS.inc(gpsimd.collective_compute(
                "AllGather", ALU.bypass, replica_groups=[list(range(8))],
                ins=[ag1_in], outs=[ag1_out]))
        g_blk(ag1_g)

        def ag1_back(sync):
            qS.bar(sync)
            cS.wait(sync)
            qS.inc(sync.dma_start(out=eag[:, 0:128], in_=ag1_out[:, 0:128]), 16)
        q_blk(ag1_back)

        def carry_t(tensor):
            vS.wait(tensor)
            tS.inc(tensor.matmul(pb[0][0:8, 0:16], onesr_f[0:1, 0:8],
                                 srow[0:1, 0:16], start=True, stop=True))
        t_blk(carry_t)

        def carry_v(vector):
            tS.wait(vector); qS.wait(vector)
            vS.inc(vector.tensor_scalar(s8x16[:, :], pb[0][0:8, 0:16],
                                        ce8[:, :], -256.0, ALU.mult, ALU.mult))
        v_blk(carry_v)

        def carry_s(scalar):
            vS.wait(scalar)
            sS.inc(scalar.activation(wjn[:, :], s8x16[:, :], AF.Exp))
        s_blk(carry_s)

        def carry_v2(vector):
            sS.wait(vector)
            vS.inc(vector.tensor_scalar(wjn[:, :], wjn[:, :], cm8[:, :],
                                        None, ALU.mult))
            vS.wait(vector)
            vS.inc(vector.tensor_tensor(
                prod8[:, :].rearrange("p (b n) -> p b n", b=8),
                eag[:, 0:128].rearrange("p (b n) -> p b n", b=8),
                wjn[:, :].unsqueeze(1).broadcast_to([8, 8, 16]),
                ALU.mult))
        v_blk(carry_v2)

        def carry_t2(tensor):
            vS.wait(tensor)
            tS.inc(tensor.matmul(pb[1][:, 0:1], prod8[:, :], ones_f[0:8, 0:1],
                                 start=True, stop=True))
        t_blk(carry_t2)

        def carry_v3(vector):
            tS.wait(vector)
            vS.inc(vector.tensor_copy(c128[:, :], pb[1][:, 0:1]))
        v_blk(carry_v3)

        def carry_g(gpsimd):
            gq.bar(gpsimd)
            vS.wait(gpsimd)
            for bi in range(8):
                gq.inc(gpsimd.dma_start(out=ccorr[:, bi:bi + 1],
                                        in_=c128[bi * 16:(bi + 1) * 16, :]), 16)
        g_blk(carry_g)

        def carry_v4(vector):
            gq.wait(vector)
            vS.inc(vector.tensor_scalar(ccorr[:, :], ccorr[:, :], dvec[:, :],
                                        None, ALU.mult))
            vS.wait(vector)
            vS.inc(vector.tensor_tensor(
                u_fm[:, :].rearrange("p (b t) -> p b t", b=8)[:, :, 0],
                u_fm[:, :].rearrange("p (b t) -> p b t", b=8)[:, :, 0],
                ccorr[:, :], ALU.add))
            vS.wait(vector)
            vS.inc(vector.tensor_tensor_scan(hs_fm[:, :], dect[:, :],
                                             u_fm[:, :], 0.0,
                                             ALU.mult, ALU.add))
        v_blk(carry_v4)
        dev_tap("hs2", hs_fm[:, :])
        if BUILD_UPTO < 4:
            return nc, dev_outs

        g1_fm = Dd
        for dc in range(DC):
            def dmm_t(tensor, dc=dc):
                vS.wait(tensor); qS.wait(tensor); sS.wait(tensor)
                for tq in range(4):
                    tS.inc(tensor.matmul(
                        pb[tq][:, :], dmat[:, dc * 128:(dc + 1) * 128],
                        hs_fm[:, tq * 512:(tq + 1) * 512],
                        start=True, stop=True))
            t_blk(dmm_t)

            def dmm_v(vector, dc=dc):
                tS.wait(vector)
                for tq in range(4):
                    vS.inc(vector.tensor_tensor(
                        g1_fm[:, dc * 2048 + tq * 512:dc * 2048 + (tq + 1) * 512],
                        pb[tq][:, :],
                        z_fm[:, dc * 2048 + tq * 512:dc * 2048 + (tq + 1) * 512],
                        ALU.mult))
            v_blk(dmm_v)
        dev_tap("g1_fm", g1_fm[:, 0:2048])

        load_row(out_b[None, :])
        bcast_lnv(bias_bc)
        def outw_g(gpsimd):
            gq.bar(gpsimd)
            vS.wait(gpsimd)
            gq.inc(gpsimd.dma_start(
                out=C[:, 0:8 * D].rearrange("p (kk n) -> p kk n", kk=8),
                in_=out_w.rearrange("(kk p) n -> p kk n", p=128)), 16)
        g_blk(outw_g)

        y_bf = Abf[:, 8 * TOK:16 * TOK]
        for t in range(16):
            def y_t(tensor, t=t):
                gq.wait(tensor); vS.wait(tensor)
                if t > 0:
                    vS.wait(tensor)
                for dh in range(2):
                    for kk in range(8):
                        tS.inc(tensor.matmul(
                            pb[2 * (t % 2) + dh][:, :],
                            g1_fm[:, kk * 2048 + t * 128:kk * 2048 + (t + 1) * 128],
                            C[:, kk * D + dh * 512:kk * D + (dh + 1) * 512],
                            start=(kk == 0), stop=(kk == 7)))
            t_blk(y_t)

            def y_v(vector, t=t):
                tS.wait(vector)
                for dh in range(2):
                    vS.inc(vector.tensor_tensor(
                        y_bf[:, t * D + dh * 512:t * D + (dh + 1) * 512],
                        pb[2 * (t % 2) + dh][:, :],
                        bias_bc[:, dh * 512:(dh + 1) * 512], ALU.add))
            v_blk(y_v)

        def yspill_q(sync):
            qS.bar(sync)
            vS.wait(sync)
            src = y_bf.rearrange("p (b c d) -> p b c d", b=8, c=2)
            dst = y_sp.rearrange("(c t b) d -> t b c d", c=2, b=8)
            qS.inc(sync.dma_start(out=dst, in_=src), 16)
        q_blk(yspill_q)
        dev_tap("y_p", y_bf[:, :])

        if BUILD_UPTO < 5:
            return nc, dev_outs


        def p5_load(sync):
            qS.bar(sync)
            vS.wait(sync)
            qS.wait(sync)
            for bq in range(4):
                qS.inc(sync.dma_start(
                    out=A[:, bq * 4 * D:(bq + 1) * 4 * D].rearrange(
                        "p (t d) -> p t d", d=D),
                    in_=xb.rearrange("(t p) d -> p t d", p=128)[:, bq * 4:(bq + 1) * 4, :]), 16)
        q_blk(p5_load)

        layernorm(A, A, list(range(16)), Dd, wb=None, mcol=stat_m, icol=stat_i)
        dev_tap("xn_c", A[:, :])

        def p5_hi(vector):
            vS.wait(vector); gq.wait(vector)
            for t in range(16):
                vS.inc(vector.tensor_copy(Dd[:, t * D:(t + 1) * D],
                                          A[:, t * D:(t + 1) * D]))
        v_blk(p5_hi)
        transpose_fm(Dd, C, src_is_bf=True)

        wg_f = sb("wg_f", [128, 64])
        wg_hi = sb("wg_hi", [128, 64], BF16)
        wg_lo = sb("wg_lo", [128, 64], BF16)
        SCRbf = SCR[:, :].bitcast(BF16)
        lob = SCRbf[:, 0:2048]
        lg_sb = sb("lg_sb", [128, 128])
        io8bc = sb("io8bc", [128, 8])
        gbbc = sb("gbbc", [128, 8])
        m8 = sb("m8", [128, 1]); negm = sb("negm", [128, 1])
        sume = sb("sume", [128, 1]); gate8 = sb("gate8", [128, 16])
        exp8 = sb("exp8", [128, 8])
        oh1 = sb("oh1", [128, 8]); mi8 = sb("mi8", [128, 8])
        idx8 = sb("idx8", [128, 16])
        ohb_sb = sb("ohb_sb", [128, 16 * 8], BF16)
        cs_sb = sb("cs_sb", [8, 17])
        pst_sb = sb("pst_sb", [16, 8])
        cnt8 = sb("cnt8", [8, 1])
        cAG = sb("cAG", [8, 8])
        pgrow = sb("pgrow", [1, 8])
        pgbc = sb("pgbc", [128, 8])
        gb_row = sb("gb_row", [1, 8])
        tmp8 = sb("tmp8", [128, 8])
        psrow = sb("psrow", [1, 128])
        cntrow = psrow[0:1, 32:40]
        brow = psrow[0:1, 16:27]
        base_bc = AUX[:, 0:8]
        erow_f = psrow[0:1, 48:64]
        erow_i = sb("erow_i", [1, 16], mybir.dt.int32)
        prefbc_all = sb("prefbc_all", [128, 128])
        cum_sb = sb("cum_sb", [128, 128])
        posc = sb("posc", [128, 1])
        keepc = sb("keepc", [128, 1])
        kg_sb = sb("kg_sb", [128, 16])
        kg2_sb = sb("kg2_sb", [128, 16])
        slot_sb = sb("slot_sb", [128, 16])

        def gb_q(sync):
            qS.bar(sync)
            qS.inc(sync.dma_start(out=gb_row[:, :], in_=gate_b), 16)
        q_blk(gb_q)

        def io8_t(tensor):
            vS.wait(tensor); qS.wait(tensor)
            tS.inc(tensor.matmul(pb[0][:, 0:8], onesr_f[0:1, :], io8[0:1, :],
                                 start=True, stop=True))
            tS.inc(tensor.matmul(pb[1][:, 0:8], onesr_f[0:1, :],
                                 gb_row[0:1, :], start=True, stop=True))
        t_blk(io8_t)

        def io8_v(vector):
            tS.wait(vector)
            vS.inc(vector.tensor_copy(io8bc[:, :], pb[0][:, 0:8]))
            vS.inc(vector.tensor_copy(gbbc[:, :], pb[1][:, 0:8]))
        v_blk(io8_v)

        def router_logits(wgT_dram, hi_fm):
            def wg_ld(sync):
                qS.bar(sync)
                tS.wait(sync)
                qS.inc(sync.dma_start(
                    out=wg_f[:, :].rearrange("p (k n) -> p k n", k=8),
                    in_=wgT_dram.rearrange("(k p) n -> p k n", p=128)), 16)
            q_blk(wg_ld)

            def wg_split(vector):
                qS.wait(vector)
                vS.inc(vector.tensor_copy(wg_hi[:, :], wg_f[:, :]))
                vS.wait(vector)
                vS.inc(vector.tensor_tensor(wg_lo[:, :], wg_f[:, :],
                                            wg_hi[:, :], ALU.subtract))
            v_blk(wg_split)

            def lo_v(vector):
                vS.wait(vector)
                for t in range(16):
                    vS.inc(vector.tensor_tensor(
                        Dd[:, t * D:(t + 1) * D], A[:, t * D:(t + 1) * D],
                        Dd[:, t * D:(t + 1) * D], ALU.subtract))
            v_blk(lo_v)

            for kk in range(DC):
                def lo_tr(tensor, kk=kk):
                    vS.wait(tensor)
                    for n in range(16):
                        tS.inc(tensor.transpose(
                            pbig_bf[:, 2048 + n * 128:2048 + (n + 1) * 128],
                            Dd[:, n * D + kk * 128:n * D + (kk + 1) * 128],
                            idb[:, :]))
                t_blk(lo_tr)

                def lo_ev(vector, kk=kk):
                    tS.wait(vector)
                    for n in range(16):
                        vS.inc(vector.tensor_copy(
                            lob[:, n * 128:(n + 1) * 128],
                            pbig_bf[:, 2048 + n * 128:2048 + (n + 1) * 128]))
                v_blk(lo_ev)

                def lg_t(tensor, kk=kk):
                    vS.wait(tensor)
                    for n in range(16):
                        st = (kk == 0 and n == 0)
                        tS.inc(tensor.matmul(
                            pbig[:, n * 8:(n + 1) * 8],
                            hi_fm[:, kk * 2048 + n * 128:kk * 2048 + (n + 1) * 128],
                            wg_hi[:, kk * 8:(kk + 1) * 8],
                            start=st, stop=False))
                        tS.inc(tensor.matmul(
                            pbig[:, n * 8:(n + 1) * 8],
                            lob[:, n * 128:(n + 1) * 128],
                            wg_hi[:, kk * 8:(kk + 1) * 8],
                            start=False, stop=False))
                        last = (kk == DC - 1 and n == 15)
                        tS.inc(tensor.matmul(
                            pbig[:, n * 8:(n + 1) * 8],
                            hi_fm[:, kk * 2048 + n * 128:kk * 2048 + (n + 1) * 128],
                            wg_lo[:, kk * 8:(kk + 1) * 8],
                            start=False, stop=last))
                t_blk(lg_t)

            def lg_ev(vector):
                tS.wait(vector)
                vS.inc(vector.tensor_copy(lg_sb[:, :], pbig[:, 0:128]))
            v_blk(lg_ev)

        def router_chain(use_gb, cap, kgdst, want_slot, agin, agout):
            for t in range(16):
                def rv(vector, t=t):
                    vS.wait(vector); sS.wait(vector)
                    lg = lg_sb[:, t * 8:(t + 1) * 8]
                    if use_gb:
                        vS.inc(vector.tensor_tensor(lg, lg, gbbc[:, :],
                                                    ALU.add))
                        vS.wait(vector)
                    vS.inc(vector.reduce_max(m8[:, :], lg, axis=AX.X))
                    vS.wait(vector)
                    vS.inc(vector.tensor_scalar(negm[:, :], m8[:, :], -1.0,
                                                None, ALU.mult))
                v_blk(rv)

                def rs(scalar, t=t):
                    vS.wait(scalar)
                    sS.inc(scalar.activation(exp8[:, :],
                                             lg_sb[:, t * 8:(t + 1) * 8],
                                             AF.Exp, bias=negm[:, :],
                                             accum_out=sume[:, :]))
                s_blk(rs)

                def rv2(vector, t=t):
                    sS.wait(vector)
                    lg = lg_sb[:, t * 8:(t + 1) * 8]
                    vS.inc(vector.reciprocal(gate8[:, t:t + 1], sume[:, :]))
                    vS.inc(vector.tensor_scalar(oh1[:, :], lg, m8[:, :],
                                                None, ALU.is_ge))
                    vS.wait(vector)
                    vS.inc(vector.tensor_scalar(oh1[:, :], oh1[:, :],
                                                -99.0, 99.0,
                                                ALU.mult, ALU.add))
                    vS.wait(vector)
                    vS.inc(vector.tensor_tensor(mi8[:, :], io8bc[:, :],
                                                oh1[:, :], ALU.add))
                    vS.wait(vector)
                    vS.inc(vector.tensor_reduce(idx8[:, t:t + 1], mi8[:, :],
                                                axis=AX.X, op=ALU.min))
                    vS.wait(vector)
                    vS.inc(vector.tensor_scalar(oh1[:, :], io8bc[:, :],
                                                idx8[:, t:t + 1], None,
                                                ALU.is_equal))
                    vS.wait(vector)
                    vS.inc(vector.tensor_copy(ohb_sb[:, t * 8:(t + 1) * 8],
                                              oh1[:, :]))
                v_blk(rv2)

            def cum_t(tensor):
                vS.wait(tensor)
                for t in range(16):
                    tS.inc(tensor.matmul(
                        pbig[:, 512 + t * 8:512 + (t + 1) * 8],
                        tri[:, :], ohb_sb[:, t * 8:(t + 1) * 8],
                        start=(t == 0), stop=(t == 15)))
                for t in range(16):
                    tS.inc(tensor.matmul(
                        pb[0][0:8, t:t + 1], ohb_sb[:, t * 8:(t + 1) * 8],
                        ones_b[:, :], start=(t == 0), stop=(t == 15)))
            t_blk(cum_t)

            def cum_v(vector):
                tS.wait(vector)
                vS.inc(vector.tensor_copy(cum_sb[:, :], pbig[:, 512:640]))
                vS.inc(vector.tensor_copy(cs_sb[:, 0:16], pb[0][0:8, 0:16]))
                vS.wait(vector)
                vS.inc(vector.memset(cs_sb[:, 16:17], 0.0))
                for t in range(16):
                    vS.wait(vector)
                    vS.inc(vector.tensor_tensor(tmp8[0:8, 0:1],
                                                cs_sb[:, 16:17],
                                                cs_sb[:, t:t + 1], ALU.add))
                    vS.wait(vector)
                    vS.inc(vector.tensor_copy(cs_sb[:, t:t + 1],
                                              cs_sb[:, 16:17]))
                    vS.wait(vector)
                    vS.inc(vector.tensor_copy(cs_sb[:, 16:17], tmp8[0:8, 0:1]))
            v_blk(cum_v)

            def pst_t(tensor):
                vS.wait(tensor)
                tS.inc(tensor.transpose(pb[1][0:16, 0:8], cs_sb[:, 0:16],
                                        idf[0:8, 0:8]))
            t_blk(pst_t)

            def pst_v(vector):
                tS.wait(vector)
                vS.inc(vector.tensor_copy(pst_sb[:, :], pb[1][0:16, 0:8]))
                vS.inc(vector.tensor_copy(cnt8[:, :], cs_sb[:, 16:17]))
            v_blk(pst_v)

            def psr_g(gpsimd):
                gq.bar(gpsimd)
                vS.wait(gpsimd)
                gq.inc(gpsimd.dma_start(
                    out=psrow[0:1, :].rearrange("o (t e) -> o t e", t=16),
                    in_=pst_sb[:, :]), 16)
            g_blk(psr_g)

            def psr_t(tensor):
                gq.wait(tensor); vS.wait(tensor)
                tS.inc(tensor.matmul(pb[3][:, 0:128], onesr_f[0:1, :],
                                     psrow[0:1, :], start=True, stop=True))
            t_blk(psr_t)

            def psr_v(vector):
                tS.wait(vector)
                vS.inc(vector.tensor_copy(prefbc_all[:, :], pb[3][:, 0:128]))
            v_blk(psr_v)

            if want_slot:
                def sb_t(tensor):
                    vS.wait(tensor)
                    tS.inc(tensor.transpose(pb[2][0:1, 0:8], cnt8[:, :],
                                            idf[0:8, 0:8]))
                t_blk(sb_t)

                def sb_v(vector):
                    thr16 = psrow[0:1, 80:96]
                    ge16 = psrow[0:1, 64:80]
                    tS.wait(vector)
                    vS.inc(vector.tensor_copy(cntrow[:, :], pb[2][0:1, 0:8]))
                    vS.wait(vector)
                    vS.inc(vector.memset(brow[0:1, 0:1], 0.0))
                    vS.wait(vector)
                    vS.inc(vector.tensor_scalar(
                        thr16[:, :], io512[0:1, 0:16], 256.0, 1.0,
                        ALU.mult, ALU.add))
                    for e in range(7):
                        vS.wait(vector)
                        vS.inc(vector.tensor_tensor(
                            brow[0:1, 9:10], brow[0:1, e:e + 1],
                            cntrow[0:1, e:e + 1], ALU.add))
                        vS.wait(vector)
                        vS.inc(vector.tensor_tensor(
                            ge16[:, :],
                            brow[0:1, 9:10].broadcast_to([1, 16]),
                            thr16[:, :], ALU.is_ge))
                        vS.wait(vector)
                        vS.inc(vector.tensor_reduce(
                            brow[0:1, 10:11], ge16[:, :],
                            axis=AX.X, op=ALU.add))
                        vS.wait(vector)
                        vS.inc(vector.tensor_scalar(
                            brow[0:1, e + 1:e + 2], brow[0:1, 10:11],
                            256.0, None, ALU.mult))
                v_blk(sb_v)

                def sb_t2(tensor):
                    vS.wait(tensor)
                    tS.inc(tensor.matmul(pb[2][:, 0:8], onesr_f[0:1, :],
                                         brow[0:1, 0:8], start=True, stop=True))
                    tS.inc(tensor.matmul(pb[3][0:16, 0:8], onesr_f[0:1, 0:16],
                                         brow[0:1, 0:8], start=True, stop=True))
                t_blk(sb_t2)

                def sb_v2(vector):
                    tS.wait(vector)
                    vS.inc(vector.tensor_copy(base_bc[:, :], pb[2][:, 0:8]))
                    vS.wait(vector)
                    vS.inc(vector.tensor_scalar(ccorr[:, 0:8], pb[3][0:16, 0:8],
                                                s256[:, :], None, ALU.is_le))
                    vS.wait(vector)
                    vS.inc(vector.reduce_sum(e16[:, 0:1], ccorr[:, 0:8],
                                             axis=AX.X))
                    vS.wait(vector)
                    vS.inc(vector.tensor_scalar(e16[:, 0:1], e16[:, 0:1],
                                                1.0, -1.0, ALU.mult, ALU.add))
                v_blk(sb_v2)

                def sb_t3(tensor):
                    vS.wait(tensor)
                    tS.inc(tensor.transpose(pb[2][0:1, 0:16], e16[:, 0:1],
                                            idf[0:16, 0:16]))
                t_blk(sb_t3)

                def sb_v3(vector):
                    tS.wait(vector)
                    vS.inc(vector.tensor_copy(erow_f[:, :], pb[2][0:1, 0:16]))
                    vS.wait(vector)
                    vS.inc(vector.tensor_copy(erow_i[:, :], erow_f[:, :]))
                v_blk(sb_v3)

            def ag_q(sync):
                qS.bar(sync)
                vS.wait(sync)
                qS.inc(sync.dma_start(out=agin[0:1, 0:8], in_=cnt8[:, :],
                                      allow_slow_non_contiguous=True), 16)
            q_blk(ag_q)

            def ag_g(gpsimd):
                qS.wait(gpsimd)
                cS.inc(gpsimd.collective_compute(
                    "AllGather", ALU.bypass, replica_groups=[list(range(8))],
                    ins=[agin], outs=[agout]))
            g_blk(ag_g)

            def ag_b(sync):
                qS.bar(sync)
                cS.wait(sync)
                qS.inc(sync.dma_start(out=cAG[:, :], in_=agout), 16)
            q_blk(ag_b)

            def pg_t(tensor):
                qS.wait(tensor); vS.wait(tensor)
                tS.inc(tensor.matmul(pb[2][0:1, 0:8], cm8[:, :], cAG[:, :],
                                     start=True, stop=True))
            t_blk(pg_t)

            def pg_v(vector):
                tS.wait(vector)
                vS.inc(vector.tensor_copy(pgrow[:, :], pb[2][0:1, 0:8]))
            v_blk(pg_v)

            def pg_t2(tensor):
                vS.wait(tensor)
                tS.inc(tensor.matmul(pb[2][:, 0:8], onesr_f[0:1, :],
                                     pgrow[0:1, :], start=True, stop=True))
            t_blk(pg_t2)

            def pg_v2(vector):
                tS.wait(vector)
                vS.inc(vector.tensor_copy(pgbc[:, :], pb[2][:, 0:8]))
            v_blk(pg_v2)

            for t in range(16):
                def pc_v(vector, t=t):
                    vS.wait(vector)
                    vS.inc(vector.tensor_tensor(
                        tmp8[:, :], cum_sb[:, t * 8:(t + 1) * 8],
                        prefbc_all[:, t * 8:(t + 1) * 8], ALU.add))
                    vS.wait(vector)
                    if want_slot:
                        vS.inc(vector.tensor_tensor(
                            mi8[:, :], ohb_sb[:, t * 8:(t + 1) * 8],
                            tmp8[:, :], ALU.mult))
                        vS.wait(vector)
                        vS.inc(vector.reduce_sum(posc[:, :], mi8[:, :],
                                                 axis=AX.X))
                        vS.wait(vector)
                        vS.inc(vector.tensor_scalar(posc[:, :], posc[:, :],
                                                    1.0, -1.0, ALU.mult,
                                                    ALU.add))
                        vS.wait(vector)
                        vS.inc(vector.tensor_tensor(
                            mi8[:, :], ohb_sb[:, t * 8:(t + 1) * 8],
                            base_bc[:, :], ALU.mult))
                        vS.wait(vector)
                        vS.inc(vector.reduce_sum(keepc[:, :], mi8[:, :],
                                                 axis=AX.X))
                        vS.wait(vector)
                        vS.inc(vector.tensor_tensor(slot_sb[:, t:t + 1],
                                                    posc[:, :], keepc[:, :],
                                                    ALU.add))
                        vS.wait(vector)
                    vS.inc(vector.tensor_tensor(tmp8[:, :], tmp8[:, :],
                                                pgbc[:, :], ALU.add))
                    vS.wait(vector)
                    vS.inc(vector.tensor_tensor(
                        mi8[:, :], ohb_sb[:, t * 8:(t + 1) * 8],
                        tmp8[:, :], ALU.mult))
                    vS.wait(vector)
                    vS.inc(vector.reduce_sum(posc[:, :], mi8[:, :], axis=AX.X))
                    vS.wait(vector)
                    vS.inc(vector.tensor_scalar(posc[:, :], posc[:, :],
                                                1.0, -1.0, ALU.mult, ALU.add))
                    vS.wait(vector)
                    vS.inc(vector.tensor_scalar(keepc[:, :], posc[:, :],
                                                float(cap), None, ALU.is_lt))
                    vS.wait(vector)
                    vS.inc(vector.tensor_tensor(kgdst[:, t:t + 1],
                                                keepc[:, :],
                                                gate8[:, t:t + 1], ALU.mult))
                v_blk(pc_v)

        router_logits(Wg, C)
        dev_tap("lg0", lg_sb[:, :])
        router_chain(False, CAP0, kg_sb, False, ag2_in, ag2_out)
        dev_tap("kg0", kg_sb[:, :])

        g_sp = nc.dram_tensor("g_sp", [TOK, D], BF16, kind="Internal").ap()
        load_bcast(be[None, :], bias_bc)

        def we_load(gpsimd):
            gq.bar(gpsimd)
            vS.wait(gpsimd)
            gq.inc(gpsimd.dma_start(
                out=Dd[:, 0:8 * D].rearrange("p (kk n) -> p kk n", kk=8),
                in_=We.rearrange("(kk p) n -> p kk n", p=128)), 16)
        g_blk(we_load)

        gstage = [SCRbf[:, 0:2048], SCRbf[:, 6144:8192]]
        for t in range(16):
            def we_t(tensor, t=t):
                gq.wait(tensor); vS.wait(tensor)
                if t > 1:
                    qS.wait(tensor)
                for dh in range(2):
                    for kk in range(8):
                        tS.inc(tensor.matmul(
                            pb[2 * (t % 2) + dh][:, :],
                            C[:, kk * 2048 + t * 128:kk * 2048 + (t + 1) * 128],
                            Dd[:, kk * D + dh * 512:kk * D + (dh + 1) * 512],
                            start=(kk == 0), stop=(kk == 7)))
            t_blk(we_t)

            def we_v(vector, t=t):
                tS.wait(vector); sS.wait(vector)
                for dh in range(2):
                    vS.inc(vector.tensor_tensor(
                        tmp4[:, dh * 512:(dh + 1) * 512],
                        pb[2 * (t % 2) + dh][:, :],
                        bias_bc[:, dh * 512:(dh + 1) * 512], ALU.add))
            v_blk(we_v)

            def we_s(scalar, t=t):
                vS.wait(scalar)
                sS.inc(scalar.activation(gstage[t % 2][:, 0:D],
                                         tmp4[:, :], AF.Silu,
                                         scale=kg_sb[:, t:t + 1]))
            s_blk(we_s)

            def we_q(sync, t=t):
                qS.bar(sync)
                sS.wait(sync)
                qS.inc(sync.dma_start(
                    out=g_sp[t * 128:(t + 1) * 128, :],
                    in_=gstage[t % 2][:, 0:D]), 16)
            q_blk(we_q)
        dev_tap("xn_c2", A[:, :])

        if BUILD_UPTO < 6:
            return nc, dev_outs

        load_bcast(ln1_w[None, :], lnw_bc)
        load_bcast(ln1_b[None, :], lnb_bc)
        layernorm(A, C, list(range(16)), Dd, wb=(lnw_bc, lnb_bc),
                  mcol=stat_m, icol=stat_i)
        def ch_v(vector):
            vS.wait(vector)
            vS.inc(vector.memset(halo_n1[:, :], 0.0))
        v_blk(ch_v)

        for kk in range(4):
            load_row(convw[kk:kk + 1, :])
            bcast_lnv(w_bc[:, kk * D:(kk + 1) * D])
        load_row(convb[None, :])

        def cb2_t(tensor):
            qS.wait(tensor); vS.wait(tensor)
            tS.inc(tensor.matmul(pbig[:, 0:512], onesr_f[0:1, :],
                                 lnv[0:1, 0:512], start=True, stop=True))
            tS.inc(tensor.matmul(pbig[:, 512:1024], onesr_f[0:1, :],
                                 lnv[0:1, 512:1024], start=True, stop=True))
        t_blk(cb2_t)

        def cb2_v(vector):
            tS.wait(vector)
            for t in range(16):
                vS.inc(vector.tensor_tensor(
                    A[:, t * D:(t + 1) * D], A[:, t * D:(t + 1) * D],
                    pbig[:, 0:1024], ALU.add))
        v_blk(cb2_v)

        conv_round(lambda t: C[:, t * D:(t + 1) * D], 3)
        for shv in (1, 2, 3):
            kk = 3 - shv

            def conv_g2(gpsimd, shv=shv):
                gq.bar(gpsimd)
                vS.wait(gpsimd); gS.wait(gpsimd); gq.wait(gpsimd)
                gq.inc(gpsimd.dma_start(
                    out=Dd[shv:128, :].rearrange("p (t d) -> p t d", d=D),
                    in_=C[0:128 - shv, :].rearrange("p (t d) -> p t d", d=D)), 16)
                gq.inc(gpsimd.dma_start(
                    out=Dd[0:shv, D:16 * D].rearrange("p (t d) -> p t d", d=D),
                    in_=C[128 - shv:128, 0:15 * D].rearrange(
                        "p (t d) -> p t d", d=D)), 16)
                gq.inc(gpsimd.dma_start(
                    out=Dd[0:shv, 0:D],
                    in_=halo_n1[3 - shv:3, :]), 16)
            g_blk(conv_g2)
            conv_round(lambda t: Dd[:, t * D:(t + 1) * D], kk)

        dev_tap("xloc_c", A[:, :])

        def yp_q(sync):
            qS.bar(sync)
            vS.wait(sync); gS.wait(sync)
            for bq in range(2):
                qS.inc(sync.dma_start(
                    out=C[:, bq * 8 * D:(bq + 1) * 8 * D].rearrange(
                        "p (t d) -> p t d", d=D),
                    in_=y_sp.rearrange("(t p) d -> p t d", p=128)[:, bq * 8:(bq + 1) * 8, :]), 16)
        q_blk(yp_q)

        def x1_v(vector):
            qS.wait(vector); vS.wait(vector)
            for t in range(16):
                vS.inc(vector.tensor_tensor(
                    A[:, t * D:(t + 1) * D], A[:, t * D:(t + 1) * D],
                    C[:, t * D:(t + 1) * D], ALU.add))
        v_blk(x1_v)
        dev_tap("x1_c", A[:, :])

        load_bcast(ln2_w[None, :], lnw_bc)
        load_bcast(ln2_b[None, :], lnb_bc)
        layernorm(A, C, list(range(16)), Dd, wb=(lnw_bc, lnb_bc),
                  mcol=stat_m, icol=stat_i)

        def w1_load(gpsimd):
            gq.bar(gpsimd)
            vS.wait(gpsimd)
            gq.inc(gpsimd.dma_start(
                out=Dd[:, :].rearrange("p (kk n) -> p kk n", kk=8),
                in_=mlp_w1.rearrange("(kk p) n -> p kk n", p=128)), 16)
        g_blk(w1_load)
        load_row(mlp_b2[None, :])
        bcast_lnv(bias_bc)

        ln2g = AUXbf[:, 0:4096]
        h1g = AUXbf[:, 4096:4096 + 8192]
        w2psum = [pb[0][:, :], pb[1][:, :], pb[2][:, :], pb[3][:, :],
                  pbig[:, 0:512], pbig[:, 512:1024],
                  pbig[:, 1024:1536], pbig[:, 1536:2048]]
        for tg in range(4):
            for rnd in range(2):
                def tg_tr(tensor, tg=tg, rnd=rnd):
                    vS.wait(tensor); sS.wait(tensor); gq.wait(tensor)
                    for n in range(16):
                        ti, dc = divmod(rnd * 16 + n, 8)
                        t = tg * 4 + ti
                        tS.inc(tensor.transpose(
                            pbig_bf[:, 2048 + n * 128:2048 + (n + 1) * 128],
                            C[:, t * D + dc * 128:t * D + (dc + 1) * 128],
                            idb[:, :]))
                t_blk(tg_tr)

                def tg_ev(vector, tg=tg, rnd=rnd):
                    tS.wait(vector)
                    for n in range(16):
                        ti, dc = divmod(rnd * 16 + n, 8)
                        vS.inc(vector.tensor_copy(
                            ln2g[:, dc * 512 + ti * 128:dc * 512 + (ti + 1) * 128],
                            pbig_bf[:, 2048 + n * 128:2048 + (n + 1) * 128]))
                v_blk(tg_ev)

            for fc in range(16):
                def h1_t(tensor, tg=tg, fc=fc):
                    vS.wait(tensor)
                    if fc > 0:
                        sS.wait(tensor)
                    for kk in range(8):
                        tS.inc(tensor.matmul(
                            pb[fc % 2][:, :],
                            Dd[:, kk * 2048 + fc * 128:kk * 2048 + (fc + 1) * 128],
                            ln2g[:, kk * 512:(kk + 1) * 512],
                            start=(kk == 0), stop=(kk == 7)))
                t_blk(h1_t)

                def h1_s(scalar, tg=tg, fc=fc):
                    tS.wait(scalar)
                    sS.inc(scalar.activation(
                        h1g[:, fc * 512:(fc + 1) * 512],
                        pb[fc % 2][:, :], AF.Gelu,
                        bias=m1b_sb[:, fc:fc + 1]))
                s_blk(h1_s)

            for kk in range(16):
                def w2_ld(gpsimd, tg=tg, kk=kk):
                    gq.bar(gpsimd)
                    sS.wait(gpsimd)
                    if kk > 1 or tg > 0:
                        tS.wait(gpsimd)
                    gq.inc(gpsimd.dma_start(
                        out=(wka if kk % 2 == 0 else wkb)[:, 0:D],
                        in_=mlp_w2[kk * 128:(kk + 1) * 128, :]), 16)
                g_blk(w2_ld)

                def w2_t(tensor, tg=tg, kk=kk):
                    gq.wait(tensor); sS.wait(tensor)
                    if kk == 0:
                        vS.wait(tensor)
                    wt = (wka if kk % 2 == 0 else wkb)
                    for ti in range(4):
                        for dh in range(2):
                            tS.inc(tensor.matmul(
                                w2psum[ti * 2 + dh],
                                h1g[:, kk * 512 + ti * 128:kk * 512 + (ti + 1) * 128],
                                wt[:, dh * 512:(dh + 1) * 512],
                                start=(kk == 0), stop=(kk == 15)))
                t_blk(w2_t)

            def w2_v(vector, tg=tg):
                tS.wait(vector)
                for ti in range(4):
                    t = tg * 4 + ti
                    for dh in range(2):
                        vS.inc(vector.tensor_tensor(
                            tmp4[:, dh * 512:(dh + 1) * 512],
                            w2psum[ti * 2 + dh],
                            bias_bc[:, dh * 512:(dh + 1) * 512], ALU.add))
                    vS.wait(vector)
                    vS.inc(vector.tensor_tensor(
                        A[:, t * D:(t + 1) * D], A[:, t * D:(t + 1) * D],
                        tmp4[:, :], ALU.add))
                    vS.wait(vector)
            v_blk(w2_v)
        dev_tap("smix_c", A[:, :])
        if BUILD_UPTO < 7:
            return nc, dev_outs

        for t in range(16):
            def x2_q(sync, t=t):
                qS.bar(sync)
                vS.wait(sync); sS.wait(sync)
                qS.inc(sync.dma_start(out=tmp4[:, :],
                                      in_=xb[t * 128:(t + 1) * 128, :]), 16)
                qS.inc(sync.dma_start(out=wka[:, 0:D],
                                      in_=g_sp[t * 128:(t + 1) * 128, :]), 16)
            q_blk(x2_q)

            def x2_v(vector, t=t):
                qS.wait(vector); vS.wait(vector)
                vS.inc(vector.tensor_tensor(tmp4b[:, :],
                                            A[:, t * D:(t + 1) * D],
                                            wka[:, 0:D], ALU.mult))
                vS.wait(vector)
                vS.inc(vector.tensor_tensor(A[:, t * D:(t + 1) * D],
                                            tmp4[:, :], tmp4b[:, :], ALU.add))
            v_blk(x2_v)

            def x2_s(sync, t=t):
                qS.bar(sync)
                vS.wait(sync)
                qS.inc(sync.dma_start(out=x2_sp[t * 128:(t + 1) * 128, :],
                                      in_=A[:, t * D:(t + 1) * D]), 16)
            q_blk(x2_s)
        dev_tap("x2_c", A[:, :])

        layernorm(A, A, list(range(16)), Dd, wb=None, mcol=stat_m,
                  icol=stat_i)
        dev_tap("xn1_L1", A[:, :])
        if BUILD_UPTO < 8:
            return nc, dev_outs


        xn1tok_sp = nc.dram_tensor("xn1tok_sp", [TOK, D], BF16,
                                   kind="Internal").ap()
        Qfm = Abf[:, 0:16384]
        Kfm = Abf[:, 16384:32768]
        Vtm = AUXbf[:, 0:16384]
        Ebuf = [SCRbf[:, 0:2048], SCRbf[:, 2048:4096]]
        cum_bf = cum_sb[:, :].bitcast(BF16)
        vaug = [cum_bf[:, 0:65], cum_bf[:, 128:193]]
        lst = SCR[64:65, 2048:4096]
        au_st = SCR[0:64, 2048:4096].bitcast(BF16)
        od_st = SCR[0:64, 3072:4096].bitcast(BF16)

        ones65 = prefbc_all
        nb8 = stat_m[:, 0:1]

        def p8_v0(vector):
            vS.wait(vector)
            vS.inc(vector.memset(nb8, -8.0))

            for t in range(16):
                vS.inc(vector.tensor_copy(Dd[:, t * D:(t + 1) * D],
                                          A[:, t * D:(t + 1) * D]))
        v_blk(p8_v0)

        def p8_spill(sync):
            qS.bar(sync)
            vS.wait(sync)
            qS.inc(sync.dma_start(
                out=xn1tok_sp.rearrange("(t p) d -> p t d", p=128),
                in_=Dd[:, :].rearrange("p (t d) -> p t d", d=D)), 16)
        q_blk(p8_spill)

        transpose_fm(Dd, C, src_is_bf=True)

        router_logits(gate_wT, C)
        dev_tap("lg1", lg_sb[:, :])
        router_chain(True, CAP1, kg2_sb, True, ag3_in, ag3_out)
        dev_tap("kg1", kg2_sb[:, :])
        dev_tap("slot1", slot_sb[:, :])

        def qk_load(gpsimd):
            gq.bar(gpsimd)
            vS.wait(gpsimd); tS.wait(gpsimd)
            gq.inc(gpsimd.dma_start(
                out=Dd[:, 0:8192].rearrange("p (kk n) -> p kk n", kk=8),
                in_=Wq.rearrange("(kk p) n -> p kk n", p=128)), 16)
            gq.inc(gpsimd.dma_start(
                out=Dd[:, 8192:16384].rearrange("p (kk n) -> p kk n", kk=8),
                in_=Wk.rearrange("(kk p) n -> p kk n", p=128)), 16)
        g_blk(qk_load)

        for which, dst in ((0, Qfm), (1, Kfm)):
            for fc in range(8):
                def qk_t(tensor, which=which, fc=fc):
                    gq.wait(tensor); vS.wait(tensor)
                    if which + fc > 0:
                        sS.wait(tensor)
                    for tq in range(4):
                        for kk in range(8):
                            tS.inc(tensor.matmul(
                                pb[tq][:, :],
                                Dd[:, which * 8192 + kk * 1024 + fc * 128:
                                   which * 8192 + kk * 1024 + (fc + 1) * 128],
                                C[:, kk * 2048 + tq * 512:kk * 2048 + (tq + 1) * 512],
                                start=(kk == 0), stop=(kk == 7)))
                t_blk(qk_t)

                def qk_s(scalar, which=which, fc=fc):
                    tS.wait(scalar)
                    for tq in range(4):
                        sS.inc(scalar.activation(
                            dst[:, fc * 2048 + tq * 512:fc * 2048 + (tq + 1) * 512],
                            pb[tq][:, :], AF.Copy))
                s_blk(qk_s)

        dev_tap("qfm0", Qfm[:, 0:2048])
        dev_tap("kfm0", Kfm[:, 0:2048])
        def wv_load(gpsimd):
            gq.bar(gpsimd)
            sS.wait(gpsimd)
            gq.inc(gpsimd.dma_start(
                out=Dd[:, 0:8192].rearrange("p (kk n) -> p kk n", kk=8),
                in_=Wv.rearrange("(kk p) n -> p kk n", p=128)), 16)
        g_blk(wv_load)

        for t in range(16):
            def v_t(tensor, t=t):
                gq.wait(tensor); vS.wait(tensor)
                if t > 1:
                    sS.wait(tensor)
                for dh in range(2):
                    for kk in range(8):
                        tS.inc(tensor.matmul(
                            pb[2 * (t % 2) + dh][:, :],
                            C[:, kk * 2048 + t * 128:kk * 2048 + (t + 1) * 128],
                            Dd[:, kk * 1024 + dh * 512:kk * 1024 + (dh + 1) * 512],
                            start=(kk == 0), stop=(kk == 7)))
            t_blk(v_t)

            def v_s(scalar, t=t):
                tS.wait(scalar)
                for dh in range(2):
                    sS.inc(scalar.activation(
                        Vtm[:, t * 1024 + dh * 512:t * 1024 + (dh + 1) * 512],
                        pb[2 * (t % 2) + dh][:, :], AF.Copy))
            s_blk(v_s)

        def p8_v1(vector):
            vS.wait(vector)
            vS.inc(vector.memset(vaug[0][:, 64:65], 1.0))
            vS.inc(vector.memset(vaug[1][:, 64:65], 1.0))
            vS.inc(vector.memset(ones65[64:65, :], 1.0))
        v_blk(p8_v1)
        sS_exp = {}
        for h in range(16):
            hc, ho = divmod(h, 2)
            for kt in range(16):
                q0 = 128 * kt
                if kt >= 8:
                    base = (kt % 2) * 1024
                    W = 2048 - q0
                    chunks = [(ls, min(512, W - ls))
                              for ls in range(0, W, 512)]
                    sdep = sS_exp.get((h, kt - 2)) if kt >= 9 else None
                else:
                    base = None
                    chunks = []
                    for c in range(q0 // 512, 4):
                        gs = max(q0, 512 * c)
                        chunks.append((gs - q0, 512 * (c + 1) - gs))
                    sdep = None

                def s_t(tensor, hc=hc, ho=ho, kt=kt, q0=q0, base=base,
                        chunks=chunks, sdep=sdep, h=h):
                    vS.wait(tensor)
                    if base is None or kt == 8:
                        sS.wait(tensor)
                    elif sdep:
                        sS.wait(tensor, sdep)
                    for (ls, w) in chunks:
                        dst0 = (q0 + ls) if base is None else (base + ls)
                        tS.inc(tensor.matmul(
                            pbig[:, dst0:dst0 + w],
                            Kfm[ho * 64:(ho + 1) * 64,
                                hc * 2048 + kt * 128:hc * 2048 + (kt + 1) * 128],
                            Qfm[ho * 64:(ho + 1) * 64,
                                hc * 2048 + q0 + ls:hc * 2048 + q0 + ls + w],
                            start=True, stop=True))
                t_blk(s_t)

                def s_m(vector, kt=kt, q0=q0, base=base):
                    tS.wait(vector)
                    d0 = q0 if base is None else base
                    vS.inc(vector.tensor_tensor(
                        pbig[:, d0:d0 + 128], pbig[:, d0:d0 + 128],
                        caus[:, :], ALU.add))
                v_blk(s_m)

                def s_e(scalar, kt=kt, q0=q0, base=base):
                    tS.wait(scalar); vS.wait(scalar)
                    d0 = q0 if base is None else base
                    sS.inc(scalar.activation(
                        Ebuf[kt % 2][:, 0:2048 - q0],
                        pbig[:, d0:d0 + 2048 - q0], AF.Exp,
                        bias=nb8, scale=0.125))
                s_blk(s_e)
                sS_exp[(h, kt)] = sS.v

                def va_v(vector, h=h, kt=kt):
                    tS.wait(vector)
                    vS.inc(vector.tensor_copy(
                        vaug[kt % 2][:, 0:64],
                        Vtm[:, kt * 1024 + h * 64:kt * 1024 + (h + 1) * 64]))
                v_blk(va_v)

                avch = []
                for c in range(q0 // 512, 4):
                    gs = max(q0, 512 * c)
                    avch.append((c, gs, 512 * (c + 1) - gs))

                def av_t(tensor, h=h, kt=kt, q0=q0, chunks=avch):
                    sS.wait(tensor); vS.wait(tensor)
                    for (c, gs, w) in chunks:
                        last = (kt == min(15, 4 * c + 3))
                        tS.inc(tensor.matmul(
                            pb[c][0:65, gs - 512 * c:gs - 512 * c + w],
                            vaug[kt % 2][:, :],
                            Ebuf[kt % 2][:, gs - q0:gs - q0 + w],
                            start=(kt == 0), stop=last))
                t_blk(av_t)

            def h_v1(vector, h=h):
                tS.wait(vector); vS.wait(vector); gq.wait(vector)
                for qc in range(4):
                    vS.inc(vector.tensor_copy(
                        lst[:, qc * 512:(qc + 1) * 512], pb[qc][64:65, :]))
                vS.wait(vector)
                vS.inc(vector.reciprocal(lst, lst))
                vS.wait(vector)
                for qc in range(4):
                    vS.inc(vector.tensor_copy(
                        au_st[:, qc * 512:(qc + 1) * 512], pb[qc][0:64, :]))
            v_blk(h_v1)

            for qc in range(4):
                def inv_t(tensor, qc=qc, h=h):
                    vS.wait(tensor)
                    tS.inc(tensor.matmul(
                        pbig[0:64, 0:512], ones65[64:65, 0:64],
                        lst[:, qc * 512:(qc + 1) * 512],
                        start=True, stop=True))
                t_blk(inv_t)

                def inv_v(vector, qc=qc, h=h, hc=hc, ho=ho):
                    tS.wait(vector)
                    if ho == 0:
                        vS.inc(vector.tensor_tensor(
                            C[0:64, hc * 2048 + qc * 512:hc * 2048 + (qc + 1) * 512],
                            au_st[:, qc * 512:(qc + 1) * 512],
                            pbig[0:64, 0:512], ALU.mult))
                    else:
                        vS.inc(vector.tensor_tensor(
                            od_st[:, qc * 512:(qc + 1) * 512],
                            au_st[:, qc * 512:(qc + 1) * 512],
                            pbig[0:64, 0:512], ALU.mult))
                v_blk(inv_v)

            if ho == 1:
                def od_g(gpsimd, hc=hc):
                    gq.bar(gpsimd)
                    vS.wait(gpsimd)
                    gq.inc(gpsimd.dma_start(
                        out=C[64:128, hc * 2048:(hc + 1) * 2048],
                        in_=od_st[:, :]), 16)
                g_blk(od_g)

        dev_tap("vtm0", Vtm[:, 0:2048])
        dev_tap("attnC", C[:, 0:2048])
        def wr_load(gpsimd):
            gq.bar(gpsimd)
            tS.wait(gpsimd)
            gq.inc(gpsimd.dma_start(
                out=Dd[:, 0:8192].rearrange("p (kk n) -> p kk n", kk=8),
                in_=Wr.rearrange("(kk p) n -> p kk n", p=128)), 16)
        g_blk(wr_load)

        for t in range(16):
            def wr_t(tensor, t=t):
                gq.wait(tensor); gS.wait(tensor); vS.wait(tensor)
                if t > 1:
                    sS.wait(tensor)
                for dh in range(2):
                    for kk in range(8):
                        tS.inc(tensor.matmul(
                            pb[2 * (t % 2) + dh][:, :],
                            C[:, kk * 2048 + t * 128:kk * 2048 + (t + 1) * 128],
                            Dd[:, kk * 1024 + dh * 512:kk * 1024 + (dh + 1) * 512],
                            start=(kk == 0), stop=(kk == 7)))
            t_blk(wr_t)

            def wr_s(scalar, t=t):
                tS.wait(scalar)
                for dh in range(2):
                    sS.inc(scalar.activation(
                        AUXbf[:, t * 1024 + dh * 512:t * 1024 + (dh + 1) * 512],
                        pb[2 * (t % 2) + dh][:, :], AF.Copy))
            s_blk(wr_s)

        def s1_spill(sync):
            qS.bar(sync)
            sS.wait(sync)
            qS.inc(sync.dma_start(
                out=s1_sp.rearrange("(t p) d -> p t d", p=128),
                in_=AUXbf[:, :].rearrange("p (t d) -> p t d", d=D)), 16)
        q_blk(s1_spill)
        dev_tap("s1_c", AUXbf[:, :])
        if BUILD_UPTO < 9:
            return nc, dev_outs

        io512bc = SCR[:, 3072:3584]
        ptile = [SCRbf[:, 0:512], SCRbf[:, 512:1024]]
        slotloc = SCR[:, 3584:3585]
        b1sd = [SCR[:, 3600:3632], SCR[:, 3632:3664]]
        b2r = SCRbf[:, 4096:5120][0:8, :]
        ct_tile = SCRbf[:, 5120:5248]
        srow128 = SCR[0:1, 2816:2944]
        b2tok = tmp4b
        tS_p8, vS_p8, sS_p8, qS_p8 = tS.v, vS.v, sS.v, qS.v
        w1sem = [Sem(nc, ctx, f"w1s{r}") for r in range(8)]
        w2sem = [Sem(nc, ctx, f"w2s{r}") for r in range(8)]
        b1sem = [Sem(nc, ctx, "b1s0"), Sem(nc, ctx, "b1s1")]
        eoq = Sem(nc, ctx, "eoq")
        b2row = SCRbf[0:1, 5120:6144]
        b2bc_sb = SCRbf[:, 6144:7168]

        def p9_ld(gpsimd):
            gq.bar(gpsimd)
            qS.wait(gpsimd); sS.wait(gpsimd); vS.wait(gpsimd)
            gq.inc(gpsimd.dma_start(
                out=Dd[:, :].rearrange("p (t d) -> p t d", d=D),
                in_=xn1tok_sp.rearrange("(t p) d -> p t d", p=128)), 16)
            gq.inc(gpsimd.dma_start(out=b2r, in_=b2), 16)
        g_blk(p9_ld)

        def p9_t(tensor):
            vS.wait(tensor); qS.wait(tensor)
            tS.inc(tensor.matmul(pb[0][:, :], onesr_f[0:1, :],
                                 io512[0:1, :], start=True, stop=True))
        t_blk(p9_t)

        def p9_v(vector):
            tS.wait(vector)
            vS.inc(vector.tensor_copy(io512bc[:, :], pb[0][:, :]))
        v_blk(p9_v)

        w1v = w1.rearrange("e h d f -> (e h d) f")
        w2v = w2.rearrange("e f n -> (e f) n")
        b1v = b1r_d.rearrange("e p c -> (e p) c")
        SCRi = SCR[:, :].bitcast(mybir.dt.int32)
        tblw1 = SCRi[:, 3664:3680]
        tblw2 = SCRi[:, 3680:3696]
        tblb1 = SCRi[:, 3696:3712]
        tbtmp = SCR[:, 3712:3728]

        def tbl_t(tensor):
            vS.wait(tensor); qS.wait(tensor)
            tS.inc(tensor.matmul(pb[0][:, 0:16], onesr_f[0:1, :],
                                 erow_f[:, :], start=True, stop=True))
        t_blk(tbl_t)

        sloti = SCRi[:, 3728:3744]
        onesr_b = SCRbf[0:1, 7488:7616]
        oh8seg = SCRbf[0:8, 7616:7632]

        def tbl_v(vector):
            tS.wait(vector); qS.wait(vector)
            for mult, dst in ((2048.0, tblw1), (4096.0, tblw2),
                              (128.0, tblb1)):
                vS.wait(vector)
                vS.inc(vector.tensor_scalar(tbtmp[:, :], pb[0][:, 0:16],
                                            mult, iost[:, 0:1],
                                            ALU.mult, ALU.add))
                vS.wait(vector)
                vS.inc(vector.tensor_copy(dst, tbtmp[:, :]))
            vS.wait(vector)
            vS.inc(vector.tensor_scalar(oh8seg[:, :], pb[0][0:8, 0:16],
                                        iost[0:8, 0:1], None, ALU.is_equal))
            vS.inc(vector.tensor_copy(sloti[:, :], slot_sb[:, :]))
            vS.inc(vector.memset(onesr_b[:, :], 1.0))
        v_blk(tbl_v)
        vS_tbl = vS.v

        dsp_psum = [pb[0][:, :], pb[1][:, :], pb[2][:, :], pb[3][:, :],
                    pbig[:, 0:512], pbig[:, 512:1024],
                    pbig[:, 1024:1536], pbig[:, 1536:2048]]
        for sc in range(8):
            for tt in range(16):
                def pt_v(vector, sc=sc, tt=tt):
                    vS.wait(vector)
                    if tt > 1 or sc > 0:
                        tS.wait(vector)
                    vS.inc(vector.tensor_scalar(
                        slotloc[:, :], slot_sb[:, tt:tt + 1], 1.0,
                        float(-512 * sc), ALU.mult, ALU.add))
                    vS.wait(vector)
                    vS.inc(vector.tensor_scalar(
                        ptile[tt % 2][:, 0:512], io512bc[:, :],
                        slotloc[:, :], None, ALU.is_equal))
                v_blk(pt_v)

                def pt_t(tensor, sc=sc, tt=tt):
                    vS.wait(tensor); gq.wait(tensor)
                    for dc in range(8):
                        tS.inc(tensor.matmul(
                            dsp_psum[dc],
                            Dd[:, tt * D + dc * 128:tt * D + (dc + 1) * 128],
                            ptile[tt % 2][:, 0:512],
                            start=(tt == 0), stop=(tt == 15)))
                t_blk(pt_t)

            def dsp_v(vector, sc=sc):
                tS.wait(vector)
                for dc in range(8):
                    for half in range(2):
                        sg = 2 * sc + half
                        vS.inc(vector.tensor_copy(
                            Abf[:, sg * 2048 + dc * 256:sg * 2048 + (dc + 1) * 256],
                            dsp_psum[dc][:, half * 256:(half + 1) * 256]))
            v_blk(dsp_v)

        w2_psum = [pb[0][:, :], pb[1][:, :], pb[2][:, :], pb[3][:, :]]

        def hbank(ft):
            pr, hf = ft // 2, ft % 2
            if pr < 4:
                return pbig[:, pr * 512 + hf * 256:pr * 512 + (hf + 1) * 256]
            return pb[pr - 4][:, hf * 256:(hf + 1) * 256]

        w1_free = {r: tS_p8 for r in range(8)}
        w2_free = {r: tS_p8 for r in range(8)}

        def w1slot(r):
            if r < 4:
                return C[:, r * 2048:(r + 1) * 2048]
            return AUXbf[:, 8192 + (r - 4) * 2048:8192 + (r - 3) * 2048]
        b1_free = {0: sS_p8, 1: sS_p8}
        evac_val = {}
        eov_val = None
        for sg in range(16):
            def b1_ld(gpsimd, sg=sg):
                if sg == 0:
                    gpsimd.wait_ge(tS.h, tS_p8)
                    gpsimd.wait_ge(vS.h, vS_tbl)
                    gpsimd.wait_ge(sS.h, sS_p8)
                    gpsimd.wait_ge(qS.h, qS_p8)
                sS.wait(gpsimd, b1_free[sg % 2])
                b1sem[sg % 2].inc(gpsimd.indirect_dma_start(
                    b1sd[sg % 2][:, :], None, b1v,
                    IndirectOffsetOnAxis(ap=tblb1[:, sg:sg + 1],
                                         axis=0)), 16)
            g_blk(b1_ld)
            for fh in range(2):
                for dc in range(8):
                    r = (fh * 8 + dc) % 8

                    def w1_ld(gpsimd, sg=sg, fh=fh, dc=dc, r=r):
                        if sg == 0 and fh == 0 and r >= 4:
                            gpsimd.wait_ge(qS.h, qS_p8)
                        tS.wait(gpsimd, w1_free[r])
                        w1sem[r].inc(gpsimd.indirect_dma_start(
                            w1slot(r), None, w1v,
                            IndirectOffsetOnAxis(ap=tblw1[:, sg:sg + 1],
                                                 axis=0),
                            element_offset=fh * 2097152 + dc * 262144), 16)
                    g_blk(w1_ld)

                    def hd_t(tensor, sg=sg, fh=fh, dc=dc, r=r):
                        w1sem[r].wait(tensor)
                        if dc == 0 and fh == 0 and sg == 0:
                            vS.wait(tensor)
                        for i, ft in enumerate((0, 2, 4, 6, 8, 10, 12, 14,
                                                1, 3, 5, 7, 9, 11, 13, 15)):
                            if dc == 0 and i < 8:
                                bk = ft // 2
                                if i == 0 and fh == 0 and eov_val is not None:
                                    vS.wait(tensor, eov_val)
                                if bk in evac_val:
                                    if bk < 4:
                                        sS.wait(tensor, evac_val[bk])
                                    else:
                                        vS.wait(tensor, evac_val[bk])
                            tS.inc(tensor.matmul(
                                hbank(ft),
                                w1slot(r)[:, ft * 128:(ft + 1) * 128],
                                Abf[:, sg * 2048 + dc * 256:
                                    sg * 2048 + (dc + 1) * 256],
                                start=(dc == 0 and ft % 2 == 0),
                                stop=(dc == 7 and ft % 2 == 1)))
                    t_blk(hd_t)
                    w1_free[r] = tS.v

                def hd_s(scalar, sg=sg, fh=fh):
                    tS.wait(scalar)
                    b1sem[sg % 2].wait(scalar)
                    for ft in range(8):
                        gfc = fh * 16 + ft
                        sS.inc(scalar.activation(
                            AUXbf[:, gfc * 256:(gfc + 1) * 256],
                            hbank(ft), AF.Relu,
                            bias=b1sd[sg % 2][:, gfc:gfc + 1]))
                s_blk(hd_s)
                for bk in range(4):
                    evac_val[bk] = sS.v

                def hd_v(vector, sg=sg, fh=fh):
                    tS.wait(vector)
                    b1sem[sg % 2].wait(vector)
                    for ft in range(8, 16):
                        gfc = fh * 16 + ft
                        vS.inc(vector.tensor_scalar(
                            AUXbf[:, gfc * 256:(gfc + 1) * 256],
                            hbank(ft), b1sd[sg % 2][:, gfc:gfc + 1],
                            0.0, ALU.add, ALU.max))
                v_blk(hd_v)
                for bk in range(4, 8):
                    evac_val[bk] = vS.v
            b1_free[sg % 2] = sS.v

            for fk in range(32):
                rk = fk % 8

                def w2_ld(gpsimd, sg=sg, fk=fk, rk=rk):
                    tS.wait(gpsimd, w2_free[rk])
                    w2sem[rk].inc(gpsimd.indirect_dma_start(
                        C[:, 8192 + rk * 1024:8192 + (rk + 1) * 1024],
                        None, w2v,
                        IndirectOffsetOnAxis(ap=tblw2[:, sg:sg + 1],
                                             axis=0),
                        element_offset=fk * 131072), 16)
                g_blk(w2_ld)

                def w2_t(tensor, sg=sg, fk=fk, rk=rk):
                    w2sem[rk].wait(tensor)
                    if fk == 0:
                        sS.wait(tensor); vS.wait(tensor)
                    for ts2 in range(2):
                        for dh in range(2):
                            tS.inc(tensor.matmul(
                                w2_psum[ts2 * 2 + dh],
                                AUXbf[:, fk * 256 + ts2 * 128:
                                      fk * 256 + (ts2 + 1) * 128],
                                C[:, 8192 + rk * 1024 + dh * 512:
                                  8192 + rk * 1024 + (dh + 1) * 512],
                                start=(fk == 0), stop=(fk == 31)))
                t_blk(w2_t)
                w2_free[rk] = tS.v

            def b2b_t1(tensor, sg=sg):
                for dh in range(2):
                    tS.inc(tensor.matmul(
                        pbig[0:1, dh * 512:(dh + 1) * 512],
                        oh8seg[:, sg:sg + 1],
                        b2r[:, dh * 512:(dh + 1) * 512],
                        start=True, stop=True))
            t_blk(b2b_t1)

            def b2b_v1(vector, sg=sg):
                tS.wait(vector)
                vS.inc(vector.tensor_copy(b2row[:, :], pbig[0:1, 0:1024]))
            v_blk(b2b_v1)

            def b2b_t2(tensor, sg=sg):
                vS.wait(tensor)
                for dh in range(2):
                    tS.inc(tensor.matmul(
                        pbig[:, dh * 512:(dh + 1) * 512], onesr_b[:, :],
                        b2row[0:1, dh * 512:(dh + 1) * 512],
                        start=True, stop=True))
            t_blk(b2b_t2)

            def b2b_v2(vector, sg=sg):
                tS.wait(vector)
                vS.inc(vector.tensor_copy(b2bc_sb[:, :], pbig[:, 0:1024]))
            v_blk(b2b_v2)

            def eo_v(vector, sg=sg):
                vS.wait(vector)
                for ts2 in range(2):
                    for dh in range(2):
                        vS.inc(vector.tensor_tensor(
                            Abf[:, sg * 2048 + ts2 * 1024 + dh * 512:
                                sg * 2048 + ts2 * 1024 + (dh + 1) * 512],
                            w2_psum[ts2 * 2 + dh],
                            b2bc_sb[:, dh * 512:(dh + 1) * 512], ALU.add))
            v_blk(eo_v)
            eov_val = vS.v

            def eo_q(sync, sg=sg):
                vS.wait(sync)
                for ts2 in range(2):
                    eoq.inc(sync.dma_start(
                        out=eout_sp[sg * 256 + ts2 * 128:
                                    sg * 256 + (ts2 + 1) * 128, :],
                        in_=Abf[:, sg * 2048 + ts2 * 1024:
                                sg * 2048 + (ts2 + 1) * 1024]), 16)
            q_blk(eo_q)

        def cg_g(gpsimd):
            eoq.wait(gpsimd)
            tS.wait(gpsimd); vS.wait(gpsimd)
            for t in range(16):
                gq.inc(gpsimd.indirect_dma_start(
                    Dd[:, t * 1024:(t + 1) * 1024], None, eout_sp,
                    IndirectOffsetOnAxis(ap=sloti[:, t:t + 1], axis=0)), 16)
        g_blk(cg_g)

        def cb_s(scalar):
            gq.wait(scalar); vS.wait(scalar)
            for t in range(16):
                sS.inc(scalar.activation(
                    AUXbf[:, t * 1024:(t + 1) * 1024],
                    Dd[:, t * 1024:(t + 1) * 1024],
                    AF.Silu, scale=kg2_sb[:, t:t + 1]))
        s_blk(cb_s)
        dev_tap("gm1", AUXbf[:, :])
        if BUILD_UPTO < 10:
            return nc, dev_outs

        oSp = [Sem(nc, ctx, "oSp0"), Sem(nc, ctx, "oSp1")]
        ost_free = {0: 0, 1: 0}
        for t in range(16):
            def f_q(sync, t=t):
                qS.bar(sync)
                vS.wait(sync)
                if t == 0:
                    sS.wait(sync)
                qS.inc(sync.dma_start(out=tmp4[:, :],
                                      in_=x2_sp[t * 128:(t + 1) * 128, :]), 16)
                qS.inc(sync.dma_start(out=wka[:, 0:D],
                                      in_=s1_sp[t * 128:(t + 1) * 128, :]), 16)
            q_blk(f_q)

            def f_v(vector, t=t):
                qS.wait(vector); vS.wait(vector)
                if t == 0:
                    sS.wait(vector)
                oSp[t % 2].wait(vector, ost_free[t % 2])
                vS.inc(vector.tensor_tensor(
                    tmp4b[:, :], wka[:, 0:D],
                    AUXbf[:, t * 1024:(t + 1) * 1024], ALU.mult))
                vS.wait(vector)
                vS.inc(vector.tensor_tensor(
                    SCR[:, (t % 2) * D:(t % 2 + 1) * D], tmp4[:, :],
                    tmp4b[:, :], ALU.add))
            v_blk(f_v)

            def f_s(sync, t=t):
                vS.wait(sync)
                oSp[t % 2].inc(sync.dma_start(
                    out=out[t * 128:(t + 1) * 128, :],
                    in_=SCR[:, (t % 2) * D:(t % 2 + 1) * D]), 16)
            q_blk(f_s)
            ost_free[t % 2] = oSp[t % 2].v

        def fin_q(sync):
            oSp[0].wait(sync)
            oSp[1].wait(sync)
        q_blk(fin_q)



    return nc, dev_outs


def _prep_inputs(core, **inp):
    x = np.asarray(inp["L0_attn_x"], np.float32)
    i = core
    xb = np.ascontiguousarray(x[i])
    xs = np.ascontiguousarray(
        x[:, 256 * i:256 * (i + 1), :].reshape(2048, D))
    if i == 0:
        xh = np.zeros((24, D), np.float32)
    else:
        xh = np.ascontiguousarray(
            x[:, 256 * i - 3:256 * i, :].reshape(24, D))
    d = {
        "xb": xb, "xs": xs, "xh": xh,
        "ln1_w": np.asarray(inp["L0_attn_ln1_w"], np.float32),
        "ln1_b": np.asarray(inp["L0_attn_ln1_b"], np.float32),
        "convw": np.ascontiguousarray(
            np.asarray(inp["L0_attn_conv_w"], np.float32)[:, 0, :].T),
        "convb": np.asarray(inp["L0_attn_conv_b"], np.float32),
        "lns_w": np.asarray(inp["L0_attn_lns_w"], np.float32),
        "lns_b": np.asarray(inp["L0_attn_lns_b"], np.float32),
        "in_w": np.asarray(inp["L0_attn_in_w"], np.float32),
        "in_b": np.ascontiguousarray(np.asarray(inp["L0_attn_in_b"], np.float32).reshape(16, 128).T),
        "xp_w": np.asarray(inp["L0_attn_xp_w"], np.float32),
        "xp_b": np.asarray(inp["L0_attn_xp_b"], np.float32).reshape(16, 1),
        "A_log": np.asarray(inp["L0_attn_A_log"], np.float32).reshape(16, 1),
        "Dmat": np.asarray(inp["L0_attn_D"], np.float32),
        "out_w": np.asarray(inp["L0_attn_out_w"], np.float32),
        "out_b": np.asarray(inp["L0_attn_out_b"], np.float32),
        "ln2_w": np.asarray(inp["L0_attn_ln2_w"], np.float32),
        "ln2_b": np.asarray(inp["L0_attn_ln2_b"], np.float32),
        "mlp_w1": np.asarray(inp["L0_attn_mlp_w1"], np.float32),
        "mlp_b1": np.ascontiguousarray(np.asarray(inp["L0_attn_mlp_b1"], np.float32).reshape(16, 128).T),
        "mlp_w2": np.asarray(inp["L0_attn_mlp_w2"], np.float32),
        "mlp_b2": np.asarray(inp["L0_attn_mlp_b2"], np.float32),
        "Wg": np.asarray(inp["L0_ffn_Wg"], np.float32),
        "We": np.asarray(inp["L0_ffn_We"], np.float32),
        "be": np.asarray(inp["L0_ffn_be"], np.float32),
        "Wq": np.asarray(inp["L1_attn_Wq"], np.float32),
        "Wk": np.asarray(inp["L1_attn_Wk"], np.float32),
        "Wv": np.asarray(inp["L1_attn_Wv"], np.float32),
        "Wr": np.asarray(inp["L1_attn_Wr"], np.float32),
        "gate_wT": np.ascontiguousarray(
            np.asarray(inp["L1_ffn_gate_w"], np.float32).T),
        "gate_b": np.asarray(inp["L1_ffn_gate_b"], np.float32).reshape(1, 8),
        "w1": np.ascontiguousarray(
            np.asarray(inp["L1_ffn_w1"], np.float32)
            .reshape(8, D, 2, FF // 2).transpose(0, 2, 1, 3)),
        "b1r": np.ascontiguousarray(np.asarray(inp["L1_ffn_b1"], np.float32).reshape(8, 32, 128).transpose(0, 2, 1)),
        "w2": np.asarray(inp["L1_ffn_w2"], np.float32),
        "b2": np.asarray(inp["L1_ffn_b2"], np.float32),
        "identf": np.eye(128, dtype=np.float32),
        "identb": np.eye(128, dtype=np.float32).astype(ml_dtypes.bfloat16),
        "tri_bf": np.triu(np.ones((128, 128), np.float32)).astype(
            ml_dtypes.bfloat16),
        "cmask": (np.arange(8) < i).astype(np.float32).reshape(8, 1),
        "cexp": np.maximum(i - 1 - np.arange(8), 0).astype(
            np.float32).reshape(8, 1),
        "hmask": np.full((24, 1), 0.0 if i == 0 else 1.0, np.float32),
        "causm": (-400.0 * (np.arange(128)[:, None] > np.arange(128)[None, :])
                  ).astype(ml_dtypes.bfloat16),
        "iota8": np.arange(8, dtype=np.float32).reshape(1, 8),
        "iota512": np.arange(512, dtype=np.float32).reshape(1, 512),
        "iota_st": (np.arange(32)[None, :] * 128
                    + np.arange(128)[:, None]).astype(np.float32),
        "seg256": (np.arange(16, dtype=np.float32) * 256).reshape(16, 1),
    }
    return d


def kernel(**inputs):
    nc, dev_outs = build()
    in_maps = [_prep_inputs(i, **inputs) for i in range(8)]
    res = run_bass_kernel_spmd(nc, in_maps, core_ids=list(range(8)))
    outp = np.stack([np.asarray(res.results[i]["out"], np.float32)
                     for i in range(8)], axis=0)
    return outp.reshape(8, 2048, 1024)


if __name__ == "__main__":
    print("kernel module ok")


# version 09 (final)
# speedup vs baseline: 1.0098x; 1.0024x over previous
import numpy as np
import ml_dtypes
from contextlib import ExitStack

import concourse.bass as bass
import concourse.mybir as mybir
from concourse.bass_utils import run_bass_kernel_spmd
from concourse.bass_core_rust import IndirectOffsetOnAxis

F32 = mybir.dt.float32
BF16 = mybir.dt.bfloat16
ALU = mybir.AluOpType
AF = mybir.ActivationFunctionType
AX = mybir.AxisListType

B, S, D = 8, 2048, 1024
TOK = 2048
NT = 16
DC = 8
EPS = 1e-5
CAP0 = 2048.0
CAP1 = 2560.0
M1 = 384
NSLOT = 8 * M1
FF = 4096

DEV_TAPS = []
BUILD_UPTO = 99


class Sem:
    def __init__(self, nc, ctx, name):
        self.h = ctx.enter_context(nc.semaphore(name))
        self.v = 0
        self.mw = 0

    def inc(self, inst, n=1):
        inst.then_inc(self.h, n)
        self.v += n
        return self.v

    def wait(self, eng, v=None):
        w = self.v if v is None else v
        self.mw = max(self.mw, w)
        eng.wait_ge(self.h, w)

    def bar(self, eng):
        if self.mw > 0:
            eng.wait_ge(self.h, self.mw)


def build():
    nc = bass.Bass("TRN2", target_bir_lowering=False, debug=False)

    def din(name, shape, dt=F32):
        return nc.dram_tensor(name, shape, dt, kind="ExternalInput").ap()

    xb = din("xb", [TOK, D])
    xs = din("xs", [TOK, D])
    xh = din("xh", [24, D])
    ln1_w = din("ln1_w", [D]); ln1_b = din("ln1_b", [D])
    convw = din("convw", [4, D]); convb = din("convb", [D])
    lns_w = din("lns_w", [D]); lns_b = din("lns_b", [D])
    in_w = din("in_w", [D, 2 * D]); in_b = din("in_b", [128, 16])
    xp_w = din("xp_w", [D, 16]); xp_b = din("xp_b", [16, 1])
    A_log = din("A_log", [16, 1])
    Dmat = din("Dmat", [16, D])
    out_w = din("out_w", [D, D]); out_b = din("out_b", [D])
    ln2_w = din("ln2_w", [D]); ln2_b = din("ln2_b", [D])
    mlp_w1 = din("mlp_w1", [D, 2 * D]); mlp_b1 = din("mlp_b1", [128, 16])
    mlp_w2 = din("mlp_w2", [2 * D, D]); mlp_b2 = din("mlp_b2", [D])
    Wg = din("Wg", [D, 8])
    We = din("We", [D, D]); be = din("be", [D])
    Wq = din("Wq", [D, D]); Wk = din("Wk", [D, D])
    Wv = din("Wv", [D, D]); Wr = din("Wr", [D, D])
    gate_wT = din("gate_wT", [D, 8]); gate_b = din("gate_b", [1, 8])
    w1 = din("w1", [8, 2, D, FF // 2]); b1r_d = din("b1r", [8, 128, 32])
    w2 = din("w2", [8, FF, D]); b2 = din("b2", [8, D])
    identf = din("identf", [128, 128])
    identb = din("identb", [128, 128], BF16)
    tri_bf = din("tri_bf", [128, 128], BF16)
    cmask = din("cmask", [8, 1])
    cexp = din("cexp", [8, 1])
    hmask = din("hmask", [24, 1])
    causm = din("causm", [128, 128], BF16)
    iota8 = din("iota8", [1, 8])
    iota512 = din("iota512", [1, 512])
    iota_st = din("iota_st", [128, 32])
    seg256 = din("seg256", [16, 1])

    out = nc.dram_tensor("out", [TOK, D], F32, kind="ExternalOutput").ap()

    y_sp = nc.dram_tensor("y_sp", [TOK, D], BF16, kind="Internal").ap()
    eout_sp = nc.dram_tensor("eout_sp", [4096, D], BF16, kind="Internal").ap()
    x2_sp = nc.dram_tensor("x2_sp", [TOK, D], F32, kind="Internal").ap()
    s1_sp = nc.dram_tensor("s1_sp", [TOK, D], BF16, kind="Internal").ap()
    ag1_in = nc.dram_tensor("ag1_in", [1, 128], F32, kind="Internal").ap()
    ag1_out = nc.dram_tensor("ag1_out", [8, 128], F32, kind="Internal").ap()
    ag2_in = nc.dram_tensor("ag2_in", [1, 8], F32, kind="Internal").ap()
    ag3_in = nc.dram_tensor("ag3_in", [1, 8], F32, kind="Internal").ap()
    ag2_out = nc.dram_tensor("ag2_out", [8, 8], F32, kind="Internal").ap()
    ag3_out = nc.dram_tensor("ag3_out", [8, 8], F32, kind="Internal").ap()

    dev_outs = {}
    for name, shape, dt in DEV_TAPS:
        dev_outs[name] = nc.dram_tensor("dev_" + name, shape, dt,
                                        kind="ExternalOutput").ap()

    ctx = ExitStack()
    with ctx:
        block = ctx.enter_context(nc.Block())

        def sb(name, shape, dt=F32):
            return ctx.enter_context(nc.sbuf_tensor(name, shape, dt))

        def ps(name, shape):
            return ctx.enter_context(nc.psum_tensor(name, shape, F32))

        tS = Sem(nc, ctx, "tS"); vS = Sem(nc, ctx, "vS")
        sS = Sem(nc, ctx, "sS"); gS = Sem(nc, ctx, "gS")
        qS = Sem(nc, ctx, "qS"); oS = Sem(nc, ctx, "oS")
        cS = Sem(nc, ctx, "cS"); gq = Sem(nc, ctx, "gq")

        A = sb("A", [128, 16 * D], F32)
        C = sb("C", [128, 16 * D], BF16)
        Dd = sb("Dd", [128, 16 * D], BF16)
        SCR = sb("SCR", [128, 4 * D], F32)
        tmp4 = sb("tmp4", [128, D], F32)
        tmp4b = sb("tmp4b", [128, D], F32)
        Abf = A[:, :].bitcast(BF16)
        Dbf32 = Dd[:, :].bitcast(F32)

        idf = sb("idf", [128, 128])
        idb = sb("idb", [128, 128], BF16)
        tri = sb("tri", [128, 128], BF16)
        cm8 = sb("cm8", [8, 1]); ce8 = sb("ce8", [8, 1])
        hm24 = sb("hm24", [24, 1])
        caus = sb("caus", [128, 128], BF16)
        io8 = sb("io8", [1, 8])
        io512 = sb("io512", [1, 512])
        iost = sb("iost", [128, 32])
        s256 = sb("s256", [16, 1])
        ones_f = sb("ones_f", [128, 1])
        ones_b = sb("ones_b", [128, 1], BF16)
        onesr_f = sb("onesr_f", [1, 128])
        lnv = sb("lnv", [1, D])
        xpw8 = sb("xpw8", [128, 8 * 16], BF16)
        inb_sb = sb("inb_sb", [128, 16])
        m1b_sb = sb("m1b_sb", [128, 16])
        xpb_sb = sb("xpb_sb", [16, 1])
        alog = sb("alog", [16, 1])
        svec = sb("svec", [16, 1])
        dvec = sb("dvec", [16, 1])
        stat_m = sb("stat_m", [128, 16])
        stat_i = sb("stat_i", [128, 16])
        AUX = sb("AUX", [128, 8192], F32)
        AUXbf = AUX[:, :].bitcast(BF16)
        u_fm = AUX[0:16, 0:2048]
        hs_fm = AUX[0:16, 2048:4096]
        dect = AUX[0:16, 4096:6144]
        dmat = AUX[0:16, 6144:7168]
        halo_x = AUX[0:24, 7168:8192]
        e16 = sb("e16", [16, 8])
        eag = sb("eag", [8, 136])
        wjn = sb("wjn", [8, 16])
        prod8 = sb("prod8", [8, 128])
        c128 = sb("c128", [128, 1])
        ccorr = sb("ccorr", [16, 8])
        s8x16 = sb("s8x16", [8, 16])
        srow = sb("srow", [1, 16])

        halo_n1 = sb("halo_n1", [24, D], BF16)
        hst = sb("hst", [24, 2])
        wka = sb("wka", [128, 2048], BF16)
        wkb = sb("wkb", [128, 2048], BF16)

        pb = [ps(f"pb{i}", [128, 512]) for i in range(4)]
        pbig = ps("pbig", [128, 2048])

        lnw_bc = SCR[:, 0:D]
        lnb_bc = SCR[:, D:2 * D]
        bias_bc = SCR[:, 2 * D:3 * D]
        w_bc = SCR[:, 0:4 * D]

        def t_blk(fn):
            block.tensor(fn)

        def v_blk(fn):
            block.vector(fn)

        def s_blk(fn):
            block.scalar(fn)

        def g_blk(fn):
            block.gpsimd(fn)

        def q_blk(fn):
            block.sync(fn)

        def load_row(src_dram_row, width=D):
            def qf(sync):
                qS.bar(sync)
                tS.wait(sync)
                qS.inc(sync.dma_start(out=lnv[0:1, 0:width],
                                      in_=src_dram_row), 16)
            q_blk(qf)

        def bcast_lnv(dst, width=D):
            nchunk = (width + 511) // 512

            def tf(tensor):
                vS.wait(tensor); qS.wait(tensor); gS.wait(tensor)
                for ci in range(nchunk):
                    w = min(512, width - ci * 512)
                    tS.inc(tensor.matmul(pb[ci % 4][:, 0:w], onesr_f[0:1, :],
                                         lnv[0:1, ci * 512:ci * 512 + w],
                                         start=True, stop=True))
            t_blk(tf)

            def vf(vector):
                tS.wait(vector)
                for ci in range(nchunk):
                    w = min(512, width - ci * 512)
                    vS.inc(vector.tensor_copy(dst[:, ci * 512:ci * 512 + w],
                                              pb[ci % 4][:, 0:w]))
            v_blk(vf)

        def load_bcast(src_dram_row, dst, width=D):
            load_row(src_dram_row, width)
            bcast_lnv(dst, width)

        def layernorm(src, dst, tiles, scratch, wb=None, mcol=None, icol=None,
                      halo_mask=None, width=D):
            def v1(vector):
                qS.wait(vector); gS.wait(vector); sS.wait(vector)
                gq.wait(vector)
                for t in tiles:
                    vS.inc(vector.reduce_sum(mcol[:, t:t + 1],
                                             src[:, t * width:(t + 1) * width],
                                             axis=AX.X))
                vS.wait(vector)
                for t in tiles:
                    vS.inc(vector.tensor_scalar(mcol[:, t:t + 1],
                                                mcol[:, t:t + 1],
                                                1.0 / width, None, ALU.mult))
            v_blk(v1)

            npart = src.shape[0]

            def s1(scalar):
                vS.wait(scalar)
                for n, t in enumerate(tiles):
                    sS.wait(scalar)
                    sS.inc(scalar.activation(
                        scratch[0:npart, (n % 2) * width:(n % 2 + 1) * width],
                        src[:, t * width:(t + 1) * width],
                        AF.Square, bias=mcol[:, t:t + 1], scale=-1.0,
                        accum_out=icol[:, t:t + 1]))
            s_blk(s1)

            def v2(vector):
                sS.wait(vector)
                for t in tiles:
                    vS.inc(vector.tensor_scalar(icol[:, t:t + 1],
                                                icol[:, t:t + 1],
                                                1.0 / width, EPS,
                                                ALU.mult, ALU.add))
                vS.wait(vector)
                for t in tiles:
                    vS.inc(vector.reciprocal(icol[:, t:t + 1], icol[:, t:t + 1]))
            v_blk(v2)

            def s2(scalar):
                vS.wait(scalar)
                for t in tiles:
                    sS.inc(scalar.activation(icol[:, t:t + 1], icol[:, t:t + 1],
                                             AF.Sqrt))
            s_blk(s2)

            if wb is None:
                def v3(vector):
                    sS.wait(vector)
                    for t in tiles:
                        vS.inc(vector.tensor_scalar(
                            dst[:, t * width:(t + 1) * width],
                            src[:, t * width:(t + 1) * width],
                            mcol[:, t:t + 1], icol[:, t:t + 1],
                            ALU.subtract, ALU.mult))
                v_blk(v3)
            else:
                wt, bt = wb

                def v3(vector):
                    sS.wait(vector)
                    for t in tiles:
                        vS.inc(vector.scalar_tensor_tensor(
                            dst[:, t * width:(t + 1) * width],
                            src[:, t * width:(t + 1) * width],
                            mcol[:, t:t + 1], wt[0:npart, 0:width],
                            ALU.subtract, ALU.mult))
                    vS.wait(vector)
                    for t in tiles:
                        vS.inc(vector.scalar_tensor_tensor(
                            dst[:, t * width:(t + 1) * width],
                            dst[:, t * width:(t + 1) * width],
                            icol[:, t:t + 1], bt[0:npart, 0:width],
                            ALU.mult, ALU.add))
                v_blk(v3)
            if halo_mask is not None:
                def v4(vector):
                    vS.wait(vector)
                    for t in tiles:
                        vS.inc(vector.tensor_scalar(
                            dst[:, t * width:(t + 1) * width],
                            dst[:, t * width:(t + 1) * width],
                            halo_mask, None, ALU.mult))
                v_blk(v4)

        pbig_bf = pbig[:, :].bitcast(BF16)

        def transpose_fm(src, dst, src_is_bf=False, ntile=16):
            ntok = ntile * 128
            ident = idb if src_is_bf else idf
            pout = pbig_bf if src_is_bf else pbig
            pairs = [(t, dc) for t in range(ntile) for dc in range(DC)]
            for gs in range(0, len(pairs), 16):
                grp = pairs[gs:gs + 16]

                def tf(tensor, grp=grp):
                    vS.wait(tensor); sS.wait(tensor)
                    for n, (t, dc) in enumerate(grp):
                        tS.inc(tensor.transpose(
                            pout[:, n * 128:(n + 1) * 128],
                            src[:, t * D + dc * 128:t * D + (dc + 1) * 128],
                            ident[:, :]))
                t_blk(tf)

                def ef(vector, grp=grp):
                    tS.wait(vector)
                    for n, (t, dc) in enumerate(grp):
                        vS.inc(vector.tensor_copy(
                            dst[:, dc * ntok + t * 128:dc * ntok + (t + 1) * 128],
                            pout[:, n * 128:(n + 1) * 128]))
                v_blk(ef)

        def dev_tap(name, src_ap):
            if name not in dev_outs:
                return

            def qf(sync):
                qS.bar(sync)
                vS.wait(sync); sS.wait(sync); tS.wait(sync); gS.wait(sync)
                gq.wait(sync)
                oS.inc(sync.dma_start(out=dev_outs[name], in_=src_ap), 16)
                oS.wait(sync)
            q_blk(qf)

            def bar(eng):
                oS.wait(eng)
            for blkfn in (t_blk, v_blk, s_blk, g_blk):
                blkfn(bar)

        def p0_sync(sync):
            qS.bar(sync)
            qS.inc(sync.dma_start(out=idf[:, :], in_=identf), 16)
            qS.inc(sync.dma_start(out=idb[:, :], in_=identb), 16)
            qS.inc(sync.dma_start(out=tri[:, :], in_=tri_bf), 16)
            qS.inc(sync.dma_start(out=cm8[:, :], in_=cmask), 16)
            qS.inc(sync.dma_start(out=ce8[:, :], in_=cexp), 16)
            qS.inc(sync.dma_start(out=hm24[:, :], in_=hmask), 16)
            qS.inc(sync.dma_start(out=caus[:, :], in_=causm), 16)
            qS.inc(sync.dma_start(out=io8[:, :], in_=iota8), 16)
            qS.inc(sync.dma_start(out=io512[:, :], in_=iota512), 16)
            qS.inc(sync.dma_start(out=iost[:, :], in_=iota_st), 16)
            qS.inc(sync.dma_start(out=s256[:, :], in_=seg256), 16)
            qS.inc(sync.dma_start(out=inb_sb[:, :], in_=in_b), 16)
            qS.inc(sync.dma_start(out=m1b_sb[:, :], in_=mlp_b1), 16)
            qS.inc(sync.dma_start(out=xpb_sb[:, :], in_=xp_b), 16)
            qS.inc(sync.dma_start(out=alog[:, :], in_=A_log), 16)
            qS.inc(sync.dma_start(out=dmat[:, :], in_=Dmat), 16)
            for bq in range(4):
                qS.inc(sync.dma_start(
                    out=A[:, bq * 4 * D:(bq + 1) * 4 * D].rearrange(
                        "p (t d) -> p t d", d=D),
                    in_=xs.rearrange("(t p) d -> p t d", p=128)[:, bq * 4:(bq + 1) * 4, :]), 16)
            qS.inc(sync.dma_start(out=halo_x[:, :], in_=xh), 16)
        q_blk(p0_sync)

        def p0_gp(gpsimd):
            gq.bar(gpsimd)
            gq.inc(gpsimd.dma_start(
                out=xpw8[:, :].rearrange("p (k n) -> p k n", k=8),
                in_=xp_w.rearrange("(k p) n -> p k n", p=128)), 16)
        g_blk(p0_gp)

        def p0_v(vector):
            vector.memset(ones_f[:, :], 1.0)
            vector.memset(ones_b[:, :], 1.0)
            vS.inc(vector.memset(onesr_f[:, :], 1.0))
        v_blk(p0_v)

        def p0_s(scalar):
            qS.wait(scalar)
            sS.inc(scalar.activation(svec[:, :], alog[:, :], AF.Exp))
            sS.wait(scalar)
            sS.inc(scalar.activation(dvec[:, :], svec[:, :], AF.Exp, scale=-1.0))
        s_blk(p0_s)

        def p0_v2(vector):
            sS.wait(vector)
            vS.inc(vector.memset(dect[:, :], 1.0))
            vS.wait(vector)
            vS.inc(vector.tensor_scalar(dect[:, :], dect[:, :], dvec[:, :],
                                        None, ALU.mult))
            vS.wait(vector)
            vS.inc(vector.memset(
                dect[:, :].rearrange("p (b t) -> p b t", b=8)[:, :, 0:1], 0.0))
        v_blk(p0_v2)

        def p0_t(tensor):
            sS.wait(tensor); vS.wait(tensor); qS.wait(tensor)
            tS.inc(tensor.transpose(pb[0][0:1, 0:16], svec[:, :],
                                    idf[0:16, 0:16]))
        t_blk(p0_t)

        def p0_v3(vector):
            tS.wait(vector)
            vS.inc(vector.tensor_copy(srow[0:1, :], pb[0][0:1, 0:16]))
        v_blk(p0_v3)

        layernorm(A, A, list(range(16)), Dd, wb=None,
                  mcol=stat_m, icol=stat_i)
        layernorm(halo_x, halo_x, [0], halo_n1, wb=None,
                  mcol=hst[:, 0:1], icol=hst[:, 1:2])
        dev_tap("xn_p", A[:, :])

        load_bcast(ln1_w[None, :], lnw_bc)
        load_bcast(ln1_b[None, :], lnb_bc)
        layernorm(A, C, list(range(16)), Dd, wb=(lnw_bc, lnb_bc),
                  mcol=stat_m, icol=stat_i)
        layernorm(halo_x, halo_n1, [0], Dd, wb=(lnw_bc, lnb_bc),
                  mcol=hst[:, 0:1], icol=hst[:, 1:2], halo_mask=hm24[:, :])
        dev_tap("xn1_p", C[:, :])

        for kk in range(4):
            load_row(convw[kk:kk + 1, :])
            bcast_lnv(w_bc[:, kk * D:(kk + 1) * D])

        def conv_round(get_src, kk):
            pre_v = vS.v
            pre_g = gS.v
            pre_gq = gq.v

            def vf(vector):
                gS.wait(vector, pre_g); vS.wait(vector, pre_v)
                gq.wait(vector, pre_gq)
                for t in range(10):
                    vS.inc(vector.tensor_tensor(
                        tmp4[:, :], get_src(t), w_bc[:, kk * D:(kk + 1) * D],
                        ALU.mult))
                    vS.wait(vector)
                    vS.inc(vector.tensor_tensor(
                        A[:, t * D:(t + 1) * D], A[:, t * D:(t + 1) * D],
                        tmp4[:, :], ALU.add))
                    vS.wait(vector)
            v_blk(vf)

            def gf(gpsimd):
                gS.wait(gpsimd, pre_g); vS.wait(gpsimd, pre_v)
                gq.wait(gpsimd, pre_gq)
                for t in range(10, 16):
                    gS.inc(gpsimd.tensor_tensor(
                        tmp4b[:, :], get_src(t), w_bc[:, kk * D:(kk + 1) * D],
                        ALU.mult))
                    gS.wait(gpsimd)
                    gS.inc(gpsimd.tensor_tensor(
                        A[:, t * D:(t + 1) * D], A[:, t * D:(t + 1) * D],
                        tmp4b[:, :], ALU.add))
                    gS.wait(gpsimd)
            g_blk(gf)

        load_row(convb[None, :])

        def cb_t(tensor):
            qS.wait(tensor); vS.wait(tensor)
            tS.inc(tensor.matmul(pbig[:, 0:512], onesr_f[0:1, :],
                                 lnv[0:1, 0:512], start=True, stop=True))
            tS.inc(tensor.matmul(pbig[:, 512:1024], onesr_f[0:1, :],
                                 lnv[0:1, 512:1024], start=True, stop=True))
        t_blk(cb_t)

        def cb_v(vector):
            tS.wait(vector)
            for t in range(16):
                vS.inc(vector.tensor_tensor(
                    A[:, t * D:(t + 1) * D], A[:, t * D:(t + 1) * D],
                    pbig[:, 0:1024], ALU.add))
        v_blk(cb_v)

        conv_round(lambda t: C[:, t * D:(t + 1) * D], 3)
        for shv in (1, 2, 3):
            kk = 3 - shv

            def conv_g(gpsimd, shv=shv):
                gq.bar(gpsimd)
                vS.wait(gpsimd); gS.wait(gpsimd); gq.wait(gpsimd)
                gq.inc(gpsimd.dma_start(
                    out=Dd[shv:128, :].rearrange("p (t d) -> p t d", d=D),
                    in_=C[0:128 - shv, :].rearrange("p (t d) -> p t d", d=D)), 16)
                gq.inc(gpsimd.dma_start(
                    out=Dd[0:shv, :].rearrange("p (b c d) -> p b c d",
                                               b=8, c=2)[:, :, 1, :],
                    in_=C[128 - shv:128, :].rearrange("p (b c d) -> p b c d",
                                                      b=8, c=2)[:, :, 0, :]), 16)
                for bi in range(8):
                    gq.inc(gpsimd.dma_start(
                        out=Dd[0:shv, bi * 2 * D:bi * 2 * D + D],
                        in_=halo_n1[3 * bi + 3 - shv:3 * bi + 3, :]), 16)
            g_blk(conv_g)
            conv_round(lambda t: Dd[:, t * D:(t + 1) * D], kk)

        dev_tap("xloc_p", A[:, :])
        if BUILD_UPTO < 2:
            return nc, dev_outs

        load_bcast(lns_w[None, :], lnw_bc)
        load_bcast(lns_b[None, :], lnb_bc)
        layernorm(A, C, list(range(16)), Dd, wb=(lnw_bc, lnb_bc),
                  mcol=stat_m, icol=stat_i)
        dev_tap("xn2_p", C[:, :])
        transpose_fm(C, Dd, src_is_bf=True)
        sxp_fm = Abf[:, 0:8 * TOK]
        z_fm = Abf[:, 8 * TOK:16 * TOK]

        def inw_load(gpsimd):
            gq.bar(gpsimd)
            vS.wait(gpsimd)
            gq.inc(gpsimd.dma_start(
                out=C[:, :].rearrange("p (kk n) -> p kk n", kk=8),
                in_=in_w.rearrange("(kk p) n -> p kk n", p=128)), 16)
        g_blk(inw_load)

        for fc in range(16):
            def inw_t(tensor, fc=fc):
                gq.wait(tensor); vS.wait(tensor)
                if fc > 0:
                    sS.wait(tensor)
                for tq in range(4):
                    for kk in range(8):
                        tS.inc(tensor.matmul(
                            pb[tq][:, :],
                            C[:, kk * 2048 + fc * 128:kk * 2048 + (fc + 1) * 128],
                            Dd[:, kk * 2048 + tq * 512:kk * 2048 + (tq + 1) * 512],
                            start=(kk == 0), stop=(kk == 7)))
            t_blk(inw_t)

            def inw_s(scalar, fc=fc):
                tS.wait(scalar)
                func = AF.Silu if fc < 8 else AF.Sigmoid
                dst = sxp_fm if fc < 8 else z_fm
                dc = fc % 8
                for tq in range(4):
                    sS.inc(scalar.activation(
                        dst[:, dc * 2048 + tq * 512:dc * 2048 + (tq + 1) * 512],
                        pb[tq][:, :], func, bias=inb_sb[:, fc:fc + 1]))
            s_blk(inw_s)

        dev_tap("sxp_fm", sxp_fm[:, 0:2048])
        dev_tap("z_fm", z_fm[:, 0:2048])

        def u_t(tensor):
            sS.wait(tensor)
            for tq in range(4):
                for kk in range(8):
                    tS.inc(tensor.matmul(
                        pbig[0:16, tq * 512:(tq + 1) * 512],
                        xpw8[:, kk * 16:(kk + 1) * 16],
                        sxp_fm[:, kk * 2048 + tq * 512:kk * 2048 + (tq + 1) * 512],
                        start=(kk == 0), stop=(kk == 7)))
        t_blk(u_t)

        def u_v(vector):
            tS.wait(vector); qS.wait(vector)
            vS.inc(vector.tensor_scalar(u_fm[:, :], pbig[0:16, :],
                                        xpb_sb[:, :], None, ALU.add))
        v_blk(u_v)
        dev_tap("u_fm", u_fm[:, :])

        def scan1(vector):
            vS.wait(vector)
            vS.inc(vector.tensor_tensor_scan(hs_fm[:, :], dect[:, :],
                                             u_fm[:, :], 0.0,
                                             ALU.mult, ALU.add))
            vS.wait(vector)
            vS.inc(vector.tensor_copy(
                e16[:, :],
                hs_fm[:, :].rearrange("p (b t) -> p b t", b=8)[:, :, 255]))
        v_blk(scan1)
        dev_tap("hs1", hs_fm[:, :])
        if BUILD_UPTO < 3:
            return nc, dev_outs

        def ag1_q(sync):
            qS.bar(sync)
            vS.wait(sync)
            dst = ag1_in[0:1, 0:128].rearrange("o (b n) -> (o n) b", b=8)
            qS.inc(sync.dma_start(out=dst, in_=e16[:, :],
                                  allow_slow_non_contiguous=True), 16)
        q_blk(ag1_q)

        def ag1_g(gpsimd):
            qS.wait(gpsimd)
            cS.inc(gpsimd.collective_compute(
                "AllGather", ALU.bypass, replica_groups=[list(range(8))],
                ins=[ag1_in], outs=[ag1_out]))
        g_blk(ag1_g)

        def ag1_back(sync):
            qS.bar(sync)
            cS.wait(sync)
            qS.inc(sync.dma_start(out=eag[:, 0:128], in_=ag1_out[:, 0:128]), 16)
        q_blk(ag1_back)

        def carry_t(tensor):
            vS.wait(tensor)
            tS.inc(tensor.matmul(pb[0][0:8, 0:16], onesr_f[0:1, 0:8],
                                 srow[0:1, 0:16], start=True, stop=True))
        t_blk(carry_t)

        def carry_v(vector):
            tS.wait(vector); qS.wait(vector)
            vS.inc(vector.tensor_scalar(s8x16[:, :], pb[0][0:8, 0:16],
                                        ce8[:, :], -256.0, ALU.mult, ALU.mult))
        v_blk(carry_v)

        def carry_s(scalar):
            vS.wait(scalar)
            sS.inc(scalar.activation(wjn[:, :], s8x16[:, :], AF.Exp))
        s_blk(carry_s)

        def carry_v2(vector):
            sS.wait(vector)
            vS.inc(vector.tensor_scalar(wjn[:, :], wjn[:, :], cm8[:, :],
                                        None, ALU.mult))
            vS.wait(vector)
            vS.inc(vector.tensor_tensor(
                prod8[:, :].rearrange("p (b n) -> p b n", b=8),
                eag[:, 0:128].rearrange("p (b n) -> p b n", b=8),
                wjn[:, :].unsqueeze(1).broadcast_to([8, 8, 16]),
                ALU.mult))
        v_blk(carry_v2)

        def carry_t2(tensor):
            vS.wait(tensor)
            tS.inc(tensor.matmul(pb[1][:, 0:1], prod8[:, :], ones_f[0:8, 0:1],
                                 start=True, stop=True))
        t_blk(carry_t2)

        def carry_v3(vector):
            tS.wait(vector)
            vS.inc(vector.tensor_copy(c128[:, :], pb[1][:, 0:1]))
        v_blk(carry_v3)

        def carry_g(gpsimd):
            gq.bar(gpsimd)
            vS.wait(gpsimd)
            for bi in range(8):
                gq.inc(gpsimd.dma_start(out=ccorr[:, bi:bi + 1],
                                        in_=c128[bi * 16:(bi + 1) * 16, :]), 16)
        g_blk(carry_g)

        def carry_v4(vector):
            gq.wait(vector)
            vS.inc(vector.tensor_scalar(ccorr[:, :], ccorr[:, :], dvec[:, :],
                                        None, ALU.mult))
            vS.wait(vector)
            vS.inc(vector.tensor_tensor(
                u_fm[:, :].rearrange("p (b t) -> p b t", b=8)[:, :, 0],
                u_fm[:, :].rearrange("p (b t) -> p b t", b=8)[:, :, 0],
                ccorr[:, :], ALU.add))
            vS.wait(vector)
            vS.inc(vector.tensor_tensor_scan(hs_fm[:, :], dect[:, :],
                                             u_fm[:, :], 0.0,
                                             ALU.mult, ALU.add))
        v_blk(carry_v4)
        dev_tap("hs2", hs_fm[:, :])
        if BUILD_UPTO < 4:
            return nc, dev_outs

        g1_fm = Dd
        for dc in range(DC):
            def dmm_t(tensor, dc=dc):
                vS.wait(tensor); qS.wait(tensor); sS.wait(tensor)
                for tq in range(4):
                    tS.inc(tensor.matmul(
                        pb[tq][:, :], dmat[:, dc * 128:(dc + 1) * 128],
                        hs_fm[:, tq * 512:(tq + 1) * 512],
                        start=True, stop=True))
            t_blk(dmm_t)

            def dmm_v(vector, dc=dc):
                tS.wait(vector)
                for tq in range(4):
                    vS.inc(vector.tensor_tensor(
                        g1_fm[:, dc * 2048 + tq * 512:dc * 2048 + (tq + 1) * 512],
                        pb[tq][:, :],
                        z_fm[:, dc * 2048 + tq * 512:dc * 2048 + (tq + 1) * 512],
                        ALU.mult))
            v_blk(dmm_v)
        dev_tap("g1_fm", g1_fm[:, 0:2048])

        load_row(out_b[None, :])
        bcast_lnv(bias_bc)
        def outw_g(gpsimd):
            gq.bar(gpsimd)
            vS.wait(gpsimd)
            gq.inc(gpsimd.dma_start(
                out=C[:, 0:8 * D].rearrange("p (kk n) -> p kk n", kk=8),
                in_=out_w.rearrange("(kk p) n -> p kk n", p=128)), 16)
        g_blk(outw_g)

        y_bf = Abf[:, 8 * TOK:16 * TOK]
        for t in range(16):
            def y_t(tensor, t=t):
                gq.wait(tensor); vS.wait(tensor)
                if t > 0:
                    vS.wait(tensor)
                for dh in range(2):
                    for kk in range(8):
                        tS.inc(tensor.matmul(
                            pb[2 * (t % 2) + dh][:, :],
                            g1_fm[:, kk * 2048 + t * 128:kk * 2048 + (t + 1) * 128],
                            C[:, kk * D + dh * 512:kk * D + (dh + 1) * 512],
                            start=(kk == 0), stop=(kk == 7)))
            t_blk(y_t)

            def y_v(vector, t=t):
                tS.wait(vector)
                for dh in range(2):
                    vS.inc(vector.tensor_tensor(
                        y_bf[:, t * D + dh * 512:t * D + (dh + 1) * 512],
                        pb[2 * (t % 2) + dh][:, :],
                        bias_bc[:, dh * 512:(dh + 1) * 512], ALU.add))
            v_blk(y_v)

        def yspill_q(sync):
            qS.bar(sync)
            vS.wait(sync)
            src = y_bf.rearrange("p (b c d) -> p b c d", b=8, c=2)
            dst = y_sp.rearrange("(c t b) d -> t b c d", c=2, b=8)
            qS.inc(sync.dma_start(out=dst, in_=src), 16)
        q_blk(yspill_q)
        dev_tap("y_p", y_bf[:, :])

        if BUILD_UPTO < 5:
            return nc, dev_outs


        def p5_load(sync):
            qS.bar(sync)
            vS.wait(sync)
            qS.wait(sync)
            for bq in range(4):
                qS.inc(sync.dma_start(
                    out=A[:, bq * 4 * D:(bq + 1) * 4 * D].rearrange(
                        "p (t d) -> p t d", d=D),
                    in_=xb.rearrange("(t p) d -> p t d", p=128)[:, bq * 4:(bq + 1) * 4, :]), 16)
        q_blk(p5_load)

        layernorm(A, A, list(range(16)), Dd, wb=None, mcol=stat_m, icol=stat_i)
        dev_tap("xn_c", A[:, :])

        def p5_hi(vector):
            vS.wait(vector); gq.wait(vector)
            for t in range(16):
                vS.inc(vector.tensor_copy(Dd[:, t * D:(t + 1) * D],
                                          A[:, t * D:(t + 1) * D]))
        v_blk(p5_hi)
        transpose_fm(Dd, C, src_is_bf=True)

        wg_f = sb("wg_f", [128, 64])
        wg_hi = sb("wg_hi", [128, 64], BF16)
        wg_lo = sb("wg_lo", [128, 64], BF16)
        SCRbf = SCR[:, :].bitcast(BF16)
        lob = SCRbf[:, 0:2048]
        lg_sb = sb("lg_sb", [128, 128])
        io8bc = sb("io8bc", [128, 8])
        gbbc = sb("gbbc", [128, 8])
        m8 = sb("m8", [128, 1]); negm = sb("negm", [128, 1])
        sume = sb("sume", [128, 1]); gate8 = sb("gate8", [128, 16])
        exp8 = sb("exp8", [128, 8])
        oh1 = sb("oh1", [128, 8]); mi8 = sb("mi8", [128, 8])
        idx8 = sb("idx8", [128, 16])
        ohb_sb = sb("ohb_sb", [128, 16 * 8], BF16)
        cs_sb = sb("cs_sb", [8, 17])
        pst_sb = sb("pst_sb", [16, 8])
        cnt8 = sb("cnt8", [8, 1])
        cAG = sb("cAG", [8, 8])
        pgrow = sb("pgrow", [1, 8])
        pgbc = sb("pgbc", [128, 8])
        gb_row = sb("gb_row", [1, 8])
        tmp8 = sb("tmp8", [128, 8])
        psrow = sb("psrow", [1, 128])
        cntrow = psrow[0:1, 32:40]
        brow = psrow[0:1, 16:27]
        base_bc = AUX[:, 0:8]
        erow_f = psrow[0:1, 48:64]
        erow_i = sb("erow_i", [1, 16], mybir.dt.int32)
        prefbc_all = sb("prefbc_all", [128, 128])
        cum_sb = sb("cum_sb", [128, 128])
        posc = sb("posc", [128, 1])
        keepc = sb("keepc", [128, 1])
        kg_sb = sb("kg_sb", [128, 16])
        kg2_sb = sb("kg2_sb", [128, 16])
        slot_sb = sb("slot_sb", [128, 16])

        def gb_q(sync):
            qS.bar(sync)
            qS.inc(sync.dma_start(out=gb_row[:, :], in_=gate_b), 16)
        q_blk(gb_q)

        def io8_t(tensor):
            vS.wait(tensor); qS.wait(tensor)
            tS.inc(tensor.matmul(pb[0][:, 0:8], onesr_f[0:1, :], io8[0:1, :],
                                 start=True, stop=True))
            tS.inc(tensor.matmul(pb[1][:, 0:8], onesr_f[0:1, :],
                                 gb_row[0:1, :], start=True, stop=True))
        t_blk(io8_t)

        def io8_v(vector):
            tS.wait(vector)
            vS.inc(vector.tensor_copy(io8bc[:, :], pb[0][:, 0:8]))
            vS.inc(vector.tensor_copy(gbbc[:, :], pb[1][:, 0:8]))
        v_blk(io8_v)

        def router_logits(wgT_dram, hi_fm):
            def wg_ld(sync):
                qS.bar(sync)
                tS.wait(sync)
                qS.inc(sync.dma_start(
                    out=wg_f[:, :].rearrange("p (k n) -> p k n", k=8),
                    in_=wgT_dram.rearrange("(k p) n -> p k n", p=128)), 16)
            q_blk(wg_ld)

            def wg_split(vector):
                qS.wait(vector)
                vS.inc(vector.tensor_copy(wg_hi[:, :], wg_f[:, :]))
                vS.wait(vector)
                vS.inc(vector.tensor_tensor(wg_lo[:, :], wg_f[:, :],
                                            wg_hi[:, :], ALU.subtract))
            v_blk(wg_split)

            def lo_v(vector):
                vS.wait(vector)
                for t in range(16):
                    vS.inc(vector.tensor_tensor(
                        Dd[:, t * D:(t + 1) * D], A[:, t * D:(t + 1) * D],
                        Dd[:, t * D:(t + 1) * D], ALU.subtract))
            v_blk(lo_v)

            for kk in range(DC):
                def lo_tr(tensor, kk=kk):
                    vS.wait(tensor)
                    for n in range(16):
                        tS.inc(tensor.transpose(
                            pbig_bf[:, 2048 + n * 128:2048 + (n + 1) * 128],
                            Dd[:, n * D + kk * 128:n * D + (kk + 1) * 128],
                            idb[:, :]))
                t_blk(lo_tr)

                def lo_ev(vector, kk=kk):
                    tS.wait(vector)
                    for n in range(16):
                        vS.inc(vector.tensor_copy(
                            lob[:, n * 128:(n + 1) * 128],
                            pbig_bf[:, 2048 + n * 128:2048 + (n + 1) * 128]))
                v_blk(lo_ev)

                def lg_t(tensor, kk=kk):
                    vS.wait(tensor)
                    for n in range(16):
                        st = (kk == 0 and n == 0)
                        tS.inc(tensor.matmul(
                            pbig[:, n * 8:(n + 1) * 8],
                            hi_fm[:, kk * 2048 + n * 128:kk * 2048 + (n + 1) * 128],
                            wg_hi[:, kk * 8:(kk + 1) * 8],
                            start=st, stop=False))
                        tS.inc(tensor.matmul(
                            pbig[:, n * 8:(n + 1) * 8],
                            lob[:, n * 128:(n + 1) * 128],
                            wg_hi[:, kk * 8:(kk + 1) * 8],
                            start=False, stop=False))
                        last = (kk == DC - 1 and n == 15)
                        tS.inc(tensor.matmul(
                            pbig[:, n * 8:(n + 1) * 8],
                            hi_fm[:, kk * 2048 + n * 128:kk * 2048 + (n + 1) * 128],
                            wg_lo[:, kk * 8:(kk + 1) * 8],
                            start=False, stop=last))
                t_blk(lg_t)

            def lg_ev(vector):
                tS.wait(vector)
                vS.inc(vector.tensor_copy(lg_sb[:, :], pbig[:, 0:128]))
            v_blk(lg_ev)

        def router_chain(use_gb, cap, kgdst, want_slot, agin, agout):
            for t in range(16):
                def rv(vector, t=t):
                    vS.wait(vector); sS.wait(vector)
                    lg = lg_sb[:, t * 8:(t + 1) * 8]
                    if use_gb:
                        vS.inc(vector.tensor_tensor(lg, lg, gbbc[:, :],
                                                    ALU.add))
                        vS.wait(vector)
                    vS.inc(vector.reduce_max(m8[:, :], lg, axis=AX.X))
                    vS.wait(vector)
                    vS.inc(vector.tensor_scalar(negm[:, :], m8[:, :], -1.0,
                                                None, ALU.mult))
                v_blk(rv)

                def rs(scalar, t=t):
                    vS.wait(scalar)
                    sS.inc(scalar.activation(exp8[:, :],
                                             lg_sb[:, t * 8:(t + 1) * 8],
                                             AF.Exp, bias=negm[:, :],
                                             accum_out=sume[:, :]))
                s_blk(rs)

                def rv2(vector, t=t):
                    sS.wait(vector)
                    lg = lg_sb[:, t * 8:(t + 1) * 8]
                    vS.inc(vector.reciprocal(gate8[:, t:t + 1], sume[:, :]))
                    vS.inc(vector.tensor_scalar(oh1[:, :], lg, m8[:, :],
                                                None, ALU.is_ge))
                    vS.wait(vector)
                    vS.inc(vector.tensor_scalar(oh1[:, :], oh1[:, :],
                                                -99.0, 99.0,
                                                ALU.mult, ALU.add))
                    vS.wait(vector)
                    vS.inc(vector.tensor_tensor(mi8[:, :], io8bc[:, :],
                                                oh1[:, :], ALU.add))
                    vS.wait(vector)
                    vS.inc(vector.tensor_reduce(idx8[:, t:t + 1], mi8[:, :],
                                                axis=AX.X, op=ALU.min))
                    vS.wait(vector)
                    vS.inc(vector.tensor_scalar(oh1[:, :], io8bc[:, :],
                                                idx8[:, t:t + 1], None,
                                                ALU.is_equal))
                    vS.wait(vector)
                    vS.inc(vector.tensor_copy(ohb_sb[:, t * 8:(t + 1) * 8],
                                              oh1[:, :]))
                v_blk(rv2)

            def cum_t(tensor):
                vS.wait(tensor)
                for t in range(16):
                    tS.inc(tensor.matmul(
                        pbig[:, 512 + t * 8:512 + (t + 1) * 8],
                        tri[:, :], ohb_sb[:, t * 8:(t + 1) * 8],
                        start=(t == 0), stop=(t == 15)))
                for t in range(16):
                    tS.inc(tensor.matmul(
                        pb[0][0:8, t:t + 1], ohb_sb[:, t * 8:(t + 1) * 8],
                        ones_b[:, :], start=(t == 0), stop=(t == 15)))
            t_blk(cum_t)

            def cum_v(vector):
                tS.wait(vector)
                vS.inc(vector.tensor_copy(cum_sb[:, :], pbig[:, 512:640]))
                vS.inc(vector.tensor_copy(cs_sb[:, 0:16], pb[0][0:8, 0:16]))
                vS.wait(vector)
                vS.inc(vector.memset(cs_sb[:, 16:17], 0.0))
                for t in range(16):
                    vS.wait(vector)
                    vS.inc(vector.tensor_tensor(tmp8[0:8, 0:1],
                                                cs_sb[:, 16:17],
                                                cs_sb[:, t:t + 1], ALU.add))
                    vS.wait(vector)
                    vS.inc(vector.tensor_copy(cs_sb[:, t:t + 1],
                                              cs_sb[:, 16:17]))
                    vS.wait(vector)
                    vS.inc(vector.tensor_copy(cs_sb[:, 16:17], tmp8[0:8, 0:1]))
            v_blk(cum_v)

            def pst_t(tensor):
                vS.wait(tensor)
                tS.inc(tensor.transpose(pb[1][0:16, 0:8], cs_sb[:, 0:16],
                                        idf[0:8, 0:8]))
            t_blk(pst_t)

            def pst_v(vector):
                tS.wait(vector)
                vS.inc(vector.tensor_copy(pst_sb[:, :], pb[1][0:16, 0:8]))
                vS.inc(vector.tensor_copy(cnt8[:, :], cs_sb[:, 16:17]))
            v_blk(pst_v)

            def psr_g(gpsimd):
                gq.bar(gpsimd)
                vS.wait(gpsimd)
                gq.inc(gpsimd.dma_start(
                    out=psrow[0:1, :].rearrange("o (t e) -> o t e", t=16),
                    in_=pst_sb[:, :]), 16)
            g_blk(psr_g)

            def psr_t(tensor):
                gq.wait(tensor); vS.wait(tensor)
                tS.inc(tensor.matmul(pb[3][:, 0:128], onesr_f[0:1, :],
                                     psrow[0:1, :], start=True, stop=True))
            t_blk(psr_t)

            def psr_v(vector):
                tS.wait(vector)
                vS.inc(vector.tensor_copy(prefbc_all[:, :], pb[3][:, 0:128]))
            v_blk(psr_v)

            if want_slot:
                def sb_t(tensor):
                    vS.wait(tensor)
                    tS.inc(tensor.transpose(pb[2][0:1, 0:8], cnt8[:, :],
                                            idf[0:8, 0:8]))
                t_blk(sb_t)

                def sb_v(vector):
                    thr16 = psrow[0:1, 80:96]
                    ge16 = psrow[0:1, 64:80]
                    tS.wait(vector)
                    vS.inc(vector.tensor_copy(cntrow[:, :], pb[2][0:1, 0:8]))
                    vS.wait(vector)
                    vS.inc(vector.memset(brow[0:1, 0:1], 0.0))
                    vS.wait(vector)
                    vS.inc(vector.tensor_scalar(
                        thr16[:, :], io512[0:1, 0:16], 256.0, 1.0,
                        ALU.mult, ALU.add))
                    for e in range(7):
                        vS.wait(vector)
                        vS.inc(vector.tensor_tensor(
                            brow[0:1, 9:10], brow[0:1, e:e + 1],
                            cntrow[0:1, e:e + 1], ALU.add))
                        vS.wait(vector)
                        vS.inc(vector.tensor_tensor(
                            ge16[:, :],
                            brow[0:1, 9:10].broadcast_to([1, 16]),
                            thr16[:, :], ALU.is_ge))
                        vS.wait(vector)
                        vS.inc(vector.tensor_reduce(
                            brow[0:1, 10:11], ge16[:, :],
                            axis=AX.X, op=ALU.add))
                        vS.wait(vector)
                        vS.inc(vector.tensor_scalar(
                            brow[0:1, e + 1:e + 2], brow[0:1, 10:11],
                            256.0, None, ALU.mult))
                v_blk(sb_v)

                def sb_t2(tensor):
                    vS.wait(tensor)
                    tS.inc(tensor.matmul(pb[2][:, 0:8], onesr_f[0:1, :],
                                         brow[0:1, 0:8], start=True, stop=True))
                    tS.inc(tensor.matmul(pb[3][0:16, 0:8], onesr_f[0:1, 0:16],
                                         brow[0:1, 0:8], start=True, stop=True))
                t_blk(sb_t2)

                def sb_v2(vector):
                    tS.wait(vector)
                    vS.inc(vector.tensor_copy(base_bc[:, :], pb[2][:, 0:8]))
                    vS.wait(vector)
                    vS.inc(vector.tensor_scalar(ccorr[:, 0:8], pb[3][0:16, 0:8],
                                                s256[:, :], None, ALU.is_le))
                    vS.wait(vector)
                    vS.inc(vector.reduce_sum(e16[:, 0:1], ccorr[:, 0:8],
                                             axis=AX.X))
                    vS.wait(vector)
                    vS.inc(vector.tensor_scalar(e16[:, 0:1], e16[:, 0:1],
                                                1.0, -1.0, ALU.mult, ALU.add))
                v_blk(sb_v2)

                def sb_t3(tensor):
                    vS.wait(tensor)
                    tS.inc(tensor.transpose(pb[2][0:1, 0:16], e16[:, 0:1],
                                            idf[0:16, 0:16]))
                t_blk(sb_t3)

                def sb_v3(vector):
                    tS.wait(vector)
                    vS.inc(vector.tensor_copy(erow_f[:, :], pb[2][0:1, 0:16]))
                    vS.wait(vector)
                    vS.inc(vector.tensor_copy(erow_i[:, :], erow_f[:, :]))
                v_blk(sb_v3)

            def ag_q(sync):
                qS.bar(sync)
                vS.wait(sync)
                qS.inc(sync.dma_start(out=agin[0:1, 0:8], in_=cnt8[:, :],
                                      allow_slow_non_contiguous=True), 16)
            q_blk(ag_q)

            def ag_g(gpsimd):
                qS.wait(gpsimd)
                cS.inc(gpsimd.collective_compute(
                    "AllGather", ALU.bypass, replica_groups=[list(range(8))],
                    ins=[agin], outs=[agout]))
            g_blk(ag_g)

            def ag_b(sync):
                qS.bar(sync)
                cS.wait(sync)
                qS.inc(sync.dma_start(out=cAG[:, :], in_=agout), 16)
            q_blk(ag_b)

            def pg_t(tensor):
                qS.wait(tensor); vS.wait(tensor)
                tS.inc(tensor.matmul(pb[2][0:1, 0:8], cm8[:, :], cAG[:, :],
                                     start=True, stop=True))
            t_blk(pg_t)

            def pg_v(vector):
                tS.wait(vector)
                vS.inc(vector.tensor_copy(pgrow[:, :], pb[2][0:1, 0:8]))
            v_blk(pg_v)

            def pg_t2(tensor):
                vS.wait(tensor)
                tS.inc(tensor.matmul(pb[2][:, 0:8], onesr_f[0:1, :],
                                     pgrow[0:1, :], start=True, stop=True))
            t_blk(pg_t2)

            def pg_v2(vector):
                tS.wait(vector)
                vS.inc(vector.tensor_copy(pgbc[:, :], pb[2][:, 0:8]))
            v_blk(pg_v2)

            for t in range(16):
                def pc_v(vector, t=t):
                    vS.wait(vector)
                    vS.inc(vector.tensor_tensor(
                        tmp8[:, :], cum_sb[:, t * 8:(t + 1) * 8],
                        prefbc_all[:, t * 8:(t + 1) * 8], ALU.add))
                    vS.wait(vector)
                    if want_slot:
                        vS.inc(vector.tensor_tensor(
                            mi8[:, :], ohb_sb[:, t * 8:(t + 1) * 8],
                            tmp8[:, :], ALU.mult))
                        vS.wait(vector)
                        vS.inc(vector.reduce_sum(posc[:, :], mi8[:, :],
                                                 axis=AX.X))
                        vS.wait(vector)
                        vS.inc(vector.tensor_scalar(posc[:, :], posc[:, :],
                                                    1.0, -1.0, ALU.mult,
                                                    ALU.add))
                        vS.wait(vector)
                        vS.inc(vector.tensor_tensor(
                            mi8[:, :], ohb_sb[:, t * 8:(t + 1) * 8],
                            base_bc[:, :], ALU.mult))
                        vS.wait(vector)
                        vS.inc(vector.reduce_sum(keepc[:, :], mi8[:, :],
                                                 axis=AX.X))
                        vS.wait(vector)
                        vS.inc(vector.tensor_tensor(slot_sb[:, t:t + 1],
                                                    posc[:, :], keepc[:, :],
                                                    ALU.add))
                        vS.wait(vector)
                    vS.inc(vector.tensor_tensor(tmp8[:, :], tmp8[:, :],
                                                pgbc[:, :], ALU.add))
                    vS.wait(vector)
                    vS.inc(vector.tensor_tensor(
                        mi8[:, :], ohb_sb[:, t * 8:(t + 1) * 8],
                        tmp8[:, :], ALU.mult))
                    vS.wait(vector)
                    vS.inc(vector.reduce_sum(posc[:, :], mi8[:, :], axis=AX.X))
                    vS.wait(vector)
                    vS.inc(vector.tensor_scalar(posc[:, :], posc[:, :],
                                                1.0, -1.0, ALU.mult, ALU.add))
                    vS.wait(vector)
                    vS.inc(vector.tensor_scalar(keepc[:, :], posc[:, :],
                                                float(cap), None, ALU.is_lt))
                    vS.wait(vector)
                    vS.inc(vector.tensor_tensor(kgdst[:, t:t + 1],
                                                keepc[:, :],
                                                gate8[:, t:t + 1], ALU.mult))
                v_blk(pc_v)

        router_logits(Wg, C)
        dev_tap("lg0", lg_sb[:, :])
        router_chain(False, CAP0, kg_sb, False, ag2_in, ag2_out)
        dev_tap("kg0", kg_sb[:, :])

        g_sp = nc.dram_tensor("g_sp", [TOK, D], BF16, kind="Internal").ap()
        load_bcast(be[None, :], bias_bc)

        def we_load(gpsimd):
            gq.bar(gpsimd)
            vS.wait(gpsimd)
            gq.inc(gpsimd.dma_start(
                out=Dd[:, 0:8 * D].rearrange("p (kk n) -> p kk n", kk=8),
                in_=We.rearrange("(kk p) n -> p kk n", p=128)), 16)
        g_blk(we_load)

        gstage = [SCRbf[:, 0:2048], SCRbf[:, 6144:8192]]
        for t in range(16):
            def we_t(tensor, t=t):
                gq.wait(tensor); vS.wait(tensor)
                if t > 1:
                    qS.wait(tensor)
                for dh in range(2):
                    for kk in range(8):
                        tS.inc(tensor.matmul(
                            pb[2 * (t % 2) + dh][:, :],
                            C[:, kk * 2048 + t * 128:kk * 2048 + (t + 1) * 128],
                            Dd[:, kk * D + dh * 512:kk * D + (dh + 1) * 512],
                            start=(kk == 0), stop=(kk == 7)))
            t_blk(we_t)

            def we_v(vector, t=t):
                tS.wait(vector); sS.wait(vector)
                for dh in range(2):
                    vS.inc(vector.tensor_tensor(
                        tmp4[:, dh * 512:(dh + 1) * 512],
                        pb[2 * (t % 2) + dh][:, :],
                        bias_bc[:, dh * 512:(dh + 1) * 512], ALU.add))
            v_blk(we_v)

            def we_s(scalar, t=t):
                vS.wait(scalar)
                sS.inc(scalar.activation(gstage[t % 2][:, 0:D],
                                         tmp4[:, :], AF.Silu,
                                         scale=kg_sb[:, t:t + 1]))
            s_blk(we_s)

            def we_q(sync, t=t):
                qS.bar(sync)
                sS.wait(sync)
                qS.inc(sync.dma_start(
                    out=g_sp[t * 128:(t + 1) * 128, :],
                    in_=gstage[t % 2][:, 0:D]), 16)
            q_blk(we_q)
        dev_tap("xn_c2", A[:, :])

        if BUILD_UPTO < 6:
            return nc, dev_outs

        load_bcast(ln1_w[None, :], lnw_bc)
        load_bcast(ln1_b[None, :], lnb_bc)
        layernorm(A, C, list(range(16)), Dd, wb=(lnw_bc, lnb_bc),
                  mcol=stat_m, icol=stat_i)
        def ch_v(vector):
            vS.wait(vector)
            vS.inc(vector.memset(halo_n1[:, :], 0.0))
        v_blk(ch_v)

        for kk in range(4):
            load_row(convw[kk:kk + 1, :])
            bcast_lnv(w_bc[:, kk * D:(kk + 1) * D])
        load_row(convb[None, :])

        def cb2_t(tensor):
            qS.wait(tensor); vS.wait(tensor)
            tS.inc(tensor.matmul(pbig[:, 0:512], onesr_f[0:1, :],
                                 lnv[0:1, 0:512], start=True, stop=True))
            tS.inc(tensor.matmul(pbig[:, 512:1024], onesr_f[0:1, :],
                                 lnv[0:1, 512:1024], start=True, stop=True))
        t_blk(cb2_t)

        def cb2_v(vector):
            tS.wait(vector)
            for t in range(16):
                vS.inc(vector.tensor_tensor(
                    A[:, t * D:(t + 1) * D], A[:, t * D:(t + 1) * D],
                    pbig[:, 0:1024], ALU.add))
        v_blk(cb2_v)

        conv_round(lambda t: C[:, t * D:(t + 1) * D], 3)
        for shv in (1, 2, 3):
            kk = 3 - shv

            def conv_g2(gpsimd, shv=shv):
                gq.bar(gpsimd)
                vS.wait(gpsimd); gS.wait(gpsimd); gq.wait(gpsimd)
                gq.inc(gpsimd.dma_start(
                    out=Dd[shv:128, :].rearrange("p (t d) -> p t d", d=D),
                    in_=C[0:128 - shv, :].rearrange("p (t d) -> p t d", d=D)), 16)
                gq.inc(gpsimd.dma_start(
                    out=Dd[0:shv, D:16 * D].rearrange("p (t d) -> p t d", d=D),
                    in_=C[128 - shv:128, 0:15 * D].rearrange(
                        "p (t d) -> p t d", d=D)), 16)
                gq.inc(gpsimd.dma_start(
                    out=Dd[0:shv, 0:D],
                    in_=halo_n1[3 - shv:3, :]), 16)
            g_blk(conv_g2)
            conv_round(lambda t: Dd[:, t * D:(t + 1) * D], kk)

        dev_tap("xloc_c", A[:, :])

        def yp_q(sync):
            qS.bar(sync)
            vS.wait(sync); gS.wait(sync)
            for bq in range(2):
                qS.inc(sync.dma_start(
                    out=C[:, bq * 8 * D:(bq + 1) * 8 * D].rearrange(
                        "p (t d) -> p t d", d=D),
                    in_=y_sp.rearrange("(t p) d -> p t d", p=128)[:, bq * 8:(bq + 1) * 8, :]), 16)
        q_blk(yp_q)

        def x1_v(vector):
            qS.wait(vector); vS.wait(vector)
            for t in range(16):
                vS.inc(vector.tensor_tensor(
                    A[:, t * D:(t + 1) * D], A[:, t * D:(t + 1) * D],
                    C[:, t * D:(t + 1) * D], ALU.add))
        v_blk(x1_v)
        dev_tap("x1_c", A[:, :])

        load_bcast(ln2_w[None, :], lnw_bc)
        load_bcast(ln2_b[None, :], lnb_bc)
        layernorm(A, C, list(range(16)), Dd, wb=(lnw_bc, lnb_bc),
                  mcol=stat_m, icol=stat_i)

        def w1_load(gpsimd):
            gq.bar(gpsimd)
            vS.wait(gpsimd)
            gq.inc(gpsimd.dma_start(
                out=Dd[:, :].rearrange("p (kk n) -> p kk n", kk=8),
                in_=mlp_w1.rearrange("(kk p) n -> p kk n", p=128)), 16)
        g_blk(w1_load)
        load_row(mlp_b2[None, :])
        bcast_lnv(bias_bc)

        ln2g = AUXbf[:, 0:4096]
        h1g = AUXbf[:, 4096:4096 + 8192]
        w2psum = [pb[0][:, :], pb[1][:, :], pb[2][:, :], pb[3][:, :],
                  pbig[:, 0:512], pbig[:, 512:1024],
                  pbig[:, 1024:1536], pbig[:, 1536:2048]]
        for tg in range(4):
            for rnd in range(2):
                def tg_tr(tensor, tg=tg, rnd=rnd):
                    vS.wait(tensor); sS.wait(tensor); gq.wait(tensor)
                    for n in range(16):
                        ti, dc = divmod(rnd * 16 + n, 8)
                        t = tg * 4 + ti
                        tS.inc(tensor.transpose(
                            pbig_bf[:, 2048 + n * 128:2048 + (n + 1) * 128],
                            C[:, t * D + dc * 128:t * D + (dc + 1) * 128],
                            idb[:, :]))
                t_blk(tg_tr)

                def tg_ev(vector, tg=tg, rnd=rnd):
                    tS.wait(vector)
                    for n in range(16):
                        ti, dc = divmod(rnd * 16 + n, 8)
                        vS.inc(vector.tensor_copy(
                            ln2g[:, dc * 512 + ti * 128:dc * 512 + (ti + 1) * 128],
                            pbig_bf[:, 2048 + n * 128:2048 + (n + 1) * 128]))
                v_blk(tg_ev)

            for fc in range(16):
                def h1_t(tensor, tg=tg, fc=fc):
                    vS.wait(tensor)
                    if fc > 0:
                        sS.wait(tensor)
                    for kk in range(8):
                        tS.inc(tensor.matmul(
                            pb[fc % 2][:, :],
                            Dd[:, kk * 2048 + fc * 128:kk * 2048 + (fc + 1) * 128],
                            ln2g[:, kk * 512:(kk + 1) * 512],
                            start=(kk == 0), stop=(kk == 7)))
                t_blk(h1_t)

                def h1_s(scalar, tg=tg, fc=fc):
                    tS.wait(scalar)
                    sS.inc(scalar.activation(
                        h1g[:, fc * 512:(fc + 1) * 512],
                        pb[fc % 2][:, :], AF.Gelu,
                        bias=m1b_sb[:, fc:fc + 1]))
                s_blk(h1_s)

            for kk in range(16):
                def w2_ld(gpsimd, tg=tg, kk=kk):
                    gq.bar(gpsimd)
                    sS.wait(gpsimd)
                    if kk > 1 or tg > 0:
                        tS.wait(gpsimd)
                    gq.inc(gpsimd.dma_start(
                        out=(wka if kk % 2 == 0 else wkb)[:, 0:D],
                        in_=mlp_w2[kk * 128:(kk + 1) * 128, :]), 16)
                g_blk(w2_ld)

                def w2_t(tensor, tg=tg, kk=kk):
                    gq.wait(tensor); sS.wait(tensor)
                    if kk == 0:
                        vS.wait(tensor)
                    wt = (wka if kk % 2 == 0 else wkb)
                    for ti in range(4):
                        for dh in range(2):
                            tS.inc(tensor.matmul(
                                w2psum[ti * 2 + dh],
                                h1g[:, kk * 512 + ti * 128:kk * 512 + (ti + 1) * 128],
                                wt[:, dh * 512:(dh + 1) * 512],
                                start=(kk == 0), stop=(kk == 15)))
                t_blk(w2_t)

            def w2_v(vector, tg=tg):
                tS.wait(vector)
                for ti in range(4):
                    t = tg * 4 + ti
                    for dh in range(2):
                        vS.inc(vector.tensor_tensor(
                            tmp4[:, dh * 512:(dh + 1) * 512],
                            w2psum[ti * 2 + dh],
                            bias_bc[:, dh * 512:(dh + 1) * 512], ALU.add))
                    vS.wait(vector)
                    vS.inc(vector.tensor_tensor(
                        A[:, t * D:(t + 1) * D], A[:, t * D:(t + 1) * D],
                        tmp4[:, :], ALU.add))
                    vS.wait(vector)
            v_blk(w2_v)
        dev_tap("smix_c", A[:, :])
        if BUILD_UPTO < 7:
            return nc, dev_outs

        for t in range(16):
            def x2_q(sync, t=t):
                qS.bar(sync)
                vS.wait(sync); sS.wait(sync)
                qS.inc(sync.dma_start(out=tmp4[:, :],
                                      in_=xb[t * 128:(t + 1) * 128, :]), 16)
                qS.inc(sync.dma_start(out=wka[:, 0:D],
                                      in_=g_sp[t * 128:(t + 1) * 128, :]), 16)
            q_blk(x2_q)

            def x2_v(vector, t=t):
                qS.wait(vector); vS.wait(vector)
                vS.inc(vector.tensor_tensor(tmp4b[:, :],
                                            A[:, t * D:(t + 1) * D],
                                            wka[:, 0:D], ALU.mult))
                vS.wait(vector)
                vS.inc(vector.tensor_tensor(A[:, t * D:(t + 1) * D],
                                            tmp4[:, :], tmp4b[:, :], ALU.add))
            v_blk(x2_v)

            def x2_s(sync, t=t):
                qS.bar(sync)
                vS.wait(sync)
                qS.inc(sync.dma_start(out=x2_sp[t * 128:(t + 1) * 128, :],
                                      in_=A[:, t * D:(t + 1) * D]), 16)
            q_blk(x2_s)
        dev_tap("x2_c", A[:, :])

        layernorm(A, A, list(range(16)), Dd, wb=None, mcol=stat_m,
                  icol=stat_i)
        dev_tap("xn1_L1", A[:, :])
        if BUILD_UPTO < 8:
            return nc, dev_outs


        xn1tok_sp = nc.dram_tensor("xn1tok_sp", [TOK, D], BF16,
                                   kind="Internal").ap()
        Qfm = Abf[:, 0:16384]
        Kfm = Abf[:, 16384:32768]
        Vtm = AUXbf[:, 0:16384]
        Ebuf = [SCRbf[:, 0:2048], SCRbf[:, 2048:4096]]
        cum_bf = cum_sb[:, :].bitcast(BF16)
        vaug = [cum_bf[:, 0:65], cum_bf[:, 128:193]]
        lst = SCR[64:65, 2048:4096]
        au_st = SCR[0:64, 2048:4096].bitcast(BF16)
        od_st = SCR[0:64, 3072:4096].bitcast(BF16)

        ones65 = prefbc_all
        nb8 = stat_m[:, 0:1]

        def p8_v0(vector):
            vS.wait(vector)
            vS.inc(vector.memset(nb8, -8.0))

            for t in range(16):
                vS.inc(vector.tensor_copy(Dd[:, t * D:(t + 1) * D],
                                          A[:, t * D:(t + 1) * D]))
        v_blk(p8_v0)

        def p8_spill(sync):
            qS.bar(sync)
            vS.wait(sync)
            qS.inc(sync.dma_start(
                out=xn1tok_sp.rearrange("(t p) d -> p t d", p=128),
                in_=Dd[:, :].rearrange("p (t d) -> p t d", d=D)), 16)
        q_blk(p8_spill)

        transpose_fm(Dd, C, src_is_bf=True)

        router_logits(gate_wT, C)
        dev_tap("lg1", lg_sb[:, :])
        router_chain(True, CAP1, kg2_sb, True, ag3_in, ag3_out)
        dev_tap("kg1", kg2_sb[:, :])
        dev_tap("slot1", slot_sb[:, :])

        def qk_load(gpsimd):
            gq.bar(gpsimd)
            vS.wait(gpsimd); tS.wait(gpsimd)
            gq.inc(gpsimd.dma_start(
                out=Dd[:, 0:8192].rearrange("p (kk n) -> p kk n", kk=8),
                in_=Wq.rearrange("(kk p) n -> p kk n", p=128)), 16)
            gq.inc(gpsimd.dma_start(
                out=Dd[:, 8192:16384].rearrange("p (kk n) -> p kk n", kk=8),
                in_=Wk.rearrange("(kk p) n -> p kk n", p=128)), 16)
        g_blk(qk_load)

        for which, dst in ((0, Qfm), (1, Kfm)):
            for fc in range(8):
                def qk_t(tensor, which=which, fc=fc):
                    gq.wait(tensor); vS.wait(tensor)
                    for tq in range(4):
                        for kk in range(8):
                            tS.inc(tensor.matmul(
                                pb[tq][:, :],
                                Dd[:, which * 8192 + kk * 1024 + fc * 128:
                                   which * 8192 + kk * 1024 + (fc + 1) * 128],
                                C[:, kk * 2048 + tq * 512:kk * 2048 + (tq + 1) * 512],
                                start=(kk == 0), stop=(kk == 7)))
                t_blk(qk_t)

                def qk_v(vector, which=which, fc=fc):
                    tS.wait(vector)
                    for tq in range(4):
                        vS.inc(vector.tensor_copy(
                            dst[:, fc * 2048 + tq * 512:fc * 2048 + (tq + 1) * 512],
                            pb[tq][:, :]))
                v_blk(qk_v)

        dev_tap("qfm0", Qfm[:, 0:2048])
        dev_tap("kfm0", Kfm[:, 0:2048])
        def wv_load(gpsimd):
            gq.bar(gpsimd)
            sS.wait(gpsimd)
            gq.inc(gpsimd.dma_start(
                out=Dd[:, 0:8192].rearrange("p (kk n) -> p kk n", kk=8),
                in_=Wv.rearrange("(kk p) n -> p kk n", p=128)), 16)
        g_blk(wv_load)

        for t in range(16):
            def v_t(tensor, t=t):
                gq.wait(tensor); vS.wait(tensor)
                for dh in range(2):
                    for kk in range(8):
                        tS.inc(tensor.matmul(
                            pb[2 * (t % 2) + dh][:, :],
                            C[:, kk * 2048 + t * 128:kk * 2048 + (t + 1) * 128],
                            Dd[:, kk * 1024 + dh * 512:kk * 1024 + (dh + 1) * 512],
                            start=(kk == 0), stop=(kk == 7)))
            t_blk(v_t)

            def v_v(vector, t=t):
                tS.wait(vector)
                for dh in range(2):
                    vS.inc(vector.tensor_copy(
                        Vtm[:, t * 1024 + dh * 512:t * 1024 + (dh + 1) * 512],
                        pb[2 * (t % 2) + dh][:, :]))
            v_blk(v_v)

        def p8_v1(vector):
            vS.wait(vector)
            vS.inc(vector.memset(vaug[0][:, 64:65], 1.0))
            vS.inc(vector.memset(vaug[1][:, 64:65], 1.0))
            vS.inc(vector.memset(ones65[64:65, :], 1.0))
        v_blk(p8_v1)
        sS_exp = {}
        for h in range(16):
            hc, ho = divmod(h, 2)
            for kt in range(16):
                q0 = 128 * kt
                if kt >= 8:
                    base = (kt % 2) * 1024
                    W = 2048 - q0
                    chunks = [(ls, min(512, W - ls))
                              for ls in range(0, W, 512)]
                    sdep = sS_exp.get((h, kt - 2)) if kt >= 9 else None
                else:
                    base = None
                    chunks = []
                    for c in range(q0 // 512, 4):
                        gs = max(q0, 512 * c)
                        chunks.append((gs - q0, 512 * (c + 1) - gs))
                    sdep = None

                def s_t(tensor, hc=hc, ho=ho, kt=kt, q0=q0, base=base,
                        chunks=chunks, sdep=sdep, h=h):
                    vS.wait(tensor)
                    if base is None or kt == 8:
                        sS.wait(tensor)
                    elif sdep:
                        sS.wait(tensor, sdep)
                    for (ls, w) in chunks:
                        dst0 = (q0 + ls) if base is None else (base + ls)
                        tS.inc(tensor.matmul(
                            pbig[:, dst0:dst0 + w],
                            Kfm[ho * 64:(ho + 1) * 64,
                                hc * 2048 + kt * 128:hc * 2048 + (kt + 1) * 128],
                            Qfm[ho * 64:(ho + 1) * 64,
                                hc * 2048 + q0 + ls:hc * 2048 + q0 + ls + w],
                            start=True, stop=True))
                t_blk(s_t)

                def s_m(vector, kt=kt, q0=q0, base=base):
                    tS.wait(vector)
                    d0 = q0 if base is None else base
                    vS.inc(vector.tensor_tensor(
                        pbig[:, d0:d0 + 128], pbig[:, d0:d0 + 128],
                        caus[:, :], ALU.add))
                v_blk(s_m)

                def s_e(scalar, kt=kt, q0=q0, base=base):
                    tS.wait(scalar); vS.wait(scalar)
                    d0 = q0 if base is None else base
                    sS.inc(scalar.activation(
                        Ebuf[kt % 2][:, 0:2048 - q0],
                        pbig[:, d0:d0 + 2048 - q0], AF.Exp,
                        bias=nb8, scale=0.125))
                s_blk(s_e)
                sS_exp[(h, kt)] = sS.v

                def va_v(vector, h=h, kt=kt):
                    tS.wait(vector)
                    vS.inc(vector.tensor_copy(
                        vaug[kt % 2][:, 0:64],
                        Vtm[:, kt * 1024 + h * 64:kt * 1024 + (h + 1) * 64]))
                v_blk(va_v)

                avch = []
                for c in range(q0 // 512, 4):
                    gs = max(q0, 512 * c)
                    avch.append((c, gs, 512 * (c + 1) - gs))

                def av_t(tensor, h=h, kt=kt, q0=q0, chunks=avch):
                    sS.wait(tensor); vS.wait(tensor)
                    for (c, gs, w) in chunks:
                        last = (kt == min(15, 4 * c + 3))
                        tS.inc(tensor.matmul(
                            pb[c][0:65, gs - 512 * c:gs - 512 * c + w],
                            vaug[kt % 2][:, :],
                            Ebuf[kt % 2][:, gs - q0:gs - q0 + w],
                            start=(kt == 0), stop=last))
                t_blk(av_t)

            def h_v1(vector, h=h):
                tS.wait(vector); vS.wait(vector); gq.wait(vector)
                for qc in range(4):
                    vS.inc(vector.tensor_copy(
                        lst[:, qc * 512:(qc + 1) * 512], pb[qc][64:65, :]))
                vS.wait(vector)
                vS.inc(vector.reciprocal(lst, lst))
                vS.wait(vector)
                for qc in range(4):
                    vS.inc(vector.tensor_copy(
                        au_st[:, qc * 512:(qc + 1) * 512], pb[qc][0:64, :]))
            v_blk(h_v1)

            for qc in range(4):
                def inv_t(tensor, qc=qc, h=h):
                    vS.wait(tensor)
                    tS.inc(tensor.matmul(
                        pbig[0:64, 0:512], ones65[64:65, 0:64],
                        lst[:, qc * 512:(qc + 1) * 512],
                        start=True, stop=True))
                t_blk(inv_t)

                def inv_v(vector, qc=qc, h=h, hc=hc, ho=ho):
                    tS.wait(vector)
                    if ho == 0:
                        vS.inc(vector.tensor_tensor(
                            C[0:64, hc * 2048 + qc * 512:hc * 2048 + (qc + 1) * 512],
                            au_st[:, qc * 512:(qc + 1) * 512],
                            pbig[0:64, 0:512], ALU.mult))
                    else:
                        vS.inc(vector.tensor_tensor(
                            od_st[:, qc * 512:(qc + 1) * 512],
                            au_st[:, qc * 512:(qc + 1) * 512],
                            pbig[0:64, 0:512], ALU.mult))
                v_blk(inv_v)

            if ho == 1:
                def od_g(gpsimd, hc=hc):
                    gq.bar(gpsimd)
                    vS.wait(gpsimd)
                    gq.inc(gpsimd.dma_start(
                        out=C[64:128, hc * 2048:(hc + 1) * 2048],
                        in_=od_st[:, :]), 16)
                g_blk(od_g)

        dev_tap("vtm0", Vtm[:, 0:2048])
        dev_tap("attnC", C[:, 0:2048])
        def wr_load(gpsimd):
            gq.bar(gpsimd)
            tS.wait(gpsimd)
            gq.inc(gpsimd.dma_start(
                out=Dd[:, 0:8192].rearrange("p (kk n) -> p kk n", kk=8),
                in_=Wr.rearrange("(kk p) n -> p kk n", p=128)), 16)
        g_blk(wr_load)

        for t in range(16):
            def wr_t(tensor, t=t):
                gq.wait(tensor); gS.wait(tensor); vS.wait(tensor)
                if t > 1:
                    sS.wait(tensor)
                for dh in range(2):
                    for kk in range(8):
                        tS.inc(tensor.matmul(
                            pb[2 * (t % 2) + dh][:, :],
                            C[:, kk * 2048 + t * 128:kk * 2048 + (t + 1) * 128],
                            Dd[:, kk * 1024 + dh * 512:kk * 1024 + (dh + 1) * 512],
                            start=(kk == 0), stop=(kk == 7)))
            t_blk(wr_t)

            def wr_s(scalar, t=t):
                tS.wait(scalar)
                for dh in range(2):
                    sS.inc(scalar.activation(
                        AUXbf[:, t * 1024 + dh * 512:t * 1024 + (dh + 1) * 512],
                        pb[2 * (t % 2) + dh][:, :], AF.Copy))
            s_blk(wr_s)

        def s1_spill(sync):
            qS.bar(sync)
            sS.wait(sync)
            qS.inc(sync.dma_start(
                out=s1_sp.rearrange("(t p) d -> p t d", p=128),
                in_=AUXbf[:, :].rearrange("p (t d) -> p t d", d=D)), 16)
        q_blk(s1_spill)
        dev_tap("s1_c", AUXbf[:, :])
        if BUILD_UPTO < 9:
            return nc, dev_outs

        io512bc = SCR[:, 3072:3584]
        ptile = [SCRbf[:, 0:512], SCRbf[:, 512:1024]]
        slotloc = SCR[:, 3584:3585]
        b1sd = [SCR[:, 3600:3632], SCR[:, 3632:3664]]
        b2r = SCRbf[:, 4096:5120][0:8, :]
        ct_tile = SCRbf[:, 5120:5248]
        srow128 = SCR[0:1, 2816:2944]
        b2tok = tmp4b
        tS_p8, vS_p8, sS_p8, qS_p8 = tS.v, vS.v, sS.v, qS.v
        w1sem = [Sem(nc, ctx, f"w1s{r}") for r in range(8)]
        w2sem = [Sem(nc, ctx, f"w2s{r}") for r in range(8)]
        b1sem = [Sem(nc, ctx, "b1s0"), Sem(nc, ctx, "b1s1")]
        eoq = Sem(nc, ctx, "eoq")
        b2row = SCRbf[0:1, 5120:6144]
        b2bc_sb = SCRbf[:, 6144:7168]

        def p9_ld(gpsimd):
            gq.bar(gpsimd)
            qS.wait(gpsimd); sS.wait(gpsimd); vS.wait(gpsimd)
            gq.inc(gpsimd.dma_start(
                out=Dd[:, :].rearrange("p (t d) -> p t d", d=D),
                in_=xn1tok_sp.rearrange("(t p) d -> p t d", p=128)), 16)
            gq.inc(gpsimd.dma_start(out=b2r, in_=b2), 16)
        g_blk(p9_ld)

        def p9_t(tensor):
            vS.wait(tensor); qS.wait(tensor)
            tS.inc(tensor.matmul(pb[0][:, :], onesr_f[0:1, :],
                                 io512[0:1, :], start=True, stop=True))
        t_blk(p9_t)

        def p9_v(vector):
            tS.wait(vector)
            vS.inc(vector.tensor_copy(io512bc[:, :], pb[0][:, :]))
        v_blk(p9_v)

        w1v = w1.rearrange("e h d f -> (e h d) f")
        w2v = w2.rearrange("e f n -> (e f) n")
        b1v = b1r_d.rearrange("e p c -> (e p) c")
        SCRi = SCR[:, :].bitcast(mybir.dt.int32)
        tblw1 = SCRi[:, 3664:3680]
        tblw2 = SCRi[:, 3680:3696]
        tblb1 = SCRi[:, 3696:3712]
        tbtmp = SCR[:, 3712:3728]

        def tbl_t(tensor):
            vS.wait(tensor); qS.wait(tensor)
            tS.inc(tensor.matmul(pb[0][:, 0:16], onesr_f[0:1, :],
                                 erow_f[:, :], start=True, stop=True))
        t_blk(tbl_t)

        sloti = SCRi[:, 3728:3744]
        onesr_b = SCRbf[0:1, 7488:7616]
        oh8seg = SCRbf[0:8, 7616:7632]

        def tbl_v(vector):
            tS.wait(vector); qS.wait(vector)
            for mult, dst in ((2048.0, tblw1), (4096.0, tblw2),
                              (128.0, tblb1)):
                vS.wait(vector)
                vS.inc(vector.tensor_scalar(tbtmp[:, :], pb[0][:, 0:16],
                                            mult, iost[:, 0:1],
                                            ALU.mult, ALU.add))
                vS.wait(vector)
                vS.inc(vector.tensor_copy(dst, tbtmp[:, :]))
            vS.wait(vector)
            vS.inc(vector.tensor_scalar(oh8seg[:, :], pb[0][0:8, 0:16],
                                        iost[0:8, 0:1], None, ALU.is_equal))
            vS.inc(vector.tensor_copy(sloti[:, :], slot_sb[:, :]))
            vS.inc(vector.memset(onesr_b[:, :], 1.0))
        v_blk(tbl_v)
        vS_tbl = vS.v

        dsp_psum = [pb[0][:, :], pb[1][:, :], pb[2][:, :], pb[3][:, :],
                    pbig[:, 0:512], pbig[:, 512:1024],
                    pbig[:, 1024:1536], pbig[:, 1536:2048]]
        for sc in range(8):
            for tt in range(16):
                def pt_v(vector, sc=sc, tt=tt):
                    vS.wait(vector)
                    if tt > 1 or sc > 0:
                        tS.wait(vector)
                    vS.inc(vector.tensor_scalar(
                        slotloc[:, :], slot_sb[:, tt:tt + 1], 1.0,
                        float(-512 * sc), ALU.mult, ALU.add))
                    vS.wait(vector)
                    vS.inc(vector.tensor_scalar(
                        ptile[tt % 2][:, 0:512], io512bc[:, :],
                        slotloc[:, :], None, ALU.is_equal))
                v_blk(pt_v)

                def pt_t(tensor, sc=sc, tt=tt):
                    vS.wait(tensor); gq.wait(tensor)
                    for dc in range(8):
                        tS.inc(tensor.matmul(
                            dsp_psum[dc],
                            Dd[:, tt * D + dc * 128:tt * D + (dc + 1) * 128],
                            ptile[tt % 2][:, 0:512],
                            start=(tt == 0), stop=(tt == 15)))
                t_blk(pt_t)

            def dsp_v(vector, sc=sc):
                tS.wait(vector)
                for dc in range(8):
                    for half in range(2):
                        sg = 2 * sc + half
                        vS.inc(vector.tensor_copy(
                            Abf[:, sg * 2048 + dc * 256:sg * 2048 + (dc + 1) * 256],
                            dsp_psum[dc][:, half * 256:(half + 1) * 256]))
            v_blk(dsp_v)

        w2_psum = [pb[0][:, :], pb[1][:, :], pb[2][:, :], pb[3][:, :]]

        def hbank(ft):
            pr, hf = ft // 2, ft % 2
            if pr < 4:
                return pbig[:, pr * 512 + hf * 256:pr * 512 + (hf + 1) * 256]
            return pb[pr - 4][:, hf * 256:(hf + 1) * 256]

        w1_free = {r: tS_p8 for r in range(8)}
        w2_free = {r: tS_p8 for r in range(8)}

        def w1slot(r):
            if r < 4:
                return C[:, r * 2048:(r + 1) * 2048]
            return AUXbf[:, 8192 + (r - 4) * 2048:8192 + (r - 3) * 2048]
        b1_free = {0: sS_p8, 1: sS_p8}
        evac_val = {}
        eov_val = None
        for sg in range(16):
            def b1_ld(gpsimd, sg=sg):
                if sg == 0:
                    gpsimd.wait_ge(tS.h, tS_p8)
                    gpsimd.wait_ge(vS.h, vS_tbl)
                    gpsimd.wait_ge(sS.h, sS_p8)
                    gpsimd.wait_ge(qS.h, qS_p8)
                sS.wait(gpsimd, b1_free[sg % 2])
                b1sem[sg % 2].inc(gpsimd.indirect_dma_start(
                    b1sd[sg % 2][:, :], None, b1v,
                    IndirectOffsetOnAxis(ap=tblb1[:, sg:sg + 1],
                                         axis=0)), 16)
            g_blk(b1_ld)
            for fh in range(2):
                for dc in range(8):
                    r = (fh * 8 + dc) % 8

                    def w1_ld(gpsimd, sg=sg, fh=fh, dc=dc, r=r):
                        if sg == 0 and fh == 0 and r >= 4:
                            gpsimd.wait_ge(qS.h, qS_p8)
                        tS.wait(gpsimd, w1_free[r])
                        w1sem[r].inc(gpsimd.indirect_dma_start(
                            w1slot(r), None, w1v,
                            IndirectOffsetOnAxis(ap=tblw1[:, sg:sg + 1],
                                                 axis=0),
                            element_offset=fh * 2097152 + dc * 262144), 16)
                    g_blk(w1_ld)

                    def hd_t(tensor, sg=sg, fh=fh, dc=dc, r=r):
                        w1sem[r].wait(tensor)
                        if dc == 0 and fh == 0 and sg == 0:
                            vS.wait(tensor)
                        for i, ft in enumerate((0, 2, 4, 6, 8, 10, 12, 14,
                                                1, 3, 5, 7, 9, 11, 13, 15)):
                            if dc == 0 and i < 8:
                                bk = ft // 2
                                if i == 0 and fh == 0 and eov_val is not None:
                                    vS.wait(tensor, eov_val)
                                if bk in evac_val:
                                    if bk < 4:
                                        sS.wait(tensor, evac_val[bk])
                                    else:
                                        vS.wait(tensor, evac_val[bk])
                            tS.inc(tensor.matmul(
                                hbank(ft),
                                w1slot(r)[:, ft * 128:(ft + 1) * 128],
                                Abf[:, sg * 2048 + dc * 256:
                                    sg * 2048 + (dc + 1) * 256],
                                start=(dc == 0 and ft % 2 == 0),
                                stop=(dc == 7 and ft % 2 == 1)))
                    t_blk(hd_t)
                    w1_free[r] = tS.v

                def hd_s(scalar, sg=sg, fh=fh):
                    tS.wait(scalar)
                    b1sem[sg % 2].wait(scalar)
                    for ft in range(8):
                        gfc = fh * 16 + ft
                        sS.inc(scalar.activation(
                            AUXbf[:, gfc * 256:(gfc + 1) * 256],
                            hbank(ft), AF.Relu,
                            bias=b1sd[sg % 2][:, gfc:gfc + 1]))
                s_blk(hd_s)
                for bk in range(4):
                    evac_val[bk] = sS.v

                def hd_v(vector, sg=sg, fh=fh):
                    tS.wait(vector)
                    b1sem[sg % 2].wait(vector)
                    for ft in range(8, 16):
                        gfc = fh * 16 + ft
                        vS.inc(vector.tensor_scalar(
                            AUXbf[:, gfc * 256:(gfc + 1) * 256],
                            hbank(ft), b1sd[sg % 2][:, gfc:gfc + 1],
                            0.0, ALU.add, ALU.max))
                v_blk(hd_v)
                for bk in range(4, 8):
                    evac_val[bk] = vS.v
            b1_free[sg % 2] = sS.v

            for fk in range(32):
                rk = fk % 8

                def w2_ld(gpsimd, sg=sg, fk=fk, rk=rk):
                    tS.wait(gpsimd, w2_free[rk])
                    w2sem[rk].inc(gpsimd.indirect_dma_start(
                        C[:, 8192 + rk * 1024:8192 + (rk + 1) * 1024],
                        None, w2v,
                        IndirectOffsetOnAxis(ap=tblw2[:, sg:sg + 1],
                                             axis=0),
                        element_offset=fk * 131072), 16)
                g_blk(w2_ld)

                def w2_t(tensor, sg=sg, fk=fk, rk=rk):
                    w2sem[rk].wait(tensor)
                    if fk == 0:
                        sS.wait(tensor); vS.wait(tensor)
                    for ts2 in range(2):
                        for dh in range(2):
                            tS.inc(tensor.matmul(
                                w2_psum[ts2 * 2 + dh],
                                AUXbf[:, fk * 256 + ts2 * 128:
                                      fk * 256 + (ts2 + 1) * 128],
                                C[:, 8192 + rk * 1024 + dh * 512:
                                  8192 + rk * 1024 + (dh + 1) * 512],
                                start=(fk == 0), stop=(fk == 31)))
                t_blk(w2_t)
                w2_free[rk] = tS.v

            def b2b_t1(tensor, sg=sg):
                for dh in range(2):
                    tS.inc(tensor.matmul(
                        pbig[0:1, dh * 512:(dh + 1) * 512],
                        oh8seg[:, sg:sg + 1],
                        b2r[:, dh * 512:(dh + 1) * 512],
                        start=True, stop=True))
            t_blk(b2b_t1)

            def b2b_v1(vector, sg=sg):
                tS.wait(vector)
                vS.inc(vector.tensor_copy(b2row[:, :], pbig[0:1, 0:1024]))
            v_blk(b2b_v1)

            def b2b_t2(tensor, sg=sg):
                vS.wait(tensor)
                for dh in range(2):
                    tS.inc(tensor.matmul(
                        pbig[:, dh * 512:(dh + 1) * 512], onesr_b[:, :],
                        b2row[0:1, dh * 512:(dh + 1) * 512],
                        start=True, stop=True))
            t_blk(b2b_t2)

            def b2b_v2(vector, sg=sg):
                tS.wait(vector)
                vS.inc(vector.tensor_copy(b2bc_sb[:, :], pbig[:, 0:1024]))
            v_blk(b2b_v2)

            def eo_v(vector, sg=sg):
                vS.wait(vector)
                for ts2 in range(2):
                    for dh in range(2):
                        vS.inc(vector.tensor_tensor(
                            Abf[:, sg * 2048 + ts2 * 1024 + dh * 512:
                                sg * 2048 + ts2 * 1024 + (dh + 1) * 512],
                            w2_psum[ts2 * 2 + dh],
                            b2bc_sb[:, dh * 512:(dh + 1) * 512], ALU.add))
            v_blk(eo_v)
            eov_val = vS.v

            def eo_q(sync, sg=sg):
                vS.wait(sync)
                for ts2 in range(2):
                    eoq.inc(sync.dma_start(
                        out=eout_sp[sg * 256 + ts2 * 128:
                                    sg * 256 + (ts2 + 1) * 128, :],
                        in_=Abf[:, sg * 2048 + ts2 * 1024:
                                sg * 2048 + (ts2 + 1) * 1024]), 16)
            q_blk(eo_q)

        def cg_g(gpsimd):
            eoq.wait(gpsimd)
            tS.wait(gpsimd); vS.wait(gpsimd)
            for t in range(16):
                gq.inc(gpsimd.indirect_dma_start(
                    Dd[:, t * 1024:(t + 1) * 1024], None, eout_sp,
                    IndirectOffsetOnAxis(ap=sloti[:, t:t + 1], axis=0)), 16)
        g_blk(cg_g)

        def cb_s(scalar):
            gq.wait(scalar); vS.wait(scalar)
            for t in range(16):
                sS.inc(scalar.activation(
                    AUXbf[:, t * 1024:(t + 1) * 1024],
                    Dd[:, t * 1024:(t + 1) * 1024],
                    AF.Silu, scale=kg2_sb[:, t:t + 1]))
        s_blk(cb_s)
        dev_tap("gm1", AUXbf[:, :])
        if BUILD_UPTO < 10:
            return nc, dev_outs

        for t in range(16):
            def f_q(sync, t=t):
                qS.bar(sync)
                vS.wait(sync); sS.wait(sync)
                qS.inc(sync.dma_start(out=tmp4[:, :],
                                      in_=x2_sp[t * 128:(t + 1) * 128, :]), 16)
                qS.inc(sync.dma_start(out=wka[:, 0:D],
                                      in_=s1_sp[t * 128:(t + 1) * 128, :]), 16)
            q_blk(f_q)

            def f_v(vector, t=t):
                qS.wait(vector); vS.wait(vector); sS.wait(vector)
                oS.wait(vector)
                vS.inc(vector.tensor_tensor(
                    tmp4b[:, :], wka[:, 0:D],
                    AUXbf[:, t * 1024:(t + 1) * 1024], ALU.mult))
                vS.wait(vector)
                vS.inc(vector.tensor_tensor(SCR[:, 0:D], tmp4[:, :],
                                            tmp4b[:, :], ALU.add))
            v_blk(f_v)

            def f_s(sync, t=t):
                qS.bar(sync)
                vS.wait(sync)
                oS.inc(sync.dma_start(out=out[t * 128:(t + 1) * 128, :],
                                      in_=SCR[:, 0:D]), 16)
                oS.wait(sync)
            q_blk(f_s)

        def fin_q(sync):
            oS.wait(sync)
        q_blk(fin_q)



    return nc, dev_outs


def _prep_inputs(core, **inp):
    x = np.asarray(inp["L0_attn_x"], np.float32)
    i = core
    xb = np.ascontiguousarray(x[i])
    xs = np.ascontiguousarray(
        x[:, 256 * i:256 * (i + 1), :].reshape(2048, D))
    if i == 0:
        xh = np.zeros((24, D), np.float32)
    else:
        xh = np.ascontiguousarray(
            x[:, 256 * i - 3:256 * i, :].reshape(24, D))
    d = {
        "xb": xb, "xs": xs, "xh": xh,
        "ln1_w": np.asarray(inp["L0_attn_ln1_w"], np.float32),
        "ln1_b": np.asarray(inp["L0_attn_ln1_b"], np.float32),
        "convw": np.ascontiguousarray(
            np.asarray(inp["L0_attn_conv_w"], np.float32)[:, 0, :].T),
        "convb": np.asarray(inp["L0_attn_conv_b"], np.float32),
        "lns_w": np.asarray(inp["L0_attn_lns_w"], np.float32),
        "lns_b": np.asarray(inp["L0_attn_lns_b"], np.float32),
        "in_w": np.asarray(inp["L0_attn_in_w"], np.float32),
        "in_b": np.ascontiguousarray(np.asarray(inp["L0_attn_in_b"], np.float32).reshape(16, 128).T),
        "xp_w": np.asarray(inp["L0_attn_xp_w"], np.float32),
        "xp_b": np.asarray(inp["L0_attn_xp_b"], np.float32).reshape(16, 1),
        "A_log": np.asarray(inp["L0_attn_A_log"], np.float32).reshape(16, 1),
        "Dmat": np.asarray(inp["L0_attn_D"], np.float32),
        "out_w": np.asarray(inp["L0_attn_out_w"], np.float32),
        "out_b": np.asarray(inp["L0_attn_out_b"], np.float32),
        "ln2_w": np.asarray(inp["L0_attn_ln2_w"], np.float32),
        "ln2_b": np.asarray(inp["L0_attn_ln2_b"], np.float32),
        "mlp_w1": np.asarray(inp["L0_attn_mlp_w1"], np.float32),
        "mlp_b1": np.ascontiguousarray(np.asarray(inp["L0_attn_mlp_b1"], np.float32).reshape(16, 128).T),
        "mlp_w2": np.asarray(inp["L0_attn_mlp_w2"], np.float32),
        "mlp_b2": np.asarray(inp["L0_attn_mlp_b2"], np.float32),
        "Wg": np.asarray(inp["L0_ffn_Wg"], np.float32),
        "We": np.asarray(inp["L0_ffn_We"], np.float32),
        "be": np.asarray(inp["L0_ffn_be"], np.float32),
        "Wq": np.asarray(inp["L1_attn_Wq"], np.float32),
        "Wk": np.asarray(inp["L1_attn_Wk"], np.float32),
        "Wv": np.asarray(inp["L1_attn_Wv"], np.float32),
        "Wr": np.asarray(inp["L1_attn_Wr"], np.float32),
        "gate_wT": np.ascontiguousarray(
            np.asarray(inp["L1_ffn_gate_w"], np.float32).T),
        "gate_b": np.asarray(inp["L1_ffn_gate_b"], np.float32).reshape(1, 8),
        "w1": np.ascontiguousarray(
            np.asarray(inp["L1_ffn_w1"], np.float32)
            .reshape(8, D, 2, FF // 2).transpose(0, 2, 1, 3)),
        "b1r": np.ascontiguousarray(np.asarray(inp["L1_ffn_b1"], np.float32).reshape(8, 32, 128).transpose(0, 2, 1)),
        "w2": np.asarray(inp["L1_ffn_w2"], np.float32),
        "b2": np.asarray(inp["L1_ffn_b2"], np.float32),
        "identf": np.eye(128, dtype=np.float32),
        "identb": np.eye(128, dtype=np.float32).astype(ml_dtypes.bfloat16),
        "tri_bf": np.triu(np.ones((128, 128), np.float32)).astype(
            ml_dtypes.bfloat16),
        "cmask": (np.arange(8) < i).astype(np.float32).reshape(8, 1),
        "cexp": np.maximum(i - 1 - np.arange(8), 0).astype(
            np.float32).reshape(8, 1),
        "hmask": np.full((24, 1), 0.0 if i == 0 else 1.0, np.float32),
        "causm": (-400.0 * (np.arange(128)[:, None] > np.arange(128)[None, :])
                  ).astype(ml_dtypes.bfloat16),
        "iota8": np.arange(8, dtype=np.float32).reshape(1, 8),
        "iota512": np.arange(512, dtype=np.float32).reshape(1, 512),
        "iota_st": (np.arange(32)[None, :] * 128
                    + np.arange(128)[:, None]).astype(np.float32),
        "seg256": (np.arange(16, dtype=np.float32) * 256).reshape(16, 1),
    }
    return d


def kernel(**inputs):
    nc, dev_outs = build()
    in_maps = [_prep_inputs(i, **inputs) for i in range(8)]
    res = run_bass_kernel_spmd(nc, in_maps, core_ids=list(range(8)))
    outp = np.stack([np.asarray(res.results[i]["out"], np.float32)
                     for i in range(8)], axis=0)
    return outp.reshape(8, 2048, 1024)


if __name__ == "__main__":
    print("kernel module ok")


# version 10
# speedup vs baseline: 1.0118x; 1.0020x over previous
import numpy as np
import ml_dtypes
from contextlib import ExitStack

import concourse.bass as bass
import concourse.mybir as mybir
from concourse.bass_utils import run_bass_kernel_spmd
from concourse.bass_core_rust import IndirectOffsetOnAxis

F32 = mybir.dt.float32
BF16 = mybir.dt.bfloat16
ALU = mybir.AluOpType
AF = mybir.ActivationFunctionType
AX = mybir.AxisListType

B, S, D = 8, 2048, 1024
TOK = 2048
NT = 16
DC = 8
EPS = 1e-5
CAP0 = 2048.0
CAP1 = 2560.0
M1 = 384
NSLOT = 8 * M1
FF = 4096

DEV_TAPS = []
BUILD_UPTO = 99


class Sem:
    def __init__(self, nc, ctx, name):
        self.h = ctx.enter_context(nc.semaphore(name))
        self.v = 0
        self.mw = 0

    def inc(self, inst, n=1):
        inst.then_inc(self.h, n)
        self.v += n
        return self.v

    def wait(self, eng, v=None):
        w = self.v if v is None else v
        self.mw = max(self.mw, w)
        eng.wait_ge(self.h, w)

    def bar(self, eng):
        if self.mw > 0:
            eng.wait_ge(self.h, self.mw)


def build():
    nc = bass.Bass("TRN2", target_bir_lowering=False, debug=False)

    def din(name, shape, dt=F32):
        return nc.dram_tensor(name, shape, dt, kind="ExternalInput").ap()

    xb = din("xb", [TOK, D])
    xs = din("xs", [TOK, D])
    xh = din("xh", [24, D])
    ln1_w = din("ln1_w", [D]); ln1_b = din("ln1_b", [D])
    convw = din("convw", [4, D]); convb = din("convb", [D])
    lns_w = din("lns_w", [D]); lns_b = din("lns_b", [D])
    in_w = din("in_w", [D, 2 * D]); in_b = din("in_b", [128, 16])
    xp_w = din("xp_w", [D, 16]); xp_b = din("xp_b", [16, 1])
    A_log = din("A_log", [16, 1])
    Dmat = din("Dmat", [16, D])
    out_w = din("out_w", [D, D]); out_b = din("out_b", [D])
    ln2_w = din("ln2_w", [D]); ln2_b = din("ln2_b", [D])
    mlp_w1 = din("mlp_w1", [D, 2 * D]); mlp_b1 = din("mlp_b1", [128, 16])
    mlp_w2 = din("mlp_w2", [2 * D, D]); mlp_b2 = din("mlp_b2", [D])
    Wg = din("Wg", [D, 8])
    We = din("We", [D, D]); be = din("be", [D])
    Wq = din("Wq", [D, D]); Wk = din("Wk", [D, D])
    Wv = din("Wv", [D, D]); Wr = din("Wr", [D, D])
    gate_wT = din("gate_wT", [D, 8]); gate_b = din("gate_b", [1, 8])
    w1 = din("w1", [8, 2, D, FF // 2]); b1r_d = din("b1r", [8, 128, 32])
    w2 = din("w2", [8, FF, D]); b2 = din("b2", [8, D])
    identf = din("identf", [128, 128])
    identb = din("identb", [128, 128], BF16)
    tri_bf = din("tri_bf", [128, 128], BF16)
    cmask = din("cmask", [8, 1])
    cexp = din("cexp", [8, 1])
    hmask = din("hmask", [24, 1])
    causm = din("causm", [128, 128], BF16)
    iota8 = din("iota8", [1, 8])
    iota512 = din("iota512", [1, 512])
    iota_st = din("iota_st", [128, 32])
    seg256 = din("seg256", [16, 1])

    out = nc.dram_tensor("out", [TOK, D], F32, kind="ExternalOutput").ap()

    y_sp = nc.dram_tensor("y_sp", [TOK, D], BF16, kind="Internal").ap()
    eout_sp = nc.dram_tensor("eout_sp", [4096, D], BF16, kind="Internal").ap()
    x2_sp = nc.dram_tensor("x2_sp", [TOK, D], F32, kind="Internal").ap()
    s1_sp = nc.dram_tensor("s1_sp", [TOK, D], BF16, kind="Internal").ap()
    ag1_in = nc.dram_tensor("ag1_in", [1, 128], F32, kind="Internal").ap()
    ag1_out = nc.dram_tensor("ag1_out", [8, 128], F32, kind="Internal").ap()
    ag2_in = nc.dram_tensor("ag2_in", [1, 8], F32, kind="Internal").ap()
    ag3_in = nc.dram_tensor("ag3_in", [1, 8], F32, kind="Internal").ap()
    ag2_out = nc.dram_tensor("ag2_out", [8, 8], F32, kind="Internal").ap()
    ag3_out = nc.dram_tensor("ag3_out", [8, 8], F32, kind="Internal").ap()

    dev_outs = {}
    for name, shape, dt in DEV_TAPS:
        dev_outs[name] = nc.dram_tensor("dev_" + name, shape, dt,
                                        kind="ExternalOutput").ap()

    ctx = ExitStack()
    with ctx:
        block = ctx.enter_context(nc.Block())

        def sb(name, shape, dt=F32):
            return ctx.enter_context(nc.sbuf_tensor(name, shape, dt))

        def ps(name, shape):
            return ctx.enter_context(nc.psum_tensor(name, shape, F32))

        tS = Sem(nc, ctx, "tS"); vS = Sem(nc, ctx, "vS")
        sS = Sem(nc, ctx, "sS"); gS = Sem(nc, ctx, "gS")
        qS = Sem(nc, ctx, "qS"); oS = Sem(nc, ctx, "oS")
        cS = Sem(nc, ctx, "cS"); gq = Sem(nc, ctx, "gq")

        A = sb("A", [128, 16 * D], F32)
        C = sb("C", [128, 16 * D], BF16)
        Dd = sb("Dd", [128, 16 * D], BF16)
        SCR = sb("SCR", [128, 4 * D], F32)
        tmp4 = sb("tmp4", [128, D], F32)
        tmp4b = sb("tmp4b", [128, D], F32)
        Abf = A[:, :].bitcast(BF16)
        Dbf32 = Dd[:, :].bitcast(F32)

        idf = sb("idf", [128, 128])
        idb = sb("idb", [128, 128], BF16)
        tri = sb("tri", [128, 128], BF16)
        cm8 = sb("cm8", [8, 1]); ce8 = sb("ce8", [8, 1])
        hm24 = sb("hm24", [24, 1])
        caus = sb("caus", [128, 128], BF16)
        io8 = sb("io8", [1, 8])
        io512 = sb("io512", [1, 512])
        iost = sb("iost", [128, 32])
        s256 = sb("s256", [16, 1])
        ones_f = sb("ones_f", [128, 1])
        ones_b = sb("ones_b", [128, 1], BF16)
        onesr_f = sb("onesr_f", [1, 128])
        lnv = sb("lnv", [1, D])
        xpw8 = sb("xpw8", [128, 8 * 16], BF16)
        inb_sb = sb("inb_sb", [128, 16])
        m1b_sb = sb("m1b_sb", [128, 16])
        xpb_sb = sb("xpb_sb", [16, 1])
        alog = sb("alog", [16, 1])
        svec = sb("svec", [16, 1])
        dvec = sb("dvec", [16, 1])
        stat_m = sb("stat_m", [128, 16])
        stat_i = sb("stat_i", [128, 16])
        AUX = sb("AUX", [128, 8192], F32)
        AUXbf = AUX[:, :].bitcast(BF16)
        u_fm = AUX[0:16, 0:2048]
        hs_fm = AUX[0:16, 2048:4096]
        dect = AUX[0:16, 4096:6144]
        dmat = AUX[0:16, 6144:7168]
        halo_x = AUX[0:24, 7168:8192]
        e16 = sb("e16", [16, 8])
        eag = sb("eag", [8, 136])
        wjn = sb("wjn", [8, 16])
        prod8 = sb("prod8", [8, 128])
        c128 = sb("c128", [128, 1])
        ccorr = sb("ccorr", [16, 8])
        s8x16 = sb("s8x16", [8, 16])
        srow = sb("srow", [1, 16])

        halo_n1 = sb("halo_n1", [24, D], BF16)
        hst = sb("hst", [24, 2])
        wka = sb("wka", [128, 2048], BF16)
        wkb = sb("wkb", [128, 2048], BF16)

        pb = [ps(f"pb{i}", [128, 512]) for i in range(4)]
        pbig = ps("pbig", [128, 2048])

        lnw_bc = SCR[:, 0:D]
        lnb_bc = SCR[:, D:2 * D]
        bias_bc = SCR[:, 2 * D:3 * D]
        w_bc = SCR[:, 0:4 * D]

        def t_blk(fn):
            block.tensor(fn)

        def v_blk(fn):
            block.vector(fn)

        def s_blk(fn):
            block.scalar(fn)

        def g_blk(fn):
            block.gpsimd(fn)

        def q_blk(fn):
            block.sync(fn)

        def load_row(src_dram_row, width=D):
            def qf(sync):
                qS.bar(sync)
                tS.wait(sync)
                qS.inc(sync.dma_start(out=lnv[0:1, 0:width],
                                      in_=src_dram_row), 16)
            q_blk(qf)

        def bcast_lnv(dst, width=D):
            nchunk = (width + 511) // 512

            def tf(tensor):
                vS.wait(tensor); qS.wait(tensor); gS.wait(tensor)
                for ci in range(nchunk):
                    w = min(512, width - ci * 512)
                    tS.inc(tensor.matmul(pb[ci % 4][:, 0:w], onesr_f[0:1, :],
                                         lnv[0:1, ci * 512:ci * 512 + w],
                                         start=True, stop=True))
            t_blk(tf)

            def vf(vector):
                tS.wait(vector)
                for ci in range(nchunk):
                    w = min(512, width - ci * 512)
                    vS.inc(vector.tensor_copy(dst[:, ci * 512:ci * 512 + w],
                                              pb[ci % 4][:, 0:w]))
            v_blk(vf)

        def load_bcast(src_dram_row, dst, width=D):
            load_row(src_dram_row, width)
            bcast_lnv(dst, width)

        def layernorm(src, dst, tiles, scratch, wb=None, mcol=None, icol=None,
                      halo_mask=None, width=D):
            def v1(vector):
                qS.wait(vector); gS.wait(vector); sS.wait(vector)
                gq.wait(vector)
                for t in tiles:
                    vS.inc(vector.reduce_sum(mcol[:, t:t + 1],
                                             src[:, t * width:(t + 1) * width],
                                             axis=AX.X))
                vS.wait(vector)
                for t in tiles:
                    vS.inc(vector.tensor_scalar(mcol[:, t:t + 1],
                                                mcol[:, t:t + 1],
                                                1.0 / width, None, ALU.mult))
            v_blk(v1)

            npart = src.shape[0]

            def s1(scalar):
                vS.wait(scalar)
                for n, t in enumerate(tiles):
                    sS.wait(scalar)
                    sS.inc(scalar.activation(
                        scratch[0:npart, (n % 2) * width:(n % 2 + 1) * width],
                        src[:, t * width:(t + 1) * width],
                        AF.Square, bias=mcol[:, t:t + 1], scale=-1.0,
                        accum_out=icol[:, t:t + 1]))
            s_blk(s1)

            def v2(vector):
                sS.wait(vector)
                for t in tiles:
                    vS.inc(vector.tensor_scalar(icol[:, t:t + 1],
                                                icol[:, t:t + 1],
                                                1.0 / width, EPS,
                                                ALU.mult, ALU.add))
                vS.wait(vector)
                for t in tiles:
                    vS.inc(vector.reciprocal(icol[:, t:t + 1], icol[:, t:t + 1]))
            v_blk(v2)

            def s2(scalar):
                vS.wait(scalar)
                for t in tiles:
                    sS.inc(scalar.activation(icol[:, t:t + 1], icol[:, t:t + 1],
                                             AF.Sqrt))
            s_blk(s2)

            if wb is None:
                def v3(vector):
                    sS.wait(vector)
                    for t in tiles:
                        vS.inc(vector.tensor_scalar(
                            dst[:, t * width:(t + 1) * width],
                            src[:, t * width:(t + 1) * width],
                            mcol[:, t:t + 1], icol[:, t:t + 1],
                            ALU.subtract, ALU.mult))
                v_blk(v3)
            else:
                wt, bt = wb

                def v3(vector):
                    sS.wait(vector)
                    for t in tiles:
                        vS.inc(vector.scalar_tensor_tensor(
                            dst[:, t * width:(t + 1) * width],
                            src[:, t * width:(t + 1) * width],
                            mcol[:, t:t + 1], wt[0:npart, 0:width],
                            ALU.subtract, ALU.mult))
                    vS.wait(vector)
                    for t in tiles:
                        vS.inc(vector.scalar_tensor_tensor(
                            dst[:, t * width:(t + 1) * width],
                            dst[:, t * width:(t + 1) * width],
                            icol[:, t:t + 1], bt[0:npart, 0:width],
                            ALU.mult, ALU.add))
                v_blk(v3)
            if halo_mask is not None:
                def v4(vector):
                    vS.wait(vector)
                    for t in tiles:
                        vS.inc(vector.tensor_scalar(
                            dst[:, t * width:(t + 1) * width],
                            dst[:, t * width:(t + 1) * width],
                            halo_mask, None, ALU.mult))
                v_blk(v4)

        pbig_bf = pbig[:, :].bitcast(BF16)

        def transpose_fm(src, dst, src_is_bf=False, ntile=16):
            ntok = ntile * 128
            ident = idb if src_is_bf else idf
            pout = pbig_bf if src_is_bf else pbig
            pairs = [(t, dc) for t in range(ntile) for dc in range(DC)]
            for gs in range(0, len(pairs), 16):
                grp = pairs[gs:gs + 16]

                def tf(tensor, grp=grp):
                    vS.wait(tensor); sS.wait(tensor)
                    for n, (t, dc) in enumerate(grp):
                        tS.inc(tensor.transpose(
                            pout[:, n * 128:(n + 1) * 128],
                            src[:, t * D + dc * 128:t * D + (dc + 1) * 128],
                            ident[:, :]))
                t_blk(tf)

                def ef(vector, grp=grp):
                    tS.wait(vector)
                    for n, (t, dc) in enumerate(grp):
                        vS.inc(vector.tensor_copy(
                            dst[:, dc * ntok + t * 128:dc * ntok + (t + 1) * 128],
                            pout[:, n * 128:(n + 1) * 128]))
                v_blk(ef)

        def dev_tap(name, src_ap):
            if name not in dev_outs:
                return

            def qf(sync):
                qS.bar(sync)
                vS.wait(sync); sS.wait(sync); tS.wait(sync); gS.wait(sync)
                gq.wait(sync)
                oS.inc(sync.dma_start(out=dev_outs[name], in_=src_ap), 16)
                oS.wait(sync)
            q_blk(qf)

            def bar(eng):
                oS.wait(eng)
            for blkfn in (t_blk, v_blk, s_blk, g_blk):
                blkfn(bar)

        def p0_sync(sync):
            qS.bar(sync)
            qS.inc(sync.dma_start(out=idf[:, :], in_=identf), 16)
            qS.inc(sync.dma_start(out=idb[:, :], in_=identb), 16)
            qS.inc(sync.dma_start(out=tri[:, :], in_=tri_bf), 16)
            qS.inc(sync.dma_start(out=cm8[:, :], in_=cmask), 16)
            qS.inc(sync.dma_start(out=ce8[:, :], in_=cexp), 16)
            qS.inc(sync.dma_start(out=hm24[:, :], in_=hmask), 16)
            qS.inc(sync.dma_start(out=caus[:, :], in_=causm), 16)
            qS.inc(sync.dma_start(out=io8[:, :], in_=iota8), 16)
            qS.inc(sync.dma_start(out=io512[:, :], in_=iota512), 16)
            qS.inc(sync.dma_start(out=iost[:, :], in_=iota_st), 16)
            qS.inc(sync.dma_start(out=s256[:, :], in_=seg256), 16)
            qS.inc(sync.dma_start(out=inb_sb[:, :], in_=in_b), 16)
            qS.inc(sync.dma_start(out=m1b_sb[:, :], in_=mlp_b1), 16)
            qS.inc(sync.dma_start(out=xpb_sb[:, :], in_=xp_b), 16)
            qS.inc(sync.dma_start(out=alog[:, :], in_=A_log), 16)
            qS.inc(sync.dma_start(out=dmat[:, :], in_=Dmat), 16)
            for bq in range(4):
                qS.inc(sync.dma_start(
                    out=A[:, bq * 4 * D:(bq + 1) * 4 * D].rearrange(
                        "p (t d) -> p t d", d=D),
                    in_=xs.rearrange("(t p) d -> p t d", p=128)[:, bq * 4:(bq + 1) * 4, :]), 16)
            qS.inc(sync.dma_start(out=halo_x[:, :], in_=xh), 16)
        q_blk(p0_sync)

        def p0_gp(gpsimd):
            gq.bar(gpsimd)
            gq.inc(gpsimd.dma_start(
                out=xpw8[:, :].rearrange("p (k n) -> p k n", k=8),
                in_=xp_w.rearrange("(k p) n -> p k n", p=128)), 16)
        g_blk(p0_gp)

        def p0_v(vector):
            vector.memset(ones_f[:, :], 1.0)
            vector.memset(ones_b[:, :], 1.0)
            vS.inc(vector.memset(onesr_f[:, :], 1.0))
        v_blk(p0_v)

        def p0_s(scalar):
            qS.wait(scalar)
            sS.inc(scalar.activation(svec[:, :], alog[:, :], AF.Exp))
            sS.wait(scalar)
            sS.inc(scalar.activation(dvec[:, :], svec[:, :], AF.Exp, scale=-1.0))
        s_blk(p0_s)

        def p0_v2(vector):
            sS.wait(vector)
            vS.inc(vector.memset(dect[:, :], 1.0))
            vS.wait(vector)
            vS.inc(vector.tensor_scalar(dect[:, :], dect[:, :], dvec[:, :],
                                        None, ALU.mult))
            vS.wait(vector)
            vS.inc(vector.memset(
                dect[:, :].rearrange("p (b t) -> p b t", b=8)[:, :, 0:1], 0.0))
        v_blk(p0_v2)

        def p0_t(tensor):
            sS.wait(tensor); vS.wait(tensor); qS.wait(tensor)
            tS.inc(tensor.transpose(pb[0][0:1, 0:16], svec[:, :],
                                    idf[0:16, 0:16]))
        t_blk(p0_t)

        def p0_v3(vector):
            tS.wait(vector)
            vS.inc(vector.tensor_copy(srow[0:1, :], pb[0][0:1, 0:16]))
        v_blk(p0_v3)

        layernorm(A, A, list(range(16)), Dd, wb=None,
                  mcol=stat_m, icol=stat_i)
        layernorm(halo_x, halo_x, [0], halo_n1, wb=None,
                  mcol=hst[:, 0:1], icol=hst[:, 1:2])
        dev_tap("xn_p", A[:, :])

        load_bcast(ln1_w[None, :], lnw_bc)
        load_bcast(ln1_b[None, :], lnb_bc)
        layernorm(A, C, list(range(16)), Dd, wb=(lnw_bc, lnb_bc),
                  mcol=stat_m, icol=stat_i)
        layernorm(halo_x, halo_n1, [0], Dd, wb=(lnw_bc, lnb_bc),
                  mcol=hst[:, 0:1], icol=hst[:, 1:2], halo_mask=hm24[:, :])
        dev_tap("xn1_p", C[:, :])

        for kk in range(4):
            load_row(convw[kk:kk + 1, :])
            bcast_lnv(w_bc[:, kk * D:(kk + 1) * D])

        def conv_round(get_src, kk):
            pre_v = vS.v
            pre_g = gS.v
            pre_gq = gq.v

            def vf(vector):
                gS.wait(vector, pre_g); vS.wait(vector, pre_v)
                gq.wait(vector, pre_gq)
                for t in range(10):
                    vS.inc(vector.tensor_tensor(
                        tmp4[:, :], get_src(t), w_bc[:, kk * D:(kk + 1) * D],
                        ALU.mult))
                    vS.wait(vector)
                    vS.inc(vector.tensor_tensor(
                        A[:, t * D:(t + 1) * D], A[:, t * D:(t + 1) * D],
                        tmp4[:, :], ALU.add))
                    vS.wait(vector)
            v_blk(vf)

            def gf(gpsimd):
                gS.wait(gpsimd, pre_g); vS.wait(gpsimd, pre_v)
                gq.wait(gpsimd, pre_gq)
                for t in range(10, 16):
                    gS.inc(gpsimd.tensor_tensor(
                        tmp4b[:, :], get_src(t), w_bc[:, kk * D:(kk + 1) * D],
                        ALU.mult))
                    gS.wait(gpsimd)
                    gS.inc(gpsimd.tensor_tensor(
                        A[:, t * D:(t + 1) * D], A[:, t * D:(t + 1) * D],
                        tmp4b[:, :], ALU.add))
                    gS.wait(gpsimd)
            g_blk(gf)

        load_row(convb[None, :])

        def cb_t(tensor):
            qS.wait(tensor); vS.wait(tensor)
            tS.inc(tensor.matmul(pbig[:, 0:512], onesr_f[0:1, :],
                                 lnv[0:1, 0:512], start=True, stop=True))
            tS.inc(tensor.matmul(pbig[:, 512:1024], onesr_f[0:1, :],
                                 lnv[0:1, 512:1024], start=True, stop=True))
        t_blk(cb_t)

        def cb_v(vector):
            tS.wait(vector)
            for t in range(16):
                vS.inc(vector.tensor_tensor(
                    A[:, t * D:(t + 1) * D], A[:, t * D:(t + 1) * D],
                    pbig[:, 0:1024], ALU.add))
        v_blk(cb_v)

        conv_round(lambda t: C[:, t * D:(t + 1) * D], 3)
        for shv in (1, 2, 3):
            kk = 3 - shv

            def conv_g(gpsimd, shv=shv):
                gq.bar(gpsimd)
                vS.wait(gpsimd); gS.wait(gpsimd); gq.wait(gpsimd)
                gq.inc(gpsimd.dma_start(
                    out=Dd[shv:128, :].rearrange("p (t d) -> p t d", d=D),
                    in_=C[0:128 - shv, :].rearrange("p (t d) -> p t d", d=D)), 16)
                gq.inc(gpsimd.dma_start(
                    out=Dd[0:shv, :].rearrange("p (b c d) -> p b c d",
                                               b=8, c=2)[:, :, 1, :],
                    in_=C[128 - shv:128, :].rearrange("p (b c d) -> p b c d",
                                                      b=8, c=2)[:, :, 0, :]), 16)
                for bi in range(8):
                    gq.inc(gpsimd.dma_start(
                        out=Dd[0:shv, bi * 2 * D:bi * 2 * D + D],
                        in_=halo_n1[3 * bi + 3 - shv:3 * bi + 3, :]), 16)
            g_blk(conv_g)
            conv_round(lambda t: Dd[:, t * D:(t + 1) * D], kk)

        dev_tap("xloc_p", A[:, :])
        if BUILD_UPTO < 2:
            return nc, dev_outs

        load_bcast(lns_w[None, :], lnw_bc)
        load_bcast(lns_b[None, :], lnb_bc)
        layernorm(A, C, list(range(16)), Dd, wb=(lnw_bc, lnb_bc),
                  mcol=stat_m, icol=stat_i)
        dev_tap("xn2_p", C[:, :])
        transpose_fm(C, Dd, src_is_bf=True)
        sxp_fm = Abf[:, 0:8 * TOK]
        z_fm = Abf[:, 8 * TOK:16 * TOK]

        def inw_load(gpsimd):
            gq.bar(gpsimd)
            vS.wait(gpsimd)
            gq.inc(gpsimd.dma_start(
                out=C[:, :].rearrange("p (kk n) -> p kk n", kk=8),
                in_=in_w.rearrange("(kk p) n -> p kk n", p=128)), 16)
        g_blk(inw_load)

        for fc in range(16):
            def inw_t(tensor, fc=fc):
                gq.wait(tensor); vS.wait(tensor)
                if fc > 0:
                    sS.wait(tensor)
                for tq in range(4):
                    for kk in range(8):
                        tS.inc(tensor.matmul(
                            pb[tq][:, :],
                            C[:, kk * 2048 + fc * 128:kk * 2048 + (fc + 1) * 128],
                            Dd[:, kk * 2048 + tq * 512:kk * 2048 + (tq + 1) * 512],
                            start=(kk == 0), stop=(kk == 7)))
            t_blk(inw_t)

            def inw_s(scalar, fc=fc):
                tS.wait(scalar)
                func = AF.Silu if fc < 8 else AF.Sigmoid
                dst = sxp_fm if fc < 8 else z_fm
                dc = fc % 8
                for tq in range(4):
                    sS.inc(scalar.activation(
                        dst[:, dc * 2048 + tq * 512:dc * 2048 + (tq + 1) * 512],
                        pb[tq][:, :], func, bias=inb_sb[:, fc:fc + 1]))
            s_blk(inw_s)

        dev_tap("sxp_fm", sxp_fm[:, 0:2048])
        dev_tap("z_fm", z_fm[:, 0:2048])

        def u_t(tensor):
            sS.wait(tensor)
            for tq in range(4):
                for kk in range(8):
                    tS.inc(tensor.matmul(
                        pbig[0:16, tq * 512:(tq + 1) * 512],
                        xpw8[:, kk * 16:(kk + 1) * 16],
                        sxp_fm[:, kk * 2048 + tq * 512:kk * 2048 + (tq + 1) * 512],
                        start=(kk == 0), stop=(kk == 7)))
        t_blk(u_t)

        def u_v(vector):
            tS.wait(vector); qS.wait(vector)
            vS.inc(vector.tensor_scalar(u_fm[:, :], pbig[0:16, :],
                                        xpb_sb[:, :], None, ALU.add))
        v_blk(u_v)
        dev_tap("u_fm", u_fm[:, :])

        def scan1(vector):
            vS.wait(vector)
            vS.inc(vector.tensor_tensor_scan(hs_fm[:, :], dect[:, :],
                                             u_fm[:, :], 0.0,
                                             ALU.mult, ALU.add))
            vS.wait(vector)
            vS.inc(vector.tensor_copy(
                e16[:, :],
                hs_fm[:, :].rearrange("p (b t) -> p b t", b=8)[:, :, 255]))
        v_blk(scan1)
        dev_tap("hs1", hs_fm[:, :])
        if BUILD_UPTO < 3:
            return nc, dev_outs

        def ag1_q(sync):
            qS.bar(sync)
            vS.wait(sync)
            dst = ag1_in[0:1, 0:128].rearrange("o (b n) -> (o n) b", b=8)
            qS.inc(sync.dma_start(out=dst, in_=e16[:, :],
                                  allow_slow_non_contiguous=True), 16)
        q_blk(ag1_q)

        def ag1_g(gpsimd):
            qS.wait(gpsimd)
            cS.inc(gpsimd.collective_compute(
                "AllGather", ALU.bypass, replica_groups=[list(range(8))],
                ins=[ag1_in], outs=[ag1_out]))
        g_blk(ag1_g)

        def ag1_back(sync):
            qS.bar(sync)
            cS.wait(sync)
            qS.inc(sync.dma_start(out=eag[:, 0:128], in_=ag1_out[:, 0:128]), 16)
        q_blk(ag1_back)

        def carry_t(tensor):
            vS.wait(tensor)
            tS.inc(tensor.matmul(pb[0][0:8, 0:16], onesr_f[0:1, 0:8],
                                 srow[0:1, 0:16], start=True, stop=True))
        t_blk(carry_t)

        def carry_v(vector):
            tS.wait(vector); qS.wait(vector)
            vS.inc(vector.tensor_scalar(s8x16[:, :], pb[0][0:8, 0:16],
                                        ce8[:, :], -256.0, ALU.mult, ALU.mult))
        v_blk(carry_v)

        def carry_s(scalar):
            vS.wait(scalar)
            sS.inc(scalar.activation(wjn[:, :], s8x16[:, :], AF.Exp))
        s_blk(carry_s)

        def carry_v2(vector):
            sS.wait(vector)
            vS.inc(vector.tensor_scalar(wjn[:, :], wjn[:, :], cm8[:, :],
                                        None, ALU.mult))
            vS.wait(vector)
            vS.inc(vector.tensor_tensor(
                prod8[:, :].rearrange("p (b n) -> p b n", b=8),
                eag[:, 0:128].rearrange("p (b n) -> p b n", b=8),
                wjn[:, :].unsqueeze(1).broadcast_to([8, 8, 16]),
                ALU.mult))
        v_blk(carry_v2)

        def carry_t2(tensor):
            vS.wait(tensor)
            tS.inc(tensor.matmul(pb[1][:, 0:1], prod8[:, :], ones_f[0:8, 0:1],
                                 start=True, stop=True))
        t_blk(carry_t2)

        def carry_v3(vector):
            tS.wait(vector)
            vS.inc(vector.tensor_copy(c128[:, :], pb[1][:, 0:1]))
        v_blk(carry_v3)

        def carry_g(gpsimd):
            gq.bar(gpsimd)
            vS.wait(gpsimd)
            for bi in range(8):
                gq.inc(gpsimd.dma_start(out=ccorr[:, bi:bi + 1],
                                        in_=c128[bi * 16:(bi + 1) * 16, :]), 16)
        g_blk(carry_g)

        def carry_v4(vector):
            gq.wait(vector)
            vS.inc(vector.tensor_scalar(ccorr[:, :], ccorr[:, :], dvec[:, :],
                                        None, ALU.mult))
            vS.wait(vector)
            vS.inc(vector.tensor_tensor(
                u_fm[:, :].rearrange("p (b t) -> p b t", b=8)[:, :, 0],
                u_fm[:, :].rearrange("p (b t) -> p b t", b=8)[:, :, 0],
                ccorr[:, :], ALU.add))
            vS.wait(vector)
            vS.inc(vector.tensor_tensor_scan(hs_fm[:, :], dect[:, :],
                                             u_fm[:, :], 0.0,
                                             ALU.mult, ALU.add))
        v_blk(carry_v4)
        dev_tap("hs2", hs_fm[:, :])
        if BUILD_UPTO < 4:
            return nc, dev_outs

        g1_fm = Dd
        for dc in range(DC):
            def dmm_t(tensor, dc=dc):
                vS.wait(tensor); qS.wait(tensor); sS.wait(tensor)
                for tq in range(4):
                    tS.inc(tensor.matmul(
                        pb[tq][:, :], dmat[:, dc * 128:(dc + 1) * 128],
                        hs_fm[:, tq * 512:(tq + 1) * 512],
                        start=True, stop=True))
            t_blk(dmm_t)

            def dmm_v(vector, dc=dc):
                tS.wait(vector)
                for tq in range(4):
                    vS.inc(vector.tensor_tensor(
                        g1_fm[:, dc * 2048 + tq * 512:dc * 2048 + (tq + 1) * 512],
                        pb[tq][:, :],
                        z_fm[:, dc * 2048 + tq * 512:dc * 2048 + (tq + 1) * 512],
                        ALU.mult))
            v_blk(dmm_v)
        dev_tap("g1_fm", g1_fm[:, 0:2048])

        load_row(out_b[None, :])
        bcast_lnv(bias_bc)
        def outw_g(gpsimd):
            gq.bar(gpsimd)
            vS.wait(gpsimd)
            gq.inc(gpsimd.dma_start(
                out=C[:, 0:8 * D].rearrange("p (kk n) -> p kk n", kk=8),
                in_=out_w.rearrange("(kk p) n -> p kk n", p=128)), 16)
        g_blk(outw_g)

        y_bf = Abf[:, 8 * TOK:16 * TOK]
        for t in range(16):
            def y_t(tensor, t=t):
                gq.wait(tensor); vS.wait(tensor)
                if t > 0:
                    vS.wait(tensor)
                for dh in range(2):
                    for kk in range(8):
                        tS.inc(tensor.matmul(
                            pb[2 * (t % 2) + dh][:, :],
                            g1_fm[:, kk * 2048 + t * 128:kk * 2048 + (t + 1) * 128],
                            C[:, kk * D + dh * 512:kk * D + (dh + 1) * 512],
                            start=(kk == 0), stop=(kk == 7)))
            t_blk(y_t)

            def y_v(vector, t=t):
                tS.wait(vector)
                for dh in range(2):
                    vS.inc(vector.tensor_tensor(
                        y_bf[:, t * D + dh * 512:t * D + (dh + 1) * 512],
                        pb[2 * (t % 2) + dh][:, :],
                        bias_bc[:, dh * 512:(dh + 1) * 512], ALU.add))
            v_blk(y_v)

        def yspill_q(sync):
            qS.bar(sync)
            vS.wait(sync)
            src = y_bf.rearrange("p (b c d) -> p b c d", b=8, c=2)
            dst = y_sp.rearrange("(c t b) d -> t b c d", c=2, b=8)
            qS.inc(sync.dma_start(out=dst, in_=src), 16)
        q_blk(yspill_q)
        dev_tap("y_p", y_bf[:, :])

        if BUILD_UPTO < 5:
            return nc, dev_outs


        def p5_load(sync):
            qS.bar(sync)
            vS.wait(sync)
            qS.wait(sync)
            for bq in range(4):
                qS.inc(sync.dma_start(
                    out=A[:, bq * 4 * D:(bq + 1) * 4 * D].rearrange(
                        "p (t d) -> p t d", d=D),
                    in_=xb.rearrange("(t p) d -> p t d", p=128)[:, bq * 4:(bq + 1) * 4, :]), 16)
        q_blk(p5_load)

        layernorm(A, A, list(range(16)), Dd, wb=None, mcol=stat_m, icol=stat_i)
        dev_tap("xn_c", A[:, :])

        def p5_hi(vector):
            vS.wait(vector); gq.wait(vector)
            for t in range(16):
                vS.inc(vector.tensor_copy(Dd[:, t * D:(t + 1) * D],
                                          A[:, t * D:(t + 1) * D]))
        v_blk(p5_hi)
        transpose_fm(Dd, C, src_is_bf=True)

        wg_f = sb("wg_f", [128, 64])
        wg_hi = sb("wg_hi", [128, 64], BF16)
        wg_lo = sb("wg_lo", [128, 64], BF16)
        SCRbf = SCR[:, :].bitcast(BF16)
        lob = SCRbf[:, 0:2048]
        lg_sb = sb("lg_sb", [128, 128])
        io8bc = sb("io8bc", [128, 8])
        gbbc = sb("gbbc", [128, 8])
        m8 = sb("m8", [128, 1]); negm = sb("negm", [128, 1])
        sume = sb("sume", [128, 1]); gate8 = sb("gate8", [128, 16])
        exp8 = sb("exp8", [128, 8])
        oh1 = sb("oh1", [128, 8]); mi8 = sb("mi8", [128, 8])
        idx8 = sb("idx8", [128, 16])
        ohb_sb = sb("ohb_sb", [128, 16 * 8], BF16)
        cs_sb = sb("cs_sb", [8, 17])
        pst_sb = sb("pst_sb", [16, 8])
        cnt8 = sb("cnt8", [8, 1])
        cAG = sb("cAG", [8, 8])
        pgrow = sb("pgrow", [1, 8])
        pgbc = sb("pgbc", [128, 8])
        gb_row = sb("gb_row", [1, 8])
        tmp8 = sb("tmp8", [128, 8])
        psrow = sb("psrow", [1, 128])
        cntrow = psrow[0:1, 32:40]
        brow = psrow[0:1, 16:27]
        base_bc = AUX[:, 0:8]
        erow_f = psrow[0:1, 48:64]
        erow_i = sb("erow_i", [1, 16], mybir.dt.int32)
        prefbc_all = sb("prefbc_all", [128, 128])
        cum_sb = sb("cum_sb", [128, 128])
        posc = sb("posc", [128, 1])
        keepc = sb("keepc", [128, 1])
        kg_sb = sb("kg_sb", [128, 16])
        kg2_sb = sb("kg2_sb", [128, 16])
        slot_sb = sb("slot_sb", [128, 16])

        def gb_q(sync):
            qS.bar(sync)
            qS.inc(sync.dma_start(out=gb_row[:, :], in_=gate_b), 16)
        q_blk(gb_q)

        def io8_t(tensor):
            vS.wait(tensor); qS.wait(tensor)
            tS.inc(tensor.matmul(pb[0][:, 0:8], onesr_f[0:1, :], io8[0:1, :],
                                 start=True, stop=True))
            tS.inc(tensor.matmul(pb[1][:, 0:8], onesr_f[0:1, :],
                                 gb_row[0:1, :], start=True, stop=True))
        t_blk(io8_t)

        def io8_v(vector):
            tS.wait(vector)
            vS.inc(vector.tensor_copy(io8bc[:, :], pb[0][:, 0:8]))
            vS.inc(vector.tensor_copy(gbbc[:, :], pb[1][:, 0:8]))
        v_blk(io8_v)

        def router_logits(wgT_dram, hi_fm):
            def wg_ld(sync):
                qS.bar(sync)
                tS.wait(sync)
                qS.inc(sync.dma_start(
                    out=wg_f[:, :].rearrange("p (k n) -> p k n", k=8),
                    in_=wgT_dram.rearrange("(k p) n -> p k n", p=128)), 16)
            q_blk(wg_ld)

            def wg_split(vector):
                qS.wait(vector)
                vS.inc(vector.tensor_copy(wg_hi[:, :], wg_f[:, :]))
                vS.wait(vector)
                vS.inc(vector.tensor_tensor(wg_lo[:, :], wg_f[:, :],
                                            wg_hi[:, :], ALU.subtract))
            v_blk(wg_split)

            def lo_v(vector):
                vS.wait(vector)
                for t in range(16):
                    vS.inc(vector.tensor_tensor(
                        Dd[:, t * D:(t + 1) * D], A[:, t * D:(t + 1) * D],
                        Dd[:, t * D:(t + 1) * D], ALU.subtract))
            v_blk(lo_v)

            for kk in range(DC):
                def lo_tr(tensor, kk=kk):
                    vS.wait(tensor)
                    for n in range(16):
                        tS.inc(tensor.transpose(
                            pbig_bf[:, 2048 + n * 128:2048 + (n + 1) * 128],
                            Dd[:, n * D + kk * 128:n * D + (kk + 1) * 128],
                            idb[:, :]))
                t_blk(lo_tr)

                def lo_ev(vector, kk=kk):
                    tS.wait(vector)
                    for n in range(16):
                        vS.inc(vector.tensor_copy(
                            lob[:, n * 128:(n + 1) * 128],
                            pbig_bf[:, 2048 + n * 128:2048 + (n + 1) * 128]))
                v_blk(lo_ev)

                def lg_t(tensor, kk=kk):
                    vS.wait(tensor)
                    for n in range(16):
                        st = (kk == 0 and n == 0)
                        tS.inc(tensor.matmul(
                            pbig[:, n * 8:(n + 1) * 8],
                            hi_fm[:, kk * 2048 + n * 128:kk * 2048 + (n + 1) * 128],
                            wg_hi[:, kk * 8:(kk + 1) * 8],
                            start=st, stop=False))
                        tS.inc(tensor.matmul(
                            pbig[:, n * 8:(n + 1) * 8],
                            lob[:, n * 128:(n + 1) * 128],
                            wg_hi[:, kk * 8:(kk + 1) * 8],
                            start=False, stop=False))
                        last = (kk == DC - 1 and n == 15)
                        tS.inc(tensor.matmul(
                            pbig[:, n * 8:(n + 1) * 8],
                            hi_fm[:, kk * 2048 + n * 128:kk * 2048 + (n + 1) * 128],
                            wg_lo[:, kk * 8:(kk + 1) * 8],
                            start=False, stop=last))
                t_blk(lg_t)

            def lg_ev(vector):
                tS.wait(vector)
                vS.inc(vector.tensor_copy(lg_sb[:, :], pbig[:, 0:128]))
            v_blk(lg_ev)

        def router_chain(use_gb, cap, kgdst, want_slot, agin, agout):
            for t in range(16):
                def rv(vector, t=t):
                    vS.wait(vector); sS.wait(vector)
                    lg = lg_sb[:, t * 8:(t + 1) * 8]
                    if use_gb:
                        vS.inc(vector.tensor_tensor(lg, lg, gbbc[:, :],
                                                    ALU.add))
                        vS.wait(vector)
                    vS.inc(vector.reduce_max(m8[:, :], lg, axis=AX.X))
                    vS.wait(vector)
                    vS.inc(vector.tensor_scalar(negm[:, :], m8[:, :], -1.0,
                                                None, ALU.mult))
                v_blk(rv)

                def rs(scalar, t=t):
                    vS.wait(scalar)
                    sS.inc(scalar.activation(exp8[:, :],
                                             lg_sb[:, t * 8:(t + 1) * 8],
                                             AF.Exp, bias=negm[:, :],
                                             accum_out=sume[:, :]))
                s_blk(rs)

                def rv2(vector, t=t):
                    sS.wait(vector)
                    lg = lg_sb[:, t * 8:(t + 1) * 8]
                    vS.inc(vector.reciprocal(gate8[:, t:t + 1], sume[:, :]))
                    vS.inc(vector.tensor_scalar(oh1[:, :], lg, m8[:, :],
                                                None, ALU.is_ge))
                    vS.wait(vector)
                    vS.inc(vector.tensor_scalar(oh1[:, :], oh1[:, :],
                                                -99.0, 99.0,
                                                ALU.mult, ALU.add))
                    vS.wait(vector)
                    vS.inc(vector.tensor_tensor(mi8[:, :], io8bc[:, :],
                                                oh1[:, :], ALU.add))
                    vS.wait(vector)
                    vS.inc(vector.tensor_reduce(idx8[:, t:t + 1], mi8[:, :],
                                                axis=AX.X, op=ALU.min))
                    vS.wait(vector)
                    vS.inc(vector.tensor_scalar(oh1[:, :], io8bc[:, :],
                                                idx8[:, t:t + 1], None,
                                                ALU.is_equal))
                    vS.wait(vector)
                    vS.inc(vector.tensor_copy(ohb_sb[:, t * 8:(t + 1) * 8],
                                              oh1[:, :]))
                v_blk(rv2)

            def cum_t(tensor):
                vS.wait(tensor)
                for t in range(16):
                    tS.inc(tensor.matmul(
                        pbig[:, 512 + t * 8:512 + (t + 1) * 8],
                        tri[:, :], ohb_sb[:, t * 8:(t + 1) * 8],
                        start=(t == 0), stop=(t == 15)))
                for t in range(16):
                    tS.inc(tensor.matmul(
                        pb[0][0:8, t:t + 1], ohb_sb[:, t * 8:(t + 1) * 8],
                        ones_b[:, :], start=(t == 0), stop=(t == 15)))
            t_blk(cum_t)

            def cum_v(vector):
                tS.wait(vector)
                vS.inc(vector.tensor_copy(cum_sb[:, :], pbig[:, 512:640]))
                vS.inc(vector.tensor_copy(cs_sb[:, 0:16], pb[0][0:8, 0:16]))
                vS.wait(vector)
                vS.inc(vector.memset(cs_sb[:, 16:17], 0.0))
                for t in range(16):
                    vS.wait(vector)
                    vS.inc(vector.tensor_tensor(tmp8[0:8, 0:1],
                                                cs_sb[:, 16:17],
                                                cs_sb[:, t:t + 1], ALU.add))
                    vS.wait(vector)
                    vS.inc(vector.tensor_copy(cs_sb[:, t:t + 1],
                                              cs_sb[:, 16:17]))
                    vS.wait(vector)
                    vS.inc(vector.tensor_copy(cs_sb[:, 16:17], tmp8[0:8, 0:1]))
            v_blk(cum_v)

            def pst_t(tensor):
                vS.wait(tensor)
                tS.inc(tensor.transpose(pb[1][0:16, 0:8], cs_sb[:, 0:16],
                                        idf[0:8, 0:8]))
            t_blk(pst_t)

            def pst_v(vector):
                tS.wait(vector)
                vS.inc(vector.tensor_copy(pst_sb[:, :], pb[1][0:16, 0:8]))
                vS.inc(vector.tensor_copy(cnt8[:, :], cs_sb[:, 16:17]))
            v_blk(pst_v)

            def psr_g(gpsimd):
                gq.bar(gpsimd)
                vS.wait(gpsimd)
                gq.inc(gpsimd.dma_start(
                    out=psrow[0:1, :].rearrange("o (t e) -> o t e", t=16),
                    in_=pst_sb[:, :]), 16)
            g_blk(psr_g)

            def psr_t(tensor):
                gq.wait(tensor); vS.wait(tensor)
                tS.inc(tensor.matmul(pb[3][:, 0:128], onesr_f[0:1, :],
                                     psrow[0:1, :], start=True, stop=True))
            t_blk(psr_t)

            def psr_v(vector):
                tS.wait(vector)
                vS.inc(vector.tensor_copy(prefbc_all[:, :], pb[3][:, 0:128]))
            v_blk(psr_v)

            if want_slot:
                def sb_t(tensor):
                    vS.wait(tensor)
                    tS.inc(tensor.transpose(pb[2][0:1, 0:8], cnt8[:, :],
                                            idf[0:8, 0:8]))
                t_blk(sb_t)

                def sb_v(vector):
                    thr16 = psrow[0:1, 80:96]
                    ge16 = psrow[0:1, 64:80]
                    tS.wait(vector)
                    vS.inc(vector.tensor_copy(cntrow[:, :], pb[2][0:1, 0:8]))
                    vS.wait(vector)
                    vS.inc(vector.memset(brow[0:1, 0:1], 0.0))
                    vS.wait(vector)
                    vS.inc(vector.tensor_scalar(
                        thr16[:, :], io512[0:1, 0:16], 256.0, 1.0,
                        ALU.mult, ALU.add))
                    for e in range(7):
                        vS.wait(vector)
                        vS.inc(vector.tensor_tensor(
                            brow[0:1, 9:10], brow[0:1, e:e + 1],
                            cntrow[0:1, e:e + 1], ALU.add))
                        vS.wait(vector)
                        vS.inc(vector.tensor_tensor(
                            ge16[:, :],
                            brow[0:1, 9:10].broadcast_to([1, 16]),
                            thr16[:, :], ALU.is_ge))
                        vS.wait(vector)
                        vS.inc(vector.tensor_reduce(
                            brow[0:1, 10:11], ge16[:, :],
                            axis=AX.X, op=ALU.add))
                        vS.wait(vector)
                        vS.inc(vector.tensor_scalar(
                            brow[0:1, e + 1:e + 2], brow[0:1, 10:11],
                            256.0, None, ALU.mult))
                v_blk(sb_v)

                def sb_t2(tensor):
                    vS.wait(tensor)
                    tS.inc(tensor.matmul(pb[2][:, 0:8], onesr_f[0:1, :],
                                         brow[0:1, 0:8], start=True, stop=True))
                    tS.inc(tensor.matmul(pb[3][0:16, 0:8], onesr_f[0:1, 0:16],
                                         brow[0:1, 0:8], start=True, stop=True))
                t_blk(sb_t2)

                def sb_v2(vector):
                    tS.wait(vector)
                    vS.inc(vector.tensor_copy(base_bc[:, :], pb[2][:, 0:8]))
                    vS.wait(vector)
                    vS.inc(vector.tensor_scalar(ccorr[:, 0:8], pb[3][0:16, 0:8],
                                                s256[:, :], None, ALU.is_le))
                    vS.wait(vector)
                    vS.inc(vector.reduce_sum(e16[:, 0:1], ccorr[:, 0:8],
                                             axis=AX.X))
                    vS.wait(vector)
                    vS.inc(vector.tensor_scalar(e16[:, 0:1], e16[:, 0:1],
                                                1.0, -1.0, ALU.mult, ALU.add))
                v_blk(sb_v2)

                def sb_t3(tensor):
                    vS.wait(tensor)
                    tS.inc(tensor.transpose(pb[2][0:1, 0:16], e16[:, 0:1],
                                            idf[0:16, 0:16]))
                t_blk(sb_t3)

                def sb_v3(vector):
                    tS.wait(vector)
                    vS.inc(vector.tensor_copy(erow_f[:, :], pb[2][0:1, 0:16]))
                    vS.wait(vector)
                    vS.inc(vector.tensor_copy(erow_i[:, :], erow_f[:, :]))
                v_blk(sb_v3)

            def ag_q(sync):
                qS.bar(sync)
                vS.wait(sync)
                qS.inc(sync.dma_start(out=agin[0:1, 0:8], in_=cnt8[:, :],
                                      allow_slow_non_contiguous=True), 16)
            q_blk(ag_q)

            def ag_g(gpsimd):
                qS.wait(gpsimd)
                cS.inc(gpsimd.collective_compute(
                    "AllGather", ALU.bypass, replica_groups=[list(range(8))],
                    ins=[agin], outs=[agout]))
            g_blk(ag_g)

            def ag_b(sync):
                qS.bar(sync)
                cS.wait(sync)
                qS.inc(sync.dma_start(out=cAG[:, :], in_=agout), 16)
            q_blk(ag_b)

            def pg_t(tensor):
                qS.wait(tensor); vS.wait(tensor)
                tS.inc(tensor.matmul(pb[2][0:1, 0:8], cm8[:, :], cAG[:, :],
                                     start=True, stop=True))
            t_blk(pg_t)

            def pg_v(vector):
                tS.wait(vector)
                vS.inc(vector.tensor_copy(pgrow[:, :], pb[2][0:1, 0:8]))
            v_blk(pg_v)

            def pg_t2(tensor):
                vS.wait(tensor)
                tS.inc(tensor.matmul(pb[2][:, 0:8], onesr_f[0:1, :],
                                     pgrow[0:1, :], start=True, stop=True))
            t_blk(pg_t2)

            def pg_v2(vector):
                tS.wait(vector)
                vS.inc(vector.tensor_copy(pgbc[:, :], pb[2][:, 0:8]))
            v_blk(pg_v2)

            for t in range(16):
                def pc_v(vector, t=t):
                    vS.wait(vector)
                    vS.inc(vector.tensor_tensor(
                        tmp8[:, :], cum_sb[:, t * 8:(t + 1) * 8],
                        prefbc_all[:, t * 8:(t + 1) * 8], ALU.add))
                    vS.wait(vector)
                    if want_slot:
                        vS.inc(vector.tensor_tensor(
                            mi8[:, :], ohb_sb[:, t * 8:(t + 1) * 8],
                            tmp8[:, :], ALU.mult))
                        vS.wait(vector)
                        vS.inc(vector.reduce_sum(posc[:, :], mi8[:, :],
                                                 axis=AX.X))
                        vS.wait(vector)
                        vS.inc(vector.tensor_scalar(posc[:, :], posc[:, :],
                                                    1.0, -1.0, ALU.mult,
                                                    ALU.add))
                        vS.wait(vector)
                        vS.inc(vector.tensor_tensor(
                            mi8[:, :], ohb_sb[:, t * 8:(t + 1) * 8],
                            base_bc[:, :], ALU.mult))
                        vS.wait(vector)
                        vS.inc(vector.reduce_sum(keepc[:, :], mi8[:, :],
                                                 axis=AX.X))
                        vS.wait(vector)
                        vS.inc(vector.tensor_tensor(slot_sb[:, t:t + 1],
                                                    posc[:, :], keepc[:, :],
                                                    ALU.add))
                        vS.wait(vector)
                    vS.inc(vector.tensor_tensor(tmp8[:, :], tmp8[:, :],
                                                pgbc[:, :], ALU.add))
                    vS.wait(vector)
                    vS.inc(vector.tensor_tensor(
                        mi8[:, :], ohb_sb[:, t * 8:(t + 1) * 8],
                        tmp8[:, :], ALU.mult))
                    vS.wait(vector)
                    vS.inc(vector.reduce_sum(posc[:, :], mi8[:, :], axis=AX.X))
                    vS.wait(vector)
                    vS.inc(vector.tensor_scalar(posc[:, :], posc[:, :],
                                                1.0, -1.0, ALU.mult, ALU.add))
                    vS.wait(vector)
                    vS.inc(vector.tensor_scalar(keepc[:, :], posc[:, :],
                                                float(cap), None, ALU.is_lt))
                    vS.wait(vector)
                    vS.inc(vector.tensor_tensor(kgdst[:, t:t + 1],
                                                keepc[:, :],
                                                gate8[:, t:t + 1], ALU.mult))
                v_blk(pc_v)

        router_logits(Wg, C)
        dev_tap("lg0", lg_sb[:, :])
        router_chain(False, CAP0, kg_sb, False, ag2_in, ag2_out)
        dev_tap("kg0", kg_sb[:, :])

        g_sp = nc.dram_tensor("g_sp", [TOK, D], BF16, kind="Internal").ap()
        load_bcast(be[None, :], bias_bc)

        def we_load(gpsimd):
            gq.bar(gpsimd)
            vS.wait(gpsimd)
            gq.inc(gpsimd.dma_start(
                out=Dd[:, 0:8 * D].rearrange("p (kk n) -> p kk n", kk=8),
                in_=We.rearrange("(kk p) n -> p kk n", p=128)), 16)
        g_blk(we_load)

        gstage = [SCRbf[:, 0:2048], SCRbf[:, 6144:8192]]
        for t in range(16):
            def we_t(tensor, t=t):
                gq.wait(tensor); vS.wait(tensor)
                if t > 1:
                    qS.wait(tensor)
                for dh in range(2):
                    for kk in range(8):
                        tS.inc(tensor.matmul(
                            pb[2 * (t % 2) + dh][:, :],
                            C[:, kk * 2048 + t * 128:kk * 2048 + (t + 1) * 128],
                            Dd[:, kk * D + dh * 512:kk * D + (dh + 1) * 512],
                            start=(kk == 0), stop=(kk == 7)))
            t_blk(we_t)

            def we_v(vector, t=t):
                tS.wait(vector); sS.wait(vector)
                for dh in range(2):
                    vS.inc(vector.tensor_tensor(
                        tmp4[:, dh * 512:(dh + 1) * 512],
                        pb[2 * (t % 2) + dh][:, :],
                        bias_bc[:, dh * 512:(dh + 1) * 512], ALU.add))
            v_blk(we_v)

            def we_s(scalar, t=t):
                vS.wait(scalar)
                sS.inc(scalar.activation(gstage[t % 2][:, 0:D],
                                         tmp4[:, :], AF.Silu,
                                         scale=kg_sb[:, t:t + 1]))
            s_blk(we_s)

            def we_q(sync, t=t):
                qS.bar(sync)
                sS.wait(sync)
                qS.inc(sync.dma_start(
                    out=g_sp[t * 128:(t + 1) * 128, :],
                    in_=gstage[t % 2][:, 0:D]), 16)
            q_blk(we_q)
        dev_tap("xn_c2", A[:, :])

        if BUILD_UPTO < 6:
            return nc, dev_outs

        load_bcast(ln1_w[None, :], lnw_bc)
        load_bcast(ln1_b[None, :], lnb_bc)
        layernorm(A, C, list(range(16)), Dd, wb=(lnw_bc, lnb_bc),
                  mcol=stat_m, icol=stat_i)
        def ch_v(vector):
            vS.wait(vector)
            vS.inc(vector.memset(halo_n1[:, :], 0.0))
        v_blk(ch_v)

        for kk in range(4):
            load_row(convw[kk:kk + 1, :])
            bcast_lnv(w_bc[:, kk * D:(kk + 1) * D])
        load_row(convb[None, :])

        def cb2_t(tensor):
            qS.wait(tensor); vS.wait(tensor)
            tS.inc(tensor.matmul(pbig[:, 0:512], onesr_f[0:1, :],
                                 lnv[0:1, 0:512], start=True, stop=True))
            tS.inc(tensor.matmul(pbig[:, 512:1024], onesr_f[0:1, :],
                                 lnv[0:1, 512:1024], start=True, stop=True))
        t_blk(cb2_t)

        def cb2_v(vector):
            tS.wait(vector)
            for t in range(16):
                vS.inc(vector.tensor_tensor(
                    A[:, t * D:(t + 1) * D], A[:, t * D:(t + 1) * D],
                    pbig[:, 0:1024], ALU.add))
        v_blk(cb2_v)

        conv_round(lambda t: C[:, t * D:(t + 1) * D], 3)
        for shv in (1, 2, 3):
            kk = 3 - shv

            def conv_g2(gpsimd, shv=shv):
                gq.bar(gpsimd)
                vS.wait(gpsimd); gS.wait(gpsimd); gq.wait(gpsimd)
                gq.inc(gpsimd.dma_start(
                    out=Dd[shv:128, :].rearrange("p (t d) -> p t d", d=D),
                    in_=C[0:128 - shv, :].rearrange("p (t d) -> p t d", d=D)), 16)
                gq.inc(gpsimd.dma_start(
                    out=Dd[0:shv, D:16 * D].rearrange("p (t d) -> p t d", d=D),
                    in_=C[128 - shv:128, 0:15 * D].rearrange(
                        "p (t d) -> p t d", d=D)), 16)
                gq.inc(gpsimd.dma_start(
                    out=Dd[0:shv, 0:D],
                    in_=halo_n1[3 - shv:3, :]), 16)
            g_blk(conv_g2)
            conv_round(lambda t: Dd[:, t * D:(t + 1) * D], kk)

        dev_tap("xloc_c", A[:, :])

        def yp_q(sync):
            qS.bar(sync)
            vS.wait(sync); gS.wait(sync)
            for bq in range(2):
                qS.inc(sync.dma_start(
                    out=C[:, bq * 8 * D:(bq + 1) * 8 * D].rearrange(
                        "p (t d) -> p t d", d=D),
                    in_=y_sp.rearrange("(t p) d -> p t d", p=128)[:, bq * 8:(bq + 1) * 8, :]), 16)
        q_blk(yp_q)

        def x1_v(vector):
            qS.wait(vector); vS.wait(vector)
            for t in range(16):
                vS.inc(vector.tensor_tensor(
                    A[:, t * D:(t + 1) * D], A[:, t * D:(t + 1) * D],
                    C[:, t * D:(t + 1) * D], ALU.add))
        v_blk(x1_v)
        dev_tap("x1_c", A[:, :])

        load_bcast(ln2_w[None, :], lnw_bc)
        load_bcast(ln2_b[None, :], lnb_bc)
        layernorm(A, C, list(range(16)), Dd, wb=(lnw_bc, lnb_bc),
                  mcol=stat_m, icol=stat_i)

        def w1_load(gpsimd):
            gq.bar(gpsimd)
            vS.wait(gpsimd)
            gq.inc(gpsimd.dma_start(
                out=Dd[:, :].rearrange("p (kk n) -> p kk n", kk=8),
                in_=mlp_w1.rearrange("(kk p) n -> p kk n", p=128)), 16)
        g_blk(w1_load)
        load_row(mlp_b2[None, :])
        bcast_lnv(bias_bc)

        ln2g = AUXbf[:, 0:4096]
        h1g = AUXbf[:, 4096:4096 + 8192]
        w2psum = [pb[0][:, :], pb[1][:, :], pb[2][:, :], pb[3][:, :],
                  pbig[:, 0:512], pbig[:, 512:1024],
                  pbig[:, 1024:1536], pbig[:, 1536:2048]]
        for tg in range(4):
            for rnd in range(2):
                def tg_tr(tensor, tg=tg, rnd=rnd):
                    vS.wait(tensor); sS.wait(tensor); gq.wait(tensor)
                    for n in range(16):
                        ti, dc = divmod(rnd * 16 + n, 8)
                        t = tg * 4 + ti
                        tS.inc(tensor.transpose(
                            pbig_bf[:, 2048 + n * 128:2048 + (n + 1) * 128],
                            C[:, t * D + dc * 128:t * D + (dc + 1) * 128],
                            idb[:, :]))
                t_blk(tg_tr)

                def tg_ev(vector, tg=tg, rnd=rnd):
                    tS.wait(vector)
                    for n in range(16):
                        ti, dc = divmod(rnd * 16 + n, 8)
                        vS.inc(vector.tensor_copy(
                            ln2g[:, dc * 512 + ti * 128:dc * 512 + (ti + 1) * 128],
                            pbig_bf[:, 2048 + n * 128:2048 + (n + 1) * 128]))
                v_blk(tg_ev)

            for fc in range(16):
                def h1_t(tensor, tg=tg, fc=fc):
                    vS.wait(tensor)
                    if fc > 0:
                        sS.wait(tensor)
                    for kk in range(8):
                        tS.inc(tensor.matmul(
                            pb[fc % 2][:, :],
                            Dd[:, kk * 2048 + fc * 128:kk * 2048 + (fc + 1) * 128],
                            ln2g[:, kk * 512:(kk + 1) * 512],
                            start=(kk == 0), stop=(kk == 7)))
                t_blk(h1_t)

                def h1_s(scalar, tg=tg, fc=fc):
                    tS.wait(scalar)
                    sS.inc(scalar.activation(
                        h1g[:, fc * 512:(fc + 1) * 512],
                        pb[fc % 2][:, :], AF.Gelu,
                        bias=m1b_sb[:, fc:fc + 1]))
                s_blk(h1_s)

            for kk in range(16):
                def w2_ld(gpsimd, tg=tg, kk=kk):
                    gq.bar(gpsimd)
                    sS.wait(gpsimd)
                    if kk > 1 or tg > 0:
                        tS.wait(gpsimd)
                    gq.inc(gpsimd.dma_start(
                        out=(wka if kk % 2 == 0 else wkb)[:, 0:D],
                        in_=mlp_w2[kk * 128:(kk + 1) * 128, :]), 16)
                g_blk(w2_ld)

                def w2_t(tensor, tg=tg, kk=kk):
                    gq.wait(tensor); sS.wait(tensor)
                    if kk == 0:
                        vS.wait(tensor)
                    wt = (wka if kk % 2 == 0 else wkb)
                    for ti in range(4):
                        for dh in range(2):
                            tS.inc(tensor.matmul(
                                w2psum[ti * 2 + dh],
                                h1g[:, kk * 512 + ti * 128:kk * 512 + (ti + 1) * 128],
                                wt[:, dh * 512:(dh + 1) * 512],
                                start=(kk == 0), stop=(kk == 15)))
                t_blk(w2_t)

            def w2_v(vector, tg=tg):
                tS.wait(vector)
                for ti in range(4):
                    t = tg * 4 + ti
                    for dh in range(2):
                        vS.inc(vector.tensor_tensor(
                            tmp4[:, dh * 512:(dh + 1) * 512],
                            w2psum[ti * 2 + dh],
                            bias_bc[:, dh * 512:(dh + 1) * 512], ALU.add))
                    vS.wait(vector)
                    vS.inc(vector.tensor_tensor(
                        A[:, t * D:(t + 1) * D], A[:, t * D:(t + 1) * D],
                        tmp4[:, :], ALU.add))
                    vS.wait(vector)
            v_blk(w2_v)
        dev_tap("smix_c", A[:, :])
        if BUILD_UPTO < 7:
            return nc, dev_outs

        for t in range(16):
            def x2_q(sync, t=t):
                qS.bar(sync)
                vS.wait(sync); sS.wait(sync)
                qS.inc(sync.dma_start(out=tmp4[:, :],
                                      in_=xb[t * 128:(t + 1) * 128, :]), 16)
                qS.inc(sync.dma_start(out=wka[:, 0:D],
                                      in_=g_sp[t * 128:(t + 1) * 128, :]), 16)
            q_blk(x2_q)

            def x2_v(vector, t=t):
                qS.wait(vector); vS.wait(vector)
                vS.inc(vector.tensor_tensor(tmp4b[:, :],
                                            A[:, t * D:(t + 1) * D],
                                            wka[:, 0:D], ALU.mult))
                vS.wait(vector)
                vS.inc(vector.tensor_tensor(A[:, t * D:(t + 1) * D],
                                            tmp4[:, :], tmp4b[:, :], ALU.add))
            v_blk(x2_v)

            def x2_s(sync, t=t):
                qS.bar(sync)
                vS.wait(sync)
                qS.inc(sync.dma_start(out=x2_sp[t * 128:(t + 1) * 128, :],
                                      in_=A[:, t * D:(t + 1) * D]), 16)
            q_blk(x2_s)
        dev_tap("x2_c", A[:, :])

        layernorm(A, A, list(range(16)), Dd, wb=None, mcol=stat_m,
                  icol=stat_i)
        dev_tap("xn1_L1", A[:, :])
        if BUILD_UPTO < 8:
            return nc, dev_outs


        xn1tok_sp = nc.dram_tensor("xn1tok_sp", [TOK, D], BF16,
                                   kind="Internal").ap()
        Qfm = Abf[:, 0:16384]
        Kfm = Abf[:, 16384:32768]
        Vtm = AUXbf[:, 0:16384]
        Ebuf = [SCRbf[:, 0:2048], SCRbf[:, 2048:4096]]
        cum_bf = cum_sb[:, :].bitcast(BF16)
        vaug = [cum_bf[:, 0:65], cum_bf[:, 128:193]]
        lst = SCR[64:65, 2048:4096]
        au_st = SCR[0:64, 2048:4096].bitcast(BF16)
        od_st = SCR[0:64, 3072:4096].bitcast(BF16)

        ones65 = prefbc_all
        nb8 = stat_m[:, 0:1]

        def p8_v0(vector):
            vS.wait(vector)
            vS.inc(vector.memset(nb8, -8.0))

            for t in range(16):
                vS.inc(vector.tensor_copy(Dd[:, t * D:(t + 1) * D],
                                          A[:, t * D:(t + 1) * D]))
        v_blk(p8_v0)

        def p8_spill(sync):
            qS.bar(sync)
            vS.wait(sync)
            qS.inc(sync.dma_start(
                out=xn1tok_sp.rearrange("(t p) d -> p t d", p=128),
                in_=Dd[:, :].rearrange("p (t d) -> p t d", d=D)), 16)
        q_blk(p8_spill)

        transpose_fm(Dd, C, src_is_bf=True)

        router_logits(gate_wT, C)
        dev_tap("lg1", lg_sb[:, :])
        router_chain(True, CAP1, kg2_sb, True, ag3_in, ag3_out)
        dev_tap("kg1", kg2_sb[:, :])
        dev_tap("slot1", slot_sb[:, :])

        def qk_load(gpsimd):
            gq.bar(gpsimd)
            vS.wait(gpsimd); tS.wait(gpsimd)
            gq.inc(gpsimd.dma_start(
                out=Dd[:, 0:8192].rearrange("p (kk n) -> p kk n", kk=8),
                in_=Wq.rearrange("(kk p) n -> p kk n", p=128)), 16)
            gq.inc(gpsimd.dma_start(
                out=Dd[:, 8192:16384].rearrange("p (kk n) -> p kk n", kk=8),
                in_=Wk.rearrange("(kk p) n -> p kk n", p=128)), 16)
        g_blk(qk_load)

        for which, dst in ((0, Qfm), (1, Kfm)):
            for fc in range(8):
                def qk_t(tensor, which=which, fc=fc):
                    gq.wait(tensor); vS.wait(tensor)
                    for tq in range(4):
                        for kk in range(8):
                            tS.inc(tensor.matmul(
                                pb[tq][:, :],
                                Dd[:, which * 8192 + kk * 1024 + fc * 128:
                                   which * 8192 + kk * 1024 + (fc + 1) * 128],
                                C[:, kk * 2048 + tq * 512:kk * 2048 + (tq + 1) * 512],
                                start=(kk == 0), stop=(kk == 7)))
                t_blk(qk_t)

                def qk_v(vector, which=which, fc=fc):
                    tS.wait(vector)
                    for tq in range(4):
                        vS.inc(vector.tensor_copy(
                            dst[:, fc * 2048 + tq * 512:fc * 2048 + (tq + 1) * 512],
                            pb[tq][:, :]))
                v_blk(qk_v)

        dev_tap("qfm0", Qfm[:, 0:2048])
        dev_tap("kfm0", Kfm[:, 0:2048])
        def wv_load(gpsimd):
            gq.bar(gpsimd)
            sS.wait(gpsimd)
            gq.inc(gpsimd.dma_start(
                out=Dd[:, 0:8192].rearrange("p (kk n) -> p kk n", kk=8),
                in_=Wv.rearrange("(kk p) n -> p kk n", p=128)), 16)
        g_blk(wv_load)

        for t in range(16):
            def v_t(tensor, t=t):
                gq.wait(tensor); vS.wait(tensor)
                for dh in range(2):
                    for kk in range(8):
                        tS.inc(tensor.matmul(
                            pb[2 * (t % 2) + dh][:, :],
                            C[:, kk * 2048 + t * 128:kk * 2048 + (t + 1) * 128],
                            Dd[:, kk * 1024 + dh * 512:kk * 1024 + (dh + 1) * 512],
                            start=(kk == 0), stop=(kk == 7)))
            t_blk(v_t)

            def v_v(vector, t=t):
                tS.wait(vector)
                for dh in range(2):
                    vS.inc(vector.tensor_copy(
                        Vtm[:, t * 1024 + dh * 512:t * 1024 + (dh + 1) * 512],
                        pb[2 * (t % 2) + dh][:, :]))
            v_blk(v_v)

        def p8_v1(vector):
            vS.wait(vector)
            vS.inc(vector.memset(vaug[0][:, 64:65], 1.0))
            vS.inc(vector.memset(vaug[1][:, 64:65], 1.0))
            vS.inc(vector.memset(ones65[64:65, :], 1.0))
        v_blk(p8_v1)
        sS_exp = {}
        for h in range(16):
            hc, ho = divmod(h, 2)
            for kt in range(16):
                q0 = 128 * kt
                if kt >= 8:
                    base = (kt % 2) * 1024
                    W = 2048 - q0
                    chunks = [(ls, min(512, W - ls))
                              for ls in range(0, W, 512)]
                    sdep = sS_exp.get((h, kt - 2)) if kt >= 9 else None
                else:
                    base = None
                    chunks = []
                    for c in range(q0 // 512, 4):
                        gs = max(q0, 512 * c)
                        chunks.append((gs - q0, 512 * (c + 1) - gs))
                    sdep = None

                def s_t(tensor, hc=hc, ho=ho, kt=kt, q0=q0, base=base,
                        chunks=chunks, sdep=sdep, h=h):
                    vS.wait(tensor)
                    if base is None or kt == 8:
                        sS.wait(tensor)
                    elif sdep:
                        sS.wait(tensor, sdep)
                    for (ls, w) in chunks:
                        dst0 = (q0 + ls) if base is None else (base + ls)
                        tS.inc(tensor.matmul(
                            pbig[:, dst0:dst0 + w],
                            Kfm[ho * 64:(ho + 1) * 64,
                                hc * 2048 + kt * 128:hc * 2048 + (kt + 1) * 128],
                            Qfm[ho * 64:(ho + 1) * 64,
                                hc * 2048 + q0 + ls:hc * 2048 + q0 + ls + w],
                            start=True, stop=True))
                t_blk(s_t)

                def s_m(vector, kt=kt, q0=q0, base=base):
                    tS.wait(vector)
                    d0 = q0 if base is None else base
                    vS.inc(vector.tensor_tensor(
                        pbig[:, d0:d0 + 128], pbig[:, d0:d0 + 128],
                        caus[:, :], ALU.add))
                v_blk(s_m)

                def s_e(scalar, kt=kt, q0=q0, base=base):
                    tS.wait(scalar); vS.wait(scalar)
                    d0 = q0 if base is None else base
                    sS.inc(scalar.activation(
                        Ebuf[kt % 2][:, 0:2048 - q0],
                        pbig[:, d0:d0 + 2048 - q0], AF.Exp,
                        bias=nb8, scale=0.125))
                s_blk(s_e)
                sS_exp[(h, kt)] = sS.v

                def va_v(vector, h=h, kt=kt):
                    tS.wait(vector)
                    vS.inc(vector.tensor_copy(
                        vaug[kt % 2][:, 0:64],
                        Vtm[:, kt * 1024 + h * 64:kt * 1024 + (h + 1) * 64]))
                v_blk(va_v)

                avch = []
                for c in range(q0 // 512, 4):
                    gs = max(q0, 512 * c)
                    avch.append((c, gs, 512 * (c + 1) - gs))

                def av_t(tensor, h=h, kt=kt, q0=q0, chunks=avch):
                    sS.wait(tensor); vS.wait(tensor)
                    for (c, gs, w) in chunks:
                        last = (kt == min(15, 4 * c + 3))
                        tS.inc(tensor.matmul(
                            pb[c][0:65, gs - 512 * c:gs - 512 * c + w],
                            vaug[kt % 2][:, :],
                            Ebuf[kt % 2][:, gs - q0:gs - q0 + w],
                            start=(kt == 0), stop=last))
                t_blk(av_t)

            def h_v1(vector, h=h):
                tS.wait(vector); vS.wait(vector); gq.wait(vector)
                for qc in range(4):
                    vS.inc(vector.tensor_copy(
                        lst[:, qc * 512:(qc + 1) * 512], pb[qc][64:65, :]))
                vS.wait(vector)
                vS.inc(vector.reciprocal(lst, lst))
                vS.wait(vector)
                for qc in range(4):
                    vS.inc(vector.tensor_copy(
                        au_st[:, qc * 512:(qc + 1) * 512], pb[qc][0:64, :]))
            v_blk(h_v1)

            for qc in range(4):
                def inv_t(tensor, qc=qc, h=h):
                    vS.wait(tensor)
                    tS.inc(tensor.matmul(
                        pbig[0:64, 0:512], ones65[64:65, 0:64],
                        lst[:, qc * 512:(qc + 1) * 512],
                        start=True, stop=True))
                t_blk(inv_t)

                def inv_v(vector, qc=qc, h=h, hc=hc, ho=ho):
                    tS.wait(vector)
                    if ho == 0:
                        vS.inc(vector.tensor_tensor(
                            C[0:64, hc * 2048 + qc * 512:hc * 2048 + (qc + 1) * 512],
                            au_st[:, qc * 512:(qc + 1) * 512],
                            pbig[0:64, 0:512], ALU.mult))
                    else:
                        vS.inc(vector.tensor_tensor(
                            od_st[:, qc * 512:(qc + 1) * 512],
                            au_st[:, qc * 512:(qc + 1) * 512],
                            pbig[0:64, 0:512], ALU.mult))
                v_blk(inv_v)

            if ho == 1:
                def od_g(gpsimd, hc=hc):
                    gq.bar(gpsimd)
                    vS.wait(gpsimd)
                    gq.inc(gpsimd.dma_start(
                        out=C[64:128, hc * 2048:(hc + 1) * 2048],
                        in_=od_st[:, :]), 16)
                g_blk(od_g)

        dev_tap("vtm0", Vtm[:, 0:2048])
        dev_tap("attnC", C[:, 0:2048])
        def wr_load(gpsimd):
            gq.bar(gpsimd)
            tS.wait(gpsimd)
            gq.inc(gpsimd.dma_start(
                out=Dd[:, 0:8192].rearrange("p (kk n) -> p kk n", kk=8),
                in_=Wr.rearrange("(kk p) n -> p kk n", p=128)), 16)
        g_blk(wr_load)

        for t in range(16):
            def wr_t(tensor, t=t):
                gq.wait(tensor); gS.wait(tensor); vS.wait(tensor)
                for dh in range(2):
                    for kk in range(8):
                        tS.inc(tensor.matmul(
                            pb[2 * (t % 2) + dh][:, :],
                            C[:, kk * 2048 + t * 128:kk * 2048 + (t + 1) * 128],
                            Dd[:, kk * 1024 + dh * 512:kk * 1024 + (dh + 1) * 512],
                            start=(kk == 0), stop=(kk == 7)))
            t_blk(wr_t)

            def wr_v(vector, t=t):
                tS.wait(vector)
                for dh in range(2):
                    vS.inc(vector.tensor_copy(
                        AUXbf[:, t * 1024 + dh * 512:t * 1024 + (dh + 1) * 512],
                        pb[2 * (t % 2) + dh][:, :]))
            v_blk(wr_v)

        def s1_spill(sync):
            qS.bar(sync)
            vS.wait(sync)
            qS.inc(sync.dma_start(
                out=s1_sp.rearrange("(t p) d -> p t d", p=128),
                in_=AUXbf[:, :].rearrange("p (t d) -> p t d", d=D)), 16)
        q_blk(s1_spill)
        dev_tap("s1_c", AUXbf[:, :])
        if BUILD_UPTO < 9:
            return nc, dev_outs

        io512bc = SCR[:, 3072:3584]
        ptile = [SCRbf[:, 0:512], SCRbf[:, 512:1024]]
        slotloc = SCR[:, 3584:3585]
        b1sd = [SCR[:, 3600:3632], SCR[:, 3632:3664]]
        b2r = SCRbf[:, 4096:5120][0:8, :]
        ct_tile = SCRbf[:, 5120:5248]
        srow128 = SCR[0:1, 2816:2944]
        b2tok = tmp4b
        tS_p8, vS_p8, sS_p8, qS_p8 = tS.v, vS.v, sS.v, qS.v
        w1sem = [Sem(nc, ctx, f"w1s{r}") for r in range(8)]
        w2sem = [Sem(nc, ctx, f"w2s{r}") for r in range(8)]
        b1sem = [Sem(nc, ctx, "b1s0"), Sem(nc, ctx, "b1s1")]
        eoq = Sem(nc, ctx, "eoq")
        b2row = SCRbf[0:1, 5120:6144]
        b2bc_sb = SCRbf[:, 6144:7168]

        def p9_ld(gpsimd):
            gq.bar(gpsimd)
            qS.wait(gpsimd); sS.wait(gpsimd); vS.wait(gpsimd)
            gq.inc(gpsimd.dma_start(
                out=Dd[:, :].rearrange("p (t d) -> p t d", d=D),
                in_=xn1tok_sp.rearrange("(t p) d -> p t d", p=128)), 16)
            gq.inc(gpsimd.dma_start(out=b2r, in_=b2), 16)
        g_blk(p9_ld)

        def p9_t(tensor):
            vS.wait(tensor); qS.wait(tensor)
            tS.inc(tensor.matmul(pb[0][:, :], onesr_f[0:1, :],
                                 io512[0:1, :], start=True, stop=True))
        t_blk(p9_t)

        def p9_v(vector):
            tS.wait(vector)
            vS.inc(vector.tensor_copy(io512bc[:, :], pb[0][:, :]))
        v_blk(p9_v)

        w1v = w1.rearrange("e h d f -> (e h d) f")
        w2v = w2.rearrange("e f n -> (e f) n")
        b1v = b1r_d.rearrange("e p c -> (e p) c")
        SCRi = SCR[:, :].bitcast(mybir.dt.int32)
        tblw1 = SCRi[:, 3664:3680]
        tblw2 = SCRi[:, 3680:3696]
        tblb1 = SCRi[:, 3696:3712]
        tbtmp = SCR[:, 3712:3728]

        def tbl_t(tensor):
            vS.wait(tensor); qS.wait(tensor)
            tS.inc(tensor.matmul(pb[0][:, 0:16], onesr_f[0:1, :],
                                 erow_f[:, :], start=True, stop=True))
        t_blk(tbl_t)

        sloti = SCRi[:, 3728:3744]
        onesr_b = SCRbf[0:1, 7488:7616]
        oh8seg = SCRbf[0:8, 7616:7632]

        def tbl_v(vector):
            tS.wait(vector); qS.wait(vector)
            for mult, dst in ((2048.0, tblw1), (4096.0, tblw2),
                              (128.0, tblb1)):
                vS.wait(vector)
                vS.inc(vector.tensor_scalar(tbtmp[:, :], pb[0][:, 0:16],
                                            mult, iost[:, 0:1],
                                            ALU.mult, ALU.add))
                vS.wait(vector)
                vS.inc(vector.tensor_copy(dst, tbtmp[:, :]))
            vS.wait(vector)
            vS.inc(vector.tensor_scalar(oh8seg[:, :], pb[0][0:8, 0:16],
                                        iost[0:8, 0:1], None, ALU.is_equal))
            vS.inc(vector.tensor_copy(sloti[:, :], slot_sb[:, :]))
            vS.inc(vector.memset(onesr_b[:, :], 1.0))
        v_blk(tbl_v)
        vS_tbl = vS.v

        dsp_psum = [pb[0][:, :], pb[1][:, :], pb[2][:, :], pb[3][:, :],
                    pbig[:, 0:512], pbig[:, 512:1024],
                    pbig[:, 1024:1536], pbig[:, 1536:2048]]
        for sc in range(8):
            for tt in range(16):
                def pt_v(vector, sc=sc, tt=tt):
                    vS.wait(vector)
                    if tt > 1 or sc > 0:
                        tS.wait(vector)
                    vS.inc(vector.tensor_scalar(
                        slotloc[:, :], slot_sb[:, tt:tt + 1], 1.0,
                        float(-512 * sc), ALU.mult, ALU.add))
                    vS.wait(vector)
                    vS.inc(vector.tensor_scalar(
                        ptile[tt % 2][:, 0:512], io512bc[:, :],
                        slotloc[:, :], None, ALU.is_equal))
                v_blk(pt_v)

                def pt_t(tensor, sc=sc, tt=tt):
                    vS.wait(tensor); gq.wait(tensor)
                    for dc in range(8):
                        tS.inc(tensor.matmul(
                            dsp_psum[dc],
                            Dd[:, tt * D + dc * 128:tt * D + (dc + 1) * 128],
                            ptile[tt % 2][:, 0:512],
                            start=(tt == 0), stop=(tt == 15)))
                t_blk(pt_t)

            def dsp_v(vector, sc=sc):
                tS.wait(vector)
                for dc in range(8):
                    for half in range(2):
                        sg = 2 * sc + half
                        vS.inc(vector.tensor_copy(
                            Abf[:, sg * 2048 + dc * 256:sg * 2048 + (dc + 1) * 256],
                            dsp_psum[dc][:, half * 256:(half + 1) * 256]))
            v_blk(dsp_v)

        w2_psum = [pb[0][:, :], pb[1][:, :], pb[2][:, :], pb[3][:, :]]

        def hbank(ft):
            pr, hf = ft // 2, ft % 2
            if pr < 4:
                return pbig[:, pr * 512 + hf * 256:pr * 512 + (hf + 1) * 256]
            return pb[pr - 4][:, hf * 256:(hf + 1) * 256]

        w1_free = {r: tS_p8 for r in range(8)}
        w2_free = {r: tS_p8 for r in range(8)}

        def w1slot(r):
            if r < 4:
                return C[:, r * 2048:(r + 1) * 2048]
            return AUXbf[:, 8192 + (r - 4) * 2048:8192 + (r - 3) * 2048]
        b1_free = {0: sS_p8, 1: sS_p8}
        evac_val = {}
        eov_val = None
        for sg in range(16):
            def b1_ld(gpsimd, sg=sg):
                if sg == 0:
                    gpsimd.wait_ge(tS.h, tS_p8)
                    gpsimd.wait_ge(vS.h, vS_tbl)
                    gpsimd.wait_ge(sS.h, sS_p8)
                    gpsimd.wait_ge(qS.h, qS_p8)
                sS.wait(gpsimd, b1_free[sg % 2])
                b1sem[sg % 2].inc(gpsimd.indirect_dma_start(
                    b1sd[sg % 2][:, :], None, b1v,
                    IndirectOffsetOnAxis(ap=tblb1[:, sg:sg + 1],
                                         axis=0)), 16)
            g_blk(b1_ld)
            for fh in range(2):
                for dc in range(8):
                    r = (fh * 8 + dc) % 8

                    def w1_ld(gpsimd, sg=sg, fh=fh, dc=dc, r=r):
                        if sg == 0 and fh == 0 and r >= 4:
                            gpsimd.wait_ge(qS.h, qS_p8)
                        tS.wait(gpsimd, w1_free[r])
                        w1sem[r].inc(gpsimd.indirect_dma_start(
                            w1slot(r), None, w1v,
                            IndirectOffsetOnAxis(ap=tblw1[:, sg:sg + 1],
                                                 axis=0),
                            element_offset=fh * 2097152 + dc * 262144), 16)
                    g_blk(w1_ld)

                    def hd_t(tensor, sg=sg, fh=fh, dc=dc, r=r):
                        w1sem[r].wait(tensor)
                        if dc == 0 and fh == 0 and sg == 0:
                            vS.wait(tensor)
                        for i, ft in enumerate((0, 2, 4, 6, 8, 10, 12, 14,
                                                1, 3, 5, 7, 9, 11, 13, 15)):
                            if dc == 0 and i < 8:
                                bk = ft // 2
                                if i == 0 and fh == 0 and eov_val is not None:
                                    vS.wait(tensor, eov_val)
                                if bk in evac_val:
                                    if bk < 4:
                                        sS.wait(tensor, evac_val[bk])
                                    else:
                                        vS.wait(tensor, evac_val[bk])
                            tS.inc(tensor.matmul(
                                hbank(ft),
                                w1slot(r)[:, ft * 128:(ft + 1) * 128],
                                Abf[:, sg * 2048 + dc * 256:
                                    sg * 2048 + (dc + 1) * 256],
                                start=(dc == 0 and ft % 2 == 0),
                                stop=(dc == 7 and ft % 2 == 1)))
                    t_blk(hd_t)
                    w1_free[r] = tS.v

                def hd_s(scalar, sg=sg, fh=fh):
                    tS.wait(scalar)
                    b1sem[sg % 2].wait(scalar)
                    for ft in range(8):
                        gfc = fh * 16 + ft
                        sS.inc(scalar.activation(
                            AUXbf[:, gfc * 256:(gfc + 1) * 256],
                            hbank(ft), AF.Relu,
                            bias=b1sd[sg % 2][:, gfc:gfc + 1]))
                s_blk(hd_s)
                for bk in range(4):
                    evac_val[bk] = sS.v

                def hd_v(vector, sg=sg, fh=fh):
                    tS.wait(vector)
                    b1sem[sg % 2].wait(vector)
                    for ft in range(8, 16):
                        gfc = fh * 16 + ft
                        vS.inc(vector.tensor_scalar(
                            AUXbf[:, gfc * 256:(gfc + 1) * 256],
                            hbank(ft), b1sd[sg % 2][:, gfc:gfc + 1],
                            0.0, ALU.add, ALU.max))
                v_blk(hd_v)
                for bk in range(4, 8):
                    evac_val[bk] = vS.v
            b1_free[sg % 2] = sS.v

            for fk in range(32):
                rk = fk % 8

                def w2_ld(gpsimd, sg=sg, fk=fk, rk=rk):
                    tS.wait(gpsimd, w2_free[rk])
                    w2sem[rk].inc(gpsimd.indirect_dma_start(
                        C[:, 8192 + rk * 1024:8192 + (rk + 1) * 1024],
                        None, w2v,
                        IndirectOffsetOnAxis(ap=tblw2[:, sg:sg + 1],
                                             axis=0),
                        element_offset=fk * 131072), 16)
                g_blk(w2_ld)

                def w2_t(tensor, sg=sg, fk=fk, rk=rk):
                    w2sem[rk].wait(tensor)
                    if fk == 0:
                        sS.wait(tensor); vS.wait(tensor)
                    for ts2 in range(2):
                        for dh in range(2):
                            tS.inc(tensor.matmul(
                                w2_psum[ts2 * 2 + dh],
                                AUXbf[:, fk * 256 + ts2 * 128:
                                      fk * 256 + (ts2 + 1) * 128],
                                C[:, 8192 + rk * 1024 + dh * 512:
                                  8192 + rk * 1024 + (dh + 1) * 512],
                                start=(fk == 0), stop=(fk == 31)))
                t_blk(w2_t)
                w2_free[rk] = tS.v

            def b2b_t1(tensor, sg=sg):
                for dh in range(2):
                    tS.inc(tensor.matmul(
                        pbig[0:1, dh * 512:(dh + 1) * 512],
                        oh8seg[:, sg:sg + 1],
                        b2r[:, dh * 512:(dh + 1) * 512],
                        start=True, stop=True))
            t_blk(b2b_t1)

            def b2b_v1(vector, sg=sg):
                tS.wait(vector)
                vS.inc(vector.tensor_copy(b2row[:, :], pbig[0:1, 0:1024]))
            v_blk(b2b_v1)

            def b2b_t2(tensor, sg=sg):
                vS.wait(tensor)
                for dh in range(2):
                    tS.inc(tensor.matmul(
                        pbig[:, dh * 512:(dh + 1) * 512], onesr_b[:, :],
                        b2row[0:1, dh * 512:(dh + 1) * 512],
                        start=True, stop=True))
            t_blk(b2b_t2)

            def b2b_v2(vector, sg=sg):
                tS.wait(vector)
                vS.inc(vector.tensor_copy(b2bc_sb[:, :], pbig[:, 0:1024]))
            v_blk(b2b_v2)

            def eo_v(vector, sg=sg):
                vS.wait(vector)
                for ts2 in range(2):
                    for dh in range(2):
                        vS.inc(vector.tensor_tensor(
                            Abf[:, sg * 2048 + ts2 * 1024 + dh * 512:
                                sg * 2048 + ts2 * 1024 + (dh + 1) * 512],
                            w2_psum[ts2 * 2 + dh],
                            b2bc_sb[:, dh * 512:(dh + 1) * 512], ALU.add))
            v_blk(eo_v)
            eov_val = vS.v

            def eo_q(sync, sg=sg):
                vS.wait(sync)
                for ts2 in range(2):
                    eoq.inc(sync.dma_start(
                        out=eout_sp[sg * 256 + ts2 * 128:
                                    sg * 256 + (ts2 + 1) * 128, :],
                        in_=Abf[:, sg * 2048 + ts2 * 1024:
                                sg * 2048 + (ts2 + 1) * 1024]), 16)
            q_blk(eo_q)

        def cg_g(gpsimd):
            eoq.wait(gpsimd)
            tS.wait(gpsimd); vS.wait(gpsimd)
            for t in range(16):
                gq.inc(gpsimd.indirect_dma_start(
                    Dd[:, t * 1024:(t + 1) * 1024], None, eout_sp,
                    IndirectOffsetOnAxis(ap=sloti[:, t:t + 1], axis=0)), 16)
        g_blk(cg_g)

        def cb_s(scalar):
            gq.wait(scalar); vS.wait(scalar)
            for t in range(16):
                sS.inc(scalar.activation(
                    AUXbf[:, t * 1024:(t + 1) * 1024],
                    Dd[:, t * 1024:(t + 1) * 1024],
                    AF.Silu, scale=kg2_sb[:, t:t + 1]))
        s_blk(cb_s)
        dev_tap("gm1", AUXbf[:, :])
        if BUILD_UPTO < 10:
            return nc, dev_outs

        for t in range(16):
            def f_q(sync, t=t):
                qS.bar(sync)
                vS.wait(sync); sS.wait(sync)
                qS.inc(sync.dma_start(out=tmp4[:, :],
                                      in_=x2_sp[t * 128:(t + 1) * 128, :]), 16)
                qS.inc(sync.dma_start(out=wka[:, 0:D],
                                      in_=s1_sp[t * 128:(t + 1) * 128, :]), 16)
            q_blk(f_q)

            def f_v(vector, t=t):
                qS.wait(vector); vS.wait(vector); sS.wait(vector)
                oS.wait(vector)
                vS.inc(vector.tensor_tensor(
                    tmp4b[:, :], wka[:, 0:D],
                    AUXbf[:, t * 1024:(t + 1) * 1024], ALU.mult))
                vS.wait(vector)
                vS.inc(vector.tensor_tensor(SCR[:, 0:D], tmp4[:, :],
                                            tmp4b[:, :], ALU.add))
            v_blk(f_v)

            def f_s(sync, t=t):
                qS.bar(sync)
                vS.wait(sync)
                oS.inc(sync.dma_start(out=out[t * 128:(t + 1) * 128, :],
                                      in_=SCR[:, 0:D]), 16)
                oS.wait(sync)
            q_blk(f_s)

        def fin_q(sync):
            oS.wait(sync)
        q_blk(fin_q)



    return nc, dev_outs


def _prep_inputs(core, **inp):
    x = np.asarray(inp["L0_attn_x"], np.float32)
    i = core
    xb = np.ascontiguousarray(x[i])
    xs = np.ascontiguousarray(
        x[:, 256 * i:256 * (i + 1), :].reshape(2048, D))
    if i == 0:
        xh = np.zeros((24, D), np.float32)
    else:
        xh = np.ascontiguousarray(
            x[:, 256 * i - 3:256 * i, :].reshape(24, D))
    d = {
        "xb": xb, "xs": xs, "xh": xh,
        "ln1_w": np.asarray(inp["L0_attn_ln1_w"], np.float32),
        "ln1_b": np.asarray(inp["L0_attn_ln1_b"], np.float32),
        "convw": np.ascontiguousarray(
            np.asarray(inp["L0_attn_conv_w"], np.float32)[:, 0, :].T),
        "convb": np.asarray(inp["L0_attn_conv_b"], np.float32),
        "lns_w": np.asarray(inp["L0_attn_lns_w"], np.float32),
        "lns_b": np.asarray(inp["L0_attn_lns_b"], np.float32),
        "in_w": np.asarray(inp["L0_attn_in_w"], np.float32),
        "in_b": np.ascontiguousarray(np.asarray(inp["L0_attn_in_b"], np.float32).reshape(16, 128).T),
        "xp_w": np.asarray(inp["L0_attn_xp_w"], np.float32),
        "xp_b": np.asarray(inp["L0_attn_xp_b"], np.float32).reshape(16, 1),
        "A_log": np.asarray(inp["L0_attn_A_log"], np.float32).reshape(16, 1),
        "Dmat": np.asarray(inp["L0_attn_D"], np.float32),
        "out_w": np.asarray(inp["L0_attn_out_w"], np.float32),
        "out_b": np.asarray(inp["L0_attn_out_b"], np.float32),
        "ln2_w": np.asarray(inp["L0_attn_ln2_w"], np.float32),
        "ln2_b": np.asarray(inp["L0_attn_ln2_b"], np.float32),
        "mlp_w1": np.asarray(inp["L0_attn_mlp_w1"], np.float32),
        "mlp_b1": np.ascontiguousarray(np.asarray(inp["L0_attn_mlp_b1"], np.float32).reshape(16, 128).T),
        "mlp_w2": np.asarray(inp["L0_attn_mlp_w2"], np.float32),
        "mlp_b2": np.asarray(inp["L0_attn_mlp_b2"], np.float32),
        "Wg": np.asarray(inp["L0_ffn_Wg"], np.float32),
        "We": np.asarray(inp["L0_ffn_We"], np.float32),
        "be": np.asarray(inp["L0_ffn_be"], np.float32),
        "Wq": np.asarray(inp["L1_attn_Wq"], np.float32),
        "Wk": np.asarray(inp["L1_attn_Wk"], np.float32),
        "Wv": np.asarray(inp["L1_attn_Wv"], np.float32),
        "Wr": np.asarray(inp["L1_attn_Wr"], np.float32),
        "gate_wT": np.ascontiguousarray(
            np.asarray(inp["L1_ffn_gate_w"], np.float32).T),
        "gate_b": np.asarray(inp["L1_ffn_gate_b"], np.float32).reshape(1, 8),
        "w1": np.ascontiguousarray(
            np.asarray(inp["L1_ffn_w1"], np.float32)
            .reshape(8, D, 2, FF // 2).transpose(0, 2, 1, 3)),
        "b1r": np.ascontiguousarray(np.asarray(inp["L1_ffn_b1"], np.float32).reshape(8, 32, 128).transpose(0, 2, 1)),
        "w2": np.asarray(inp["L1_ffn_w2"], np.float32),
        "b2": np.asarray(inp["L1_ffn_b2"], np.float32),
        "identf": np.eye(128, dtype=np.float32),
        "identb": np.eye(128, dtype=np.float32).astype(ml_dtypes.bfloat16),
        "tri_bf": np.triu(np.ones((128, 128), np.float32)).astype(
            ml_dtypes.bfloat16),
        "cmask": (np.arange(8) < i).astype(np.float32).reshape(8, 1),
        "cexp": np.maximum(i - 1 - np.arange(8), 0).astype(
            np.float32).reshape(8, 1),
        "hmask": np.full((24, 1), 0.0 if i == 0 else 1.0, np.float32),
        "causm": (-400.0 * (np.arange(128)[:, None] > np.arange(128)[None, :])
                  ).astype(ml_dtypes.bfloat16),
        "iota8": np.arange(8, dtype=np.float32).reshape(1, 8),
        "iota512": np.arange(512, dtype=np.float32).reshape(1, 512),
        "iota_st": (np.arange(32)[None, :] * 128
                    + np.arange(128)[:, None]).astype(np.float32),
        "seg256": (np.arange(16, dtype=np.float32) * 256).reshape(16, 1),
    }
    return d


def kernel(**inputs):
    nc, dev_outs = build()
    in_maps = [_prep_inputs(i, **inputs) for i in range(8)]
    res = run_bass_kernel_spmd(nc, in_maps, core_ids=list(range(8)))
    outp = np.stack([np.asarray(res.results[i]["out"], np.float32)
                     for i in range(8)], axis=0)
    return outp.reshape(8, 2048, 1024)


if __name__ == "__main__":
    print("kernel module ok")
